# version 08
# baseline (speedup 1.0000x reference)
import math
import numpy as np
import ml_dtypes
import concourse.bass as bass
import concourse.mybir as mybir
from concourse.bass_utils import run_bass_kernel_spmd

F32 = mybir.dt.float32
BF16 = mybir.dt.bfloat16
I16 = mybir.dt.int16
I32 = mybir.dt.int32
OP = mybir.AluOpType
AF = mybir.ActivationFunctionType
AX = mybir.AxisListType.X

BFNP = ml_dtypes.bfloat16

D = 2048
S = 4096
TOWN = 2048
NCORE = 8
SKEW_W = 7680
SKEW_BASE = 3968

SLOPES = [float(np.float32((2.0 ** (-0.5)) ** (h + 1))) for h in range(16)]
MARGIN = 75.0
KEEPC = [257 + int(math.ceil(MARGIN / s)) for s in SLOPES]


def kept_js(hmax, qt):
    i0, i1 = qt * 1024, qt * 1024 + 1024
    C = KEEPC[hmax]
    out = []
    for j in range(32):
        j0, j1 = j * 128, j * 128 + 128
        dmin = max(0, j0 - (i1 - 1), i0 - (j1 - 1))
        if dmin <= C:
            out.append(j)
    return out

C_EXPB, C_ZERO, C_ONE, C_MONE, C_LNSC, C_EPS, C_QSC, C_VMSC, C_MHALF = range(9)
CSTVALS = [-15.0, 0.0, 1.0, -1.0, 1.0 / 2048.0, 1e-5, 1.0 / math.sqrt(128.0),
           1.0 / 4096.0, -0.5]

ENGS = ("t", "v", "a", "g", "d")
NDSEM = 16
DSEMS = tuple(f"d{i}" for i in range(NDSEM))
ALLSEMS = ("t", "v", "a", "g") + DSEMS


class Ctx:

    def __init__(self):
        self.q = {e: [] for e in ENGS}
        self.cnt = {s: 0 for s in ALLSEMS}
        self.events = {}
        self.dtotal = 0

    def op(self, eng, fn, ev=None, waits=(), flush=False):
        w = []
        for x in waits:
            if x is None:
                continue
            we, wv = self.events[x] if isinstance(x, str) else x
            if we != eng:
                w.append((we, wv))
        pf = getattr(self, "need_flush", None)
        if pf is None:
            self.need_flush = pf = {}
        if eng in pf and pf[eng] is not None:
            w.append((eng, pf[eng]))
            pf[eng] = None
        if eng == "d":
            sem = DSEMS[self.dtotal % NDSEM]
            self.dtotal += 1
            self.cnt[sem] += 16
            self.q[eng].append((fn, w, sem))
            if ev is not None:
                self.events[ev] = (sem, self.cnt[sem])
            return (sem, self.cnt[sem])
        self.q[eng].append((fn, w, eng))
        self.cnt[eng] += 1
        if flush:
            pf[eng] = self.cnt[eng]
        if ev is not None:
            self.events[ev] = (eng, self.cnt[eng])
        return (eng, self.cnt[eng])

    def t(self, fn, **kw):
        return self.op("t", fn, **kw)

    def v(self, fn, **kw):
        return self.op("v", fn, **kw)

    def a(self, fn, **kw):
        return self.op("a", fn, **kw)

    def g(self, fn, **kw):
        return self.op("g", fn, **kw)

    def d(self, fn, **kw):
        return self.op("d", fn, **kw)

    def bar(self, name):
        evs = []
        for e in ("t", "v", "a", "g"):
            self.events[f"{name}:{e}"] = (e, self.cnt[e])
            evs.append(f"{name}:{e}")
        for i, s in enumerate(DSEMS):
            self.events[f"{name}:d{i}"] = (s, self.cnt[s])
            evs.append(f"{name}:d{i}")
        self.events[f"{name}:d"] = (DSEMS[(self.dtotal - 1) % NDSEM],
                                    self.cnt[DSEMS[(self.dtotal - 1) % NDSEM]])
        return evs


def hmat(n):
    H = np.ones((1, 1), np.float32)
    while H.shape[0] < n:
        H = np.block([[H, H], [H, -H]])
    return H


def build_askew(qh):
    p = np.arange(128)[:, None]
    x = np.arange(SKEW_W)[None, :]
    d = (x - SKEW_BASE + qh * 512) - p
    g = np.where(np.abs(d) <= 256, -32768.0, 257.0 - np.abs(d))
    return g.astype(np.int16)


def own_rows(qh):
    return np.concatenate([np.arange(qh * 512 + k * 1024, qh * 512 + k * 1024 + 512)
                           for k in range(4)])


DEBUG = False


def build_program():
    nc = bass.Bass("TRN2", target_bir_lowering=False)
    ctx = Ctx()

    x_full = nc.dram_tensor("x", [S, D], F32, kind="ExternalInput")
    x_own = nc.dram_tensor("xo", [TOWN, D], F32, kind="ExternalInput")
    wq_d = nc.dram_tensor("wq", [D, D], F32, kind="ExternalInput")
    wkv_d = nc.dram_tensor("wkv", [D, 1024], F32, kind="ExternalInput")
    wo_d = nc.dram_tensor("wo", [D, D], F32, kind="ExternalInput")
    bo_d = nc.dram_tensor("bo", [128, 16], F32, kind="ExternalInput")
    wup_d = nc.dram_tensor("wupT", [D, 4096], F32, kind="ExternalInput")
    wdn_d = nc.dram_tensor("wdnT", [4096, D], F32, kind="ExternalInput")
    idf_d = nc.dram_tensor("idf_d", [128, 128], F32, kind="ExternalInput")
    idb_d = nc.dram_tensor("idb_d", [128, 128], BF16, kind="ExternalInput")
    onesb_d = nc.dram_tensor("onesb_d", [128, 128], BF16, kind="ExternalInput")
    onesf_d = nc.dram_tensor("onesf_d", [128, 128], F32, kind="ExternalInput")
    h128_d = nc.dram_tensor("h128_d", [128, 128], BF16, kind="ExternalInput")
    askew_d = nc.dram_tensor("askew_d", [128, SKEW_W], I16, kind="ExternalInput")
    umask_d = nc.dram_tensor("umask_d", [128, 1], F32, kind="ExternalInput")
    cst_d = nc.dram_tensor("cst_d", [128, len(CSTVALS)], F32, kind="ExternalInput")
    out_d = nc.dram_tensor("out", [TOWN, D], F32, kind="ExternalOutput")

    kscr = "ExternalOutput" if DEBUG else "Internal"
    q_scr = nc.dram_tensor("q_scr", [D, TOWN], BF16, kind=kscr)
    ao_scr = nc.dram_tensor("ao_scr", [D, TOWN], BF16, kind=kscr)
    ap_scr = nc.dram_tensor("ap_scr", [TOWN, D], F32, kind=kscr)
    i1_scr = nc.dram_tensor("i1_scr", [D, TOWN], BF16, kind=kscr)
    g_scr = nc.dram_tensor("g_scr", [4096, TOWN], F32, kind=kscr)
    td_scr = nc.dram_tensor("td_scr", [4096, D], BF16, kind=kscr)
    if DEBUG:
        dbg_d = nc.dram_tensor("dbg", [128, 72], F32, kind="ExternalOutput")
        kv_dbg = nc.dram_tensor("kv_dbg", [128, 2 * 16384], BF16, kind="ExternalOutput")
        yf_dbg = nc.dram_tensor("yf_dbg", [128, 8192], F32, kind="ExternalOutput")
        xnf_dbg = nc.dram_tensor("xnf_dbg", [128, 8192], BF16, kind="ExternalOutput")
        wa_dbg = nc.dram_tensor("wa_dbg", [128, 4], F32, kind="ExternalOutput")

    qsr = q_scr.rearrange("(o p) q -> p o q", p=128)
    aosr = ao_scr.rearrange("(o p) q -> p o q", p=128)
    apsr = ap_scr.rearrange("(i p) d -> p i d", p=128)
    i1sr = i1_scr.rearrange("(o p) q -> p o q", p=128)
    gsr = g_scr.rearrange("(o p) q -> o p q", p=128)
    gslr = g_scr.rearrange("(c p) q -> p c q", p=128)
    tdsr = td_scr.rearrange("(i p) d -> p i d", p=128)
    outr = out_d.rearrange("(i p) d -> p i d", p=128)
    xor_ = x_own.rearrange("(i p) d -> p i d", p=128)
    wup_r = wup_d.rearrange("(i p) d -> p i d", p=128)
    wdn_r = wdn_d.rearrange("(i p) d -> p i d", p=128)
    wkv_r = wkv_d.rearrange("(i p) d -> i p d", p=128)

    def sb(name, shape, dtype, bag):
        g = nc.sbuf_tensor(name, shape, dtype)
        t = g.__enter__()
        bag.append(g)
        return t

    def ps(name, shape, dtype, bag):
        g = nc.psum_tensor(name, shape, dtype)
        t = g.__enter__()
        bag.append(g)
        return t

    def freebag(bag):
        for g in reversed(bag):
            g.__exit__(None, None, None)
        bag.clear()

    perm = []
    idf = sb("idf", [128, 128], F32, perm)
    idb = sb("idb", [128, 128], BF16, perm)
    onesb = sb("onesb", [128, 128], BF16, perm)
    onesf = sb("onesf", [128, 128], F32, perm)
    h128 = sb("h128", [128, 128], BF16, perm)
    cst = sb("cst", [128, len(CSTVALS)], F32, perm)
    umask = sb("umask", [128, 1], F32, perm)
    bo_s = sb("bo_s", [128, 16], F32, perm)
    svals = sb("svals", [128, 8], F32, perm)
    mxc1b = sb("mxc1b", [128, 16], F32, perm)
    alpsq = sb("alpsq", [128, 16], F32, perm)
    betab = sb("betab", [128, 16], F32, perm)

    def cc(i):
        return cst[:, i:i + 1]

    for dst, src in ((idf, idf_d), (idb, idb_d), (onesb, onesb_d), (onesf, onesf_d),
                     (h128, h128_d), (cst, cst_d), (umask, umask_d), (bo_s, bo_d)):
        ctx.d(lambda s, dst=dst, src=src: s.dma_start(dst[:], src[:, :]))
    ctx.bar("cb")

    def emit_ln(tag, prev_tag, xs, t0, xt, xcb, sqb, rb_full, rpar, xld_waits=(),
                xc_waits=(), want_cw=False):
        for stage in range(3):
            emit_ln_stage(stage, tag, prev_tag, xs, t0, xt, xcb, sqb, rb_full, rpar,
                          xld_waits, xc_waits, want_cw)

    def emit_ln_stage(stage, tag, prev_tag, xs, t0, xt, xcb, sqb, rb_full, rpar,
                      xld_waits=(), xc_waits=(), want_cw=False):
        rb = rb_full[:, rpar * 8:(rpar + 1) * 8]
        if stage == 0:
            ctx.d(lambda s: s.dma_start(xt[:], xs[t0:t0 + 128, :]), ev=f"{tag}:xld",
                  waits=xld_waits)
            ctx.v(lambda v: v.tensor_reduce(rb[:, 0:1], xt[:], AX, OP.add),
                  waits=[f"{tag}:xld"], flush=True)
            ctx.v(lambda v: v.tensor_single_scalar(rb[:, 1:2], rb[:, 0:1],
                                                   -1.0 / 2048.0, OP.mult), flush=True)
            return
        if stage == 1:
            ctx.v(lambda v: v.tensor_scalar(xcb[:], xt[:], rb[:, 1:2], None, OP.add),
                  ev=f"{tag}:xc", waits=xc_waits)
            ctx.v(lambda v: v.tensor_mul(sqb[:], xcb[:], xcb[:]))
            ctx.v(lambda v: v.tensor_reduce(rb[:, 2:3], sqb[:], AX, OP.add),
                  ev=f"{tag}:ssq",
                  waits=[f"{prev_tag}:rstd"] if prev_tag else ())
            ctx.a(lambda a: a.activation(rb[:, 3:4], rb[:, 2:3], AF.Ln,
                                         bias=cc(C_EPS), scale=cc(C_LNSC)),
                  waits=[f"{tag}:ssq"], flush=True)
            ctx.a(lambda a: a.activation(rb[:, 4:5], rb[:, 3:4], AF.Exp,
                                         bias=cc(C_ZERO), scale=cc(C_MHALF)),
                  ev=f"{tag}:rstd")
            return
        ctx.v(lambda v: v.tensor_scalar(xcb[:], xcb[:], rb[:, 4:5], None, OP.mult),
              ev=f"{tag}:xn", waits=[f"{tag}:rstd"])
        if want_cw:
            ctx.v(lambda v: v.tensor_reduce(rb[:, 5:6], xcb[:], AX, OP.add,
                                            apply_absolute_value=True),
                  ev=f"{tag}:asum",
                  waits=[f"{prev_tag}:cw"] if prev_tag else ())
            ctx.a(lambda a: a.activation(rb[:, 6:7], rb[:, 5:6], AF.Exp,
                                         bias=cc(C_ZERO), scale=cc(C_MONE)),
                  waits=[f"{tag}:asum"], flush=True)
            ctx.a(lambda a: a.activation(rb[:, 6:7], rb[:, 6:7], AF.Ln,
                                         bias=cc(C_ONE), scale=cc(C_ONE)), flush=True)
            ctx.a(lambda a: a.activation(rb[:, 7:8], rb[:, 6:7], AF.Exp,
                                         bias=cc(C_ZERO), scale=cc(C_MONE)),
                  ev=f"{tag}:cw")
            ctx.v(lambda v: v.tensor_scalar(xcb[:], xcb[:], rb[:, 7:8], None, OP.mult),
                  ev=f"{tag}:xn", waits=[f"{tag}:cw"])

    def emit_tr(tag, src, dst, dst_col, ptr, nch, out_stride, evac, dtw=(), ident=None):
        last = [None, None]
        for c in range(nch):
            par = c % 2
            w = list(dtw) if c < 2 else []
            if last[par] is not None:
                w.append(last[par])
            idm = ident if ident is not None else idf
            ctx.t(lambda t, c=c, par=par, idm=idm: t.transpose(
                ptr[par][:], src[:, c * 128:(c + 1) * 128], idm[:]),
                ev=f"{tag}:tr{c}", waits=w)
            o0 = c * out_stride + dst_col
            if evac == "a":
                ctx.a(lambda a, c=c, par=par, o0=o0: a.activation(
                    dst[:, o0:o0 + 128], ptr[par][:], AF.Copy,
                    bias=0.0, scale=1.0),
                    ev=f"{tag}:ev{c}", waits=[f"{tag}:tr{c}"])
            else:
                ctx.v(lambda v, c=c, par=par, o0=o0: v.tensor_copy(
                    dst[:, o0:o0 + 128], ptr[par][:]),
                    ev=f"{tag}:ev{c}", waits=[f"{tag}:tr{c}"])
            last[par] = f"{tag}:ev{c}"
        ctx.events[f"{tag}:T"] = ctx.events[f"{tag}:tr{nch-1}"]
        ctx.events[f"{tag}:E"] = ctx.events[f"{tag}:ev{nch-1}"]

    bag, pbag = [], []
    wqb = sb("wqb", [128, 16 * 2048], BF16, bag)
    wstr = [sb(f"wstr{i}", [128, 2048], F32, bag) for i in range(2)]
    xt_b = [sb(f"xt{i}", [128, 2048], F32, bag) for i in range(2)]
    xc2_b = [sb(f"xc{i}", [128, 2048], F32, bag) for i in range(2)]
    sq_b = sb("sq", [128, 2048], F32, bag)
    r_b = sb("r", [128, 16], F32, bag)
    xncwfm = sb("xncwfm", [128, 16 * 512], BF16, bag)
    qfm = sb("qfm", [128, 16 * 512], BF16, bag)
    ptrA = [ps(f"ptrA{i}", [128, 128], F32, pbag) for i in range(2)]
    pq = [ps(f"pq{i}", [128, 512], F32, pbag) for i in range(2)]

    for ic in range(16):
        par = ic % 2
        w = [f"cb:d{i}" for i in range(16)] if ic < 2 else [f"q:wc{ic-2}"]
        ctx.d(lambda s, ic=ic, par=par: s.dma_start(
            wstr[par][:], wq_d[ic * 128:(ic + 1) * 128, :]), ev=f"q:wl{ic}", waits=w)
        ctx.v(lambda v, ic=ic, par=par: v.tensor_copy(
            wqb[:, ic * 2048:(ic + 1) * 2048], wstr[par][:]),
            ev=f"q:wc{ic}", waits=[f"q:wl{ic}"])

    for qt in range(4):
        for tp_ in range(2):
            tgs = [qt * 4 + tp_ * 2, qt * 4 + tp_ * 2 + 1]
            for st_ in range(3):
                for i_, tg in enumerate(tgs):
                    tag = f"q:t{tg}"
                    prevtag = f"q:t{tg-2}" if tg >= 2 else None
                    emit_ln_stage(st_, tag, prevtag, x_own, tg * 128, xt_b[i_],
                                  xc2_b[i_], sq_b, r_b, tg % 2,
                                  xld_waits=[f"q:t{tg-2}:xc"] if tg >= 2 else
                                  [f"cb:d{i}" for i in range(16)],
                                  xc_waits=[f"q:r{tg-2}:T"] if tg >= 2 else (),
                                  want_cw=True)
            for i_, tg in enumerate(tgs):
                tt = tg - qt * 4
                emit_tr(f"q:r{tg}", xc2_b[i_], xncwfm, tt * 128, ptrA, 16, 512, "a",
                        dtw=[f"q:t{tg}:xn"] +
                            ([f"q:r{tg-1}:E"] if tg >= 1 else []) +
                            ([f"q:mm{qt-1}_15_last"] if (tt == 0 and qt > 0) else []))
        trall = [f"q:r{qt * 4 + tt}:E" for tt in range(4)]
        for oc in range(16):
            par = oc % 2
            for ic in range(16):
                w = []
                if ic == 0:
                    w = list(trall)
                    if oc >= 2:
                        w.append(f"q:qe{qt}_{oc-2}")
                    elif qt > 0:
                        w.append(f"q:qe{qt-1}_{oc+14}")
                    if oc < 2:
                        w.append(f"q:qs{qt-1}" if qt > 0 else None)
                ctx.t(lambda t, oc=oc, ic=ic, par=par: t.matmul(
                    pq[par][:], wqb[:, ic * 2048 + oc * 128: ic * 2048 + oc * 128 + 128],
                    xncwfm[:, ic * 512:(ic + 1) * 512],
                    start=(ic == 0), stop=(ic == 15)),
                    ev=(f"q:mm{qt}_{oc}" if ic == 15 else None), waits=w)
            ctx.a(lambda a, oc=oc, par=par: a.activation(
                qfm[:, oc * 512:(oc + 1) * 512], pq[par][:], AF.Copy,
                bias=0.0, scale=CSTVALS[C_QSC]),
                ev=f"q:qe{qt}_{oc}", waits=[f"q:mm{qt}_{oc}"] +
                ([f"q:qs{qt-1}"] if (qt > 0 and oc < 2) else []))
        ctx.events[f"q:mm{qt}_15_last"] = ctx.events[f"q:mm{qt}_15"]
        ctx.d(lambda s, qt=qt: s.dma_start(
            qsr[:, :, qt * 512:(qt + 1) * 512],
            qfm[:].rearrange("p (o c) -> p o c", c=512)),
            ev=f"q:qs{qt}", waits=[f"q:qe{qt}_{o}" for o in range(16)])

    ctx.bar("q")
    freebag(pbag)
    freebag(bag)

    bag, pbag = [], []
    xt_b = [sb(f"fxt{i}", [128, 2048], F32, bag) for i in range(2)]
    xc2_b = [sb(f"fxc{i}", [128, 2048], F32, bag) for i in range(2)]
    sq_b = sb("fsq", [128, 2048], F32, bag)
    r_b = sb("fr", [128, 16], F32, bag)
    yfm = sb("yfm", [128, 16 * 512], F32, bag)
    hia = sb("hia", [128, 2048], BF16, bag)
    loa = sb("loa", [128, 2048], BF16, bag)
    ubuf = sb("ubuf", [128, 2048], F32, bag)
    i32b = sb("i32b", [128, 2048], I32, bag)
    gab = sb("gab", [128, 2048], F32, bag)
    gbb = sb("gbb", [128, 2048], F32, bag)
    intstm = sb("intstm", [128, 2048], BF16, bag)
    intsfm = sb("intsfm", [128, 16 * 512], BF16, bag)
    rq = sb("rq", [128, 4], F32, bag)
    ptrA = [ps(f"fptr{i}", [128, 128], F32, pbag) for i in range(2)]
    ptm = ps("ptm", [128, 2048], F32, pbag)
    pbtr = [ps(f"fbtr{i}", [128, 128], BF16, pbag) for i in range(2)]

    for qt in range(4):
        for tp_ in range(2):
            tgs = [qt * 4 + tp_ * 2, qt * 4 + tp_ * 2 + 1]
            for st_ in range(3):
                for i_, tg in enumerate(tgs):
                    tag = f"f:t{tg}"
                    prevtag = f"f:t{tg-2}" if tg >= 2 else None
                    emit_ln_stage(st_, tag, prevtag, x_own, tg * 128, xt_b[i_],
                                  xc2_b[i_], sq_b, r_b, tg % 2,
                                  xld_waits=[f"f:t{tg-2}:xc"] if tg >= 2 else
                                  [f"q:d{i}" for i in range(16)] + ["q:t", "q:v", "q:a"],
                                  xc_waits=[f"f:r{tg-2}:T"] if tg >= 2 else ())
            for i_, tg in enumerate(tgs):
                tt = tg - qt * 4
                emit_tr(f"f:r{tg}", xc2_b[i_], yfm, tt * 128, ptrA, 16, 512, "v",
                        dtw=[f"f:t{tg}:xn"] +
                            ([f"f:r{tg-1}:E"] if tg >= 1 else []) +
                            ([f"f:losplit{qt-1}"] if (tt == 0 and qt > 0) else []))
        ydone = [f"f:r{qt * 4 + tt}:E" for tt in range(4)]
        bit = 1
        k = 0
        while bit < 16:
            for c in range(16):
                if c & bit:
                    continue
                a0, b0 = c * 512, (c | bit) * 512
                ctx.v(lambda v, a0=a0, b0=b0: v.tensor_sub(
                    yfm[:, b0:b0 + 512], yfm[:, a0:a0 + 512], yfm[:, b0:b0 + 512]),
                    waits=ydone if k == 0 else ())
                ctx.v(lambda v, a0=a0, b0=b0: v.scalar_tensor_tensor(
                    yfm[:, a0:a0 + 512], yfm[:, a0:a0 + 512], 2.0,
                    yfm[:, b0:b0 + 512], OP.mult, OP.subtract),
                    ev=f"f:bf{qt}_{k}")
                k += 1
            bit *= 2
        bfd = f"f:bf{qt}_{k-1}"
        if DEBUG and qt == 0:
            ctx.d(lambda s: s.dma_start(yf_dbg[:, :], yfm[:]), waits=[bfd])
        for tt in range(4):
            tg = qt * 4 + tt
            yv = yfm[:].rearrange("p (c q) -> p c q", q=512)[:, :, tt * 128:tt * 128 + 128]
            w = [bfd]
            if tg > 0:
                w.append(f"f:fm{tg-1}_15")
            ctx.v(lambda v, yv=yv: v.tensor_copy(hia[:], yv), ev=f"f:hi{tg}", waits=w)
            ctx.v(lambda v, yv=yv: v.tensor_tensor(
                loa[:].rearrange("p (c q) -> p c q", q=128), yv,
                hia[:].rearrange("p (c q) -> p c q", q=128), OP.subtract),
                ev=f"f:lo{tg}")
            for c in range(16):
                ctx.t(lambda t, c=c: t.matmul(
                    ptm[:, c * 128:(c + 1) * 128], hia[:, c * 128:(c + 1) * 128],
                    h128[:], start=(c % 4 == 0), stop=False),
                    waits=[f"f:lo{tg}"] +
                          ([f"f:u{tg-1}"] if (c == 0 and tg > 0) else []))
                ctx.t(lambda t, c=c: t.matmul(
                    ptm[:, c * 128:(c + 1) * 128], loa[:, c * 128:(c + 1) * 128],
                    h128[:], start=False, stop=(c % 4 == 3)),
                    ev=f"f:fm{tg}_{c}")
            mmd = [f"f:fm{tg}_{c}" for c in range(3, 16, 4)]
            ctx.v(lambda v: v.tensor_reduce(rq[:, 0:1], ptm[:], AX, OP.max,
                                            apply_absolute_value=True), waits=mmd,
                  flush=True)
            ctx.v(lambda v, tg=tg: v.tensor_single_scalar(
                mxc1b[:, tg:tg + 1], rq[:, 0:1],
                float(np.float32(1e-5) * np.float32(math.sqrt(2048.0))), OP.max),
                flush=True)
            ctx.v(lambda v, tg=tg: v.reciprocal(rq[:, 2:3], mxc1b[:, tg:tg + 1]),
                  flush=True)
            ctx.v(lambda v: v.tensor_scalar(ubuf[:], ptm[:], rq[:, 2:3], 127.0,
                                            OP.mult, OP.mult),
                  ev=f"f:u{tg}")
            ctx.v(lambda v: v.tensor_copy(i32b[:], ubuf[:]), ev=f"f:i32{tg}")
            ctx.v(lambda v: v.tensor_sub(gbb[:], ubuf[:], i32b[:]), ev=f"f:ib{tg}")
            ctx.v(lambda v: v.tensor_single_scalar(gab[:], gbb[:], 0.5, OP.is_gt))
            ctx.v(lambda v: v.tensor_single_scalar(gbb[:], gbb[:], -0.5, OP.is_lt))
            ctx.v(lambda v: v.tensor_add(gab[:], gab[:], i32b[:]), ev=f"f:ia{tg}")
            ctx.v(lambda v: v.tensor_sub(intstm[:], gab[:], gbb[:]),
                  ev=f"f:ints{tg}",
                  waits=[f"f:i1r{tg-1}:T"] if tg > 0 else ())
            emit_tr(f"f:i1r{tg}", intstm, intsfm, tt * 128, pbtr, 16, 512, "v",
                    dtw=[f"f:ints{tg}"] +
                        ([f"f:is{qt-1}"] if (tt == 0 and qt > 0) else []), ident=idb)
        ctx.events[f"f:losplit{qt}"] = ctx.events[f"f:lo{qt * 4 + 3}"]
        ctx.d(lambda s, qt=qt: s.dma_start(
            i1sr[:, :, qt * 512:(qt + 1) * 512],
            intsfm[:].rearrange("p (o c) -> p o c", c=512)),
            ev=f"f:is{qt}", waits=[f"f:i1r{qt * 4 + tt}:E" for tt in range(4)])

    ctx.bar("f")
    freebag(pbag)
    freebag(bag)

    kvbag = []
    k_res = sb("k_res", [128, 4 * 4096], BF16, kvbag)
    v_res = sb("v_res", [128, 4 * 32 * 128], BF16, kvbag)
    askew = sb("askew", [128, SKEW_W], I16, kvbag)

    bag, pbag = [], []
    wkvb = sb("wkvb", [128, 16 * 1024], BF16, bag)
    kstr = [sb(f"kstr{i}", [128, 1024], F32, bag) for i in range(2)]
    xt_b = [sb(f"axt{i}", [128, 2048], F32, bag) for i in range(2)]
    xc2_b = [sb(f"axc{i}", [128, 2048], F32, bag) for i in range(4)]
    sq_b = sb("asq", [128, 2048], F32, bag)
    r_b = sb("ar", [128, 32], F32, bag)
    xnfm = sb("axnfm", [128, 16 * 512], BF16, bag)
    vtmp = sb("vtmp", [128, 512], BF16, bag)
    wsb = [sb("wsb0", [128, 2048], F32, bag)] * 2
    wacc = sb("wacc", [128, 4], F32, bag)
    ptrA = [ps(f"aptr{i}", [128, 128], F32, pbag) for i in range(2)]
    pkv = [ps(f"pkv{i}", [128, 512], F32, pbag) for i in range(2)]
    pvtr = [ps(f"avtr{i}", [128, 128], BF16, pbag) for i in range(2)]
    psml = ps("psml", [128, 4], F32, pbag)

    ctx.d(lambda s: s.dma_start(askew[:], askew_d[:, :]), waits=[f"f:d{i}" for i in range(16)])
    for ic in range(16):
        par = ic % 2
        w = [f"f:d{i}" for i in range(16)] if ic < 2 else [f"a:wkc{ic-2}"]
        ctx.d(lambda s, ic=ic, par=par: s.dma_start(kstr[par][:], wkv_r[ic, :, :]),
              ev=f"a:wkl{ic}", waits=w)
        ctx.v(lambda v, ic=ic, par=par: v.tensor_copy(
            wkvb[:, ic * 1024:(ic + 1) * 1024], kstr[par][:]),
            ev=f"a:wkc{ic}", waits=[f"a:wkl{ic}"])
    ctx.v(lambda v: v.memset(wacc[:, 0:4], 0.0), ev="a:wz", waits=["f:v"])

    wchunks = []
    for k in range(32):
        wchunks.append((wup_d[(k % 16) * 128:(k % 16 + 1) * 128,
                              (k // 16) * 2048:(k // 16 + 1) * 2048], 0))
    for k in range(32):
        wchunks.append((wdn_d[(k % 32) * 128:(k % 32 + 1) * 128,
                              (k // 32) * 2048:(k // 32 + 1) * 2048], 1))

    for grp in range(8):
        for tp_ in range(2):
            tgs = [grp * 4 + tp_ * 2, grp * 4 + tp_ * 2 + 1]
            for st_ in range(3):
                for i_, tg in enumerate(tgs):
                    tag = f"a:t{tg}"
                    prevtag = f"a:t{tg-2}" if tg >= 2 else None
                    emit_ln_stage(st_, tag, prevtag, x_full, tg * 128, xt_b[tg % 2],
                                  xc2_b[tg % 4], sq_b, r_b, tg % 4,
                                  xld_waits=[f"a:t{tg-2}:xc"] if tg >= 2 else
                                  ["f:v", "f:t", "f:a", "f:g"],
                                  xc_waits=[f"a:r{tg-4}:T"] if tg >= 4 else ())
            for i_, tg in enumerate(tgs):
                tt = tg - grp * 4
                emit_tr(f"a:r{tg}", xc2_b[tg % 4], xnfm, tt * 128, ptrA, 16, 512, "a",
                        dtw=[f"a:t{tg}:xn"] +
                            ([f"a:r{tg-1}:E"] if tg >= 1 else []) +
                            ([f"a:kv{grp-1}_7mm"] if (tt == 0 and grp > 0) else []))
        trall = [f"a:r{grp * 4 + tt}:E" for tt in range(4)]
        if DEBUG and grp == 0:
            ctx.d(lambda s: s.dma_start(xnf_dbg[:, :], xnfm[:]), waits=trall)
        for oc in range(8):
            par = oc % 2
            for ic in range(16):
                w = []
                if ic == 0:
                    w = list(trall) + ["a:wkc15"]
                    if oc >= 2:
                        w.append(f"a:kve{grp}_{oc-2}")
                    elif grp > 0:
                        w.append(f"a:kve{grp-1}_{oc+6}")
                ctx.t(lambda t, oc=oc, ic=ic, par=par: t.matmul(
                    pkv[par][:], wkvb[:, ic * 1024 + oc * 128: ic * 1024 + oc * 128 + 128],
                    xnfm[:, ic * 512:(ic + 1) * 512], start=(ic == 0), stop=(ic == 15)),
                    ev=(f"a:kvm{grp}_{oc}" if ic == 15 else None), waits=w)
            if oc < 4:
                ctx.a(lambda a, oc=oc, grp=grp, par=par: a.activation(
                    k_res[:, oc * 4096 + grp * 512: oc * 4096 + grp * 512 + 512],
                    pkv[par][:], AF.Copy, bias=0.0, scale=1.0),
                    ev=f"a:kve{grp}_{oc}", waits=[f"a:kvm{grp}_{oc}"])
            else:
                gkv = oc - 4
                w = [f"a:kvm{grp}_{oc}"]
                if oc > 4:
                    w.append(f"a:vtr{grp}_{oc-1}_3")
                elif grp > 0:
                    w.append(f"a:vtr{grp-1}_7_3")
                ctx.a(lambda a, par=par: a.activation(
                    vtmp[:], pkv[par][:], AF.Copy, bias=0.0, scale=1.0),
                    ev=f"a:vt{grp}_{oc}", waits=w)
                for jl in range(4):
                    tpar = jl % 2
                    w2 = [f"a:vt{grp}_{oc}"]
                    if jl >= 2:
                        w2.append(f"a:vev{grp}_{oc}_{jl-2}")
                    ctx.t(lambda t, jl=jl, tpar=tpar: t.transpose(
                        pvtr[tpar][:], vtmp[:, jl * 128:(jl + 1) * 128], idb[:]),
                        ev=f"a:vtr{grp}_{oc}_{jl}", waits=w2)
                    col = (gkv * 32 + grp * 4 + jl) * 128
                    ctx.v(lambda v, col=col, tpar=tpar: v.tensor_copy(
                        v_res[:, col:col + 128], pvtr[tpar][:]),
                        ev=f"a:vev{grp}_{oc}_{jl}", waits=[f"a:vtr{grp}_{oc}_{jl}"])
                ctx.events[f"a:kve{grp}_{oc}"] = ctx.events[f"a:vev{grp}_{oc}_3"]
            if oc == 7:
                ctx.events[f"a:kv{grp}_7mm"] = ctx.events[f"a:kvm{grp}_{oc}"]
        for i in range(8):
            k = grp * 8 + i
            src, col = wchunks[k]
            par = k % 2
            w = ["a:wz"] if k < 1 else [f"a:wr{k-1}"]
            ctx.d(lambda s, src=src, par=par: s.dma_start(wsb[par][:], src),
                  ev=f"a:wl{k}", waits=w)
            ctx.v(lambda v, par=par: v.tensor_reduce(
                wacc[:, 2:3], wsb[par][:], AX, OP.add, apply_absolute_value=True),
                waits=[f"a:wl{k}"], flush=True)
            ctx.v(lambda v, col=col: v.tensor_add(
                wacc[:, col:col + 1], wacc[:, col:col + 1], wacc[:, 2:3]),
                ev=f"a:wr{k}", flush=True)
    ctx.t(lambda t: t.matmul(psml[:, 0:2], onesf[:], wacc[:, 0:2],
                             start=True, stop=True), ev="a:ws", waits=["a:wr63"])
    ctx.v(lambda v: v.tensor_scalar(svals[:, 0:1], psml[:, 0:1],
                                    1.0 / 8388608.0, 1e-5, OP.mult, OP.max),
          waits=["a:ws"], flush=True)
    ctx.v(lambda v: v.tensor_scalar(svals[:, 2:3], psml[:, 1:2],
                                    1.0 / 8388608.0, 1e-5, OP.mult, OP.max), flush=True)
    ctx.v(lambda v: v.reciprocal(svals[:, 1:2], svals[:, 0:1]), flush=True)
    ctx.v(lambda v: v.reciprocal(svals[:, 3:4], svals[:, 2:3]), ev="a:sv", flush=True)
    if DEBUG:
        ctx.d(lambda s: s.dma_start(wa_dbg[:, :], wacc[:]), waits=["a:sv"])

    if DEBUG:
        ctx.d(lambda s: s.dma_start(kv_dbg[:, 0:16384], k_res[:]),
              waits=[f"a:kve7_{o}" for o in range(8)])
        ctx.d(lambda s: s.dma_start(kv_dbg[:, 16384:32768], v_res[:]))

    ctx.bar("pa")
    freebag(pbag)
    freebag(bag)

    bag, pbag = [], []
    qt_b = [sb(f"qtl{i}", [128, 16 * 512], BF16, bag) for i in range(2)]
    ao_b = sb("ao", [128, 16 * 512], BF16, bag)
    t_b = [sb(f"tb{i}", [128, 512], F32, bag) for i in range(4)]
    p_b = [sb(f"pb{i}", [128, 512], BF16, bag) for i in range(4)]
    zg_b = [sb(f"zg{i}", [128, 512], F32, bag) for i in range(2)]
    rz_b = [sb(f"rz{i}", [128, 512], F32, bag) for i in range(2)]
    vme_b = sb("vme", [128, 4 * 64], BF16, bag)
    dt_b = sb("dt", [128, 64], F32, bag)

    vmbag = []
    pvm = ps("pvm", [128, 64], F32, vmbag)
    for gkv in range(4):
        for j in range(32):
            w = []
            if gkv == 0 and j == 0:
                w = ["pa:t", "pa:v", "pa:a"]
            elif j == 0:
                w = [f"p2:vme{gkv-1}"]
            ctx.t(lambda t, gkv=gkv, j=j: t.matmul(
                pvm[:], v_res[:, (gkv * 32 + j) * 128:(gkv * 32 + j) * 128 + 128],
                onesb[:, 0:64], start=(j == 0), stop=(j == 31)),
                ev=(f"p2:vmm{gkv}" if j == 31 else None), waits=w)
        ctx.a(lambda a, gkv=gkv: a.activation(
            vme_b[:, gkv * 64:(gkv + 1) * 64], pvm[:], AF.Copy,
            bias=0.0, scale=CSTVALS[C_VMSC]),
            ev=f"p2:vme{gkv}", waits=[f"p2:vmm{gkv}"])
    freebag(vmbag)

    ppv = [ps(f"ppv{i}", [128, 512], F32, pbag) for i in range(2)]
    pz = [ps(f"pz{i}", [128, 512], F32, pbag) for i in range(2)]
    psc = [ps(f"psc{i}", [128, 512], F32, pbag) for i in range(4)]

    def njof(qt, pr):
        return len(kept_js(2 * pr + 1, qt))

    def _lastevt(kind, qt, pr, hi_, k):
        if k >= 0:
            return f"p2:{kind}{qt}_{pr}_{2 * pr + hi_}_{k}"
        if pr > 0:
            n = njof(qt, pr - 1)
            return f"p2:{kind}{qt}_{pr-1}_{2 * (pr - 1) + hi_}_{n + k}"
        if qt > 0:
            n = njof(qt - 1, 7)
            return f"p2:{kind}{qt-1}_7_{14 + hi_}_{n + k}"
        return None

    def lastap(qt, pr, hi_, k):
        return _lastevt("ap", qt, pr, hi_, k)

    def lastex(qt, pr, hi_, k):
        return _lastevt("ex", qt, pr, hi_, k)

    def lastpv(qt, pr, hi_, k):
        return _lastevt("pz", qt, pr, hi_, k)

    ctx.d(lambda s: s.dma_start(
        qt_b[0][:].rearrange("p (o c) -> p o c", c=512), qsr[:, :, 0:512]),
        ev="p2:ql0", waits=["q:qs0"] + [f"pa:d{i}" for i in range(16)])

    for qt in range(4):
        qpar = qt % 2
        if qt + 1 < 4:
            ctx.d(lambda s, qt=qt: s.dma_start(
                qt_b[(qt + 1) % 2][:].rearrange("p (o c) -> p o c", c=512),
                qsr[:, :, (qt + 1) * 512:(qt + 2) * 512]),
                ev=f"p2:ql{qt+1}",
                waits=[f"q:qs{qt+1}"] +
                      ([f"p2:sc{qt-1}_7_15_{njof(qt-1, 7) - 1}"] if qt >= 1 else []))
        for pr in range(8):
            h0, h1 = 2 * pr, 2 * pr + 1
            gkv = h0 // 4
            jset = kept_js(h1, qt)
            nj = len(jset)
            for k in range(nj + 1):
                if k < nj:
                    j = jset[k]
                    for hi_, h in enumerate((h0, h1)):
                        sbk = 2 * hi_ + (k % 2)
                        w = [f"p2:ql{qt}"]
                        ap = lastap(qt, pr, hi_, k - 2)
                        if ap:
                            w.append(ap)
                        if qt == 0 and pr == 0 and k == 0:
                            w += ["pa:t", "pa:a", "pa:v", f"p2:vme3"]
                        ctx.t(lambda t, gkv=gkv, j=j, h=h, sbk=sbk, qpar=qpar: t.matmul(
                            psc[sbk][:],
                            k_res[:, gkv * 4096 + j * 128: gkv * 4096 + j * 128 + 128],
                            qt_b[qpar][:, h * 512:(h + 1) * 512],
                            start=True, stop=True),
                            ev=f"p2:sc{qt}_{pr}_{h}_{k}", waits=w)
                kk = k - 1
                if kk < 0 or kk > nj - 1:
                    continue
                jj = jset[kk]
                for hi_, h in enumerate((h0, h1)):
                    sbk = 2 * hi_ + (kk % 2)
                    xoff = qt * 1024 - jj * 128 + SKEW_BASE
                    w = [f"p2:sc{qt}_{pr}_{h}_{kk}"]
                    ex = lastex(qt, pr, hi_, kk - 2)
                    if ex:
                        w.append(ex)
                    if qt == 0 and pr == 0 and kk == 0:
                        w.append("pa:d")
                    ctx.v(lambda v, h=h, sbk=sbk, xoff=xoff: v.scalar_tensor_tensor(
                        t_b[sbk][:], askew[:, xoff:xoff + 512], SLOPES[h], psc[sbk][:],
                        OP.mult, OP.add),
                        ev=(None if jj == 0 else f"p2:ap{qt}_{pr}_{h}_{kk}"), waits=w)
                    if jj == 0:
                        ctx.v(lambda v, sbk=sbk: v.memset(t_b[sbk][0:64, :], -100000.0),
                              ev=f"p2:ap{qt}_{pr}_{h}_{kk}")
                    w = [f"p2:ap{qt}_{pr}_{h}_{kk}"]
                    pv = lastpv(qt, pr, hi_, kk - 2)
                    if pv:
                        w.append(pv)
                    ctx.a(lambda a, sbk=sbk: a.activation(
                        p_b[sbk][:], t_b[sbk][:], AF.Exp,
                        bias=cc(C_EXPB), scale=cc(C_ONE)),
                        ev=f"p2:ex{qt}_{pr}_{h}_{kk}", waits=w)
                    w = [f"p2:ex{qt}_{pr}_{h}_{kk}"]
                    if kk == 0:
                        if pr > 0:
                            w.append(f"p2:aow{qt}_{pr-1}_{2 * (pr - 1) + hi_}")
                        elif qt > 0:
                            w.append(f"p2:aow{qt-1}_7_{14 + hi_}")
                    ctx.t(lambda t, gkv=gkv, jj=jj, hi_=hi_, sbk=sbk, kk=kk, nj=nj: t.matmul(
                        ppv[hi_][:],
                        v_res[:, (gkv * 32 + jj) * 128:(gkv * 32 + jj) * 128 + 128],
                        p_b[sbk][:], start=(kk == 0), stop=(kk == nj - 1)),
                        ev=(f"p2:pvm{qt}_{pr}_{h}" if kk == nj - 1 else None), waits=w)
                    ctx.t(lambda t, hi_=hi_, sbk=sbk, kk=kk, nj=nj: t.matmul(
                        pz[hi_][:], onesb[:], p_b[sbk][:],
                        start=(kk == 0), stop=(kk == nj - 1)),
                        ev=f"p2:pz{qt}_{pr}_{h}_{kk}")
            for hi_, h in enumerate((h0, h1)):
                ctx.v(lambda v, hi_=hi_: v.tensor_single_scalar(
                    zg_b[hi_][:], pz[hi_][:], 1e-35, OP.max),
                    ev=f"p2:zg{qt}_{pr}_{h}", waits=[f"p2:pz{qt}_{pr}_{h}_{nj - 1}"])
                ctx.a(lambda a, hi_=hi_: a.activation(
                    rz_b[hi_][:], zg_b[hi_][:], AF.Ln,
                    bias=cc(C_ZERO), scale=cc(C_ONE)),
                    waits=[f"p2:zg{qt}_{pr}_{h}"])
                ctx.a(lambda a, hi_=hi_: a.activation(
                    rz_b[hi_][:], rz_b[hi_][:], AF.Exp,
                    bias=cc(C_ZERO), scale=cc(C_MONE)),
                    ev=f"p2:rz{qt}_{pr}_{h}")
                w = [f"p2:rz{qt}_{pr}_{h}", f"p2:pvm{qt}_{pr}_{h}"]
                if qt > 0 and pr == 0:
                    w.append(f"p2:aod{qt-1}")
                ctx.v(lambda v, h=h, hi_=hi_: v.tensor_mul(
                    ao_b[:, h * 512:(h + 1) * 512], ppv[hi_][:], rz_b[hi_][:]),
                    ev=f"p2:ao0{qt}_{pr}_{h}", waits=w)
                if qt == 0:
                    ctx.v(lambda v, h=h, gkv=gkv: v.tensor_sub(
                        dt_b[:], vme_b[:, gkv * 64:(gkv + 1) * 64],
                        ao_b[:, h * 512:h * 512 + 64]),
                        waits=[f"p2:ao0{qt}_{pr}_{h}", f"p2:vme{gkv}"], flush=True)
                    ctx.v(lambda v, h=h: v.scalar_tensor_tensor(
                        ao_b[:, h * 512:h * 512 + 64], dt_b[:], umask[:, 0:1],
                        ao_b[:, h * 512:h * 512 + 64], OP.mult, OP.add),
                        ev=f"p2:aow{qt}_{pr}_{h}")
                else:
                    ctx.events[f"p2:aow{qt}_{pr}_{h}"] = ctx.events[f"p2:ao0{qt}_{pr}_{h}"]
        ctx.d(lambda s, qt=qt: s.dma_start(
            aosr[:, :, qt * 512:(qt + 1) * 512],
            ao_b[:].rearrange("p (o c) -> p o c", c=512)),
            ev=f"p2:aod{qt}",
            waits=[f"p2:aow{qt}_{pr}_{h}" for pr in range(8)
                   for h in (2 * pr, 2 * pr + 1)])

    ctx.bar("p2")
    freebag(pbag)
    freebag(bag)
    freebag(kvbag)

    bag, pbag = [], []
    aores = sb("aores", [128, 16 * 2048], BF16, bag)
    wob = sb("wob", [128, 16 * 2048], BF16, bag)
    wstr3 = [sb(f"wstr3{i}", [128, 2048], F32, bag) for i in range(2)]
    apfm = sb("apfm", [128, 2048], F32, bag)
    aptm = sb("aptm", [128, 16 * 128], F32, bag)
    poc3 = ps("poc3", [128, 2048], F32, pbag)
    ptr3 = [ps(f"ptr3{i}", [128, 128], F32, pbag) for i in range(2)]

    for ic in range(16):
        par = ic % 2
        w = [f"p2:d{i}" for i in range(16)] if ic < 2 else [f"p3:wc{ic-2}"]
        ctx.d(lambda s, ic=ic, par=par: s.dma_start(
            wstr3[par][:], wo_d[ic * 128:(ic + 1) * 128, :]),
            ev=f"p3:wl{ic}", waits=w)
        ctx.v(lambda v, ic=ic, par=par: v.tensor_copy(
            wob[:, ic * 2048:(ic + 1) * 2048], wstr3[par][:]),
            ev=f"p3:wc{ic}", waits=[f"p3:wl{ic}"])
    ctx.d(lambda s: s.dma_start(
        aores[:].rearrange("p (o c) -> p o c", c=2048), aosr[:, :, :]),
        ev="p3:aol", waits=[f"p2:d{i}" for i in range(16)])

    for oc in range(16):
        for ic in range(16):
            for ns in range(4):
                w = []
                if ic == 0 and ns == 0:
                    w = (["p3:aol", "p3:wc15"] if oc == 0 else [f"p3:ape{oc-1}"])
                ctx.t(lambda t, oc=oc, ic=ic, ns=ns: t.matmul(
                    poc3[:, ns * 512:(ns + 1) * 512],
                    wob[:, ic * 2048 + oc * 128: ic * 2048 + oc * 128 + 128],
                    aores[:, ic * 2048 + ns * 512: ic * 2048 + ns * 512 + 512],
                    start=(ic == 0), stop=(ic == 15)),
                    ev=(f"p3:mm{oc}" if (ic == 15 and ns == 3) else None), waits=w)
        w = [f"p3:mm{oc}"]
        if oc > 0:
            w.append(f"p3:tr{oc-1}:T")
        ctx.v(lambda v, oc=oc: v.tensor_scalar(
            apfm[:], poc3[:], bo_s[:, oc:oc + 1], None, OP.add),
            ev=f"p3:ape{oc}", waits=w)
        emit_tr(f"p3:tr{oc}", apfm, aptm, 0, ptr3, 16, 128, "v",
                dtw=[f"p3:ape{oc}"] + ([f"p3:aps{oc-1}"] if oc > 0 else []))
        ctx.d(lambda s, oc=oc: s.dma_start(
            apsr[:, :, oc * 128:(oc + 1) * 128],
            aptm[:].rearrange("p (i c) -> p i c", c=128)),
            ev=f"p3:aps{oc}", waits=[f"p3:tr{oc}:E"])

    ctx.bar("p3")
    freebag(pbag)
    freebag(bag)

    bag, pbag = [], []
    i1res = sb("i1res", [128, 16 * 2048], BF16, bag)
    wt4 = [sb(f"wt4{i}", [128, 16 * 128], F32, bag) for i in range(2)]
    tern4 = [sb(f"tern4{i}", [128, 16 * 128], BF16, bag) for i in range(2)]
    ga4 = sb("ga4", [128, 2048], F32, bag)
    gb4 = sb("gb4", [128, 2048], F32, bag)
    h4 = sb("h4", [128, 2048], F32, bag)
    g4 = [sb(f"g4{i}", [128, 2048], F32, bag) for i in range(2)]
    poc4 = [ps(f"poc4{i}", [128, 2048], F32, pbag) for i in range(2)]

    ctx.d(lambda s: s.dma_start(
        i1res[:].rearrange("p (o c) -> p o c", c=2048), i1sr[:, :, :]),
        ev="p4:i1l", waits=[f"p3:d{i}" for i in range(16)])
    for oc in range(2):
        ctx.d(lambda s, oc=oc: s.dma_start(
            wt4[oc % 2][:].rearrange("p (i c) -> p i c", c=128),
            wup_r[:, :, oc * 128:(oc + 1) * 128]),
            ev=f"p4:wl{oc}", waits=[f"p3:d{i}" for i in range(16)])
    for ocx in range(32):
        par = ocx % 2
        w = [f"p4:wl{ocx}", "a:sv"]
        if ocx >= 2:
            w.append(f"p4:mm{ocx-2}")
        ctx.v(lambda v, par=par: v.tensor_scalar(
            ga4[:], wt4[par][:], svals[:, 1:2], 0.5, OP.mult, OP.is_gt), waits=w)
        ctx.v(lambda v, par=par: v.tensor_scalar(
            gb4[:], wt4[par][:], svals[:, 1:2], -0.5, OP.mult, OP.is_lt))
        ctx.v(lambda v, par=par: v.tensor_sub(tern4[par][:], ga4[:], gb4[:]),
              ev=f"p4:tq{ocx}")
        if ocx + 2 < 32:
            ctx.d(lambda s, ocx=ocx, par=par: s.dma_start(
                wt4[par][:].rearrange("p (i c) -> p i c", c=128),
                wup_r[:, :, (ocx + 2) * 128:(ocx + 3) * 128]),
                ev=f"p4:wl{ocx+2}", waits=[f"p4:tq{ocx}"])
        for ic in range(16):
            for ns in range(4):
                w = []
                if ic == 0 and ns == 0:
                    w = [f"p4:tq{ocx}"] + (["p4:i1l"] if ocx == 0 else []) + \
                        ([f"p4:gs{ocx-2}"] if ocx >= 2 else [])
                ctx.t(lambda t, ocx=ocx, ic=ic, ns=ns, par=par: t.matmul(
                    poc4[par][:, ns * 512:(ns + 1) * 512],
                    tern4[par][:, ic * 128:(ic + 1) * 128],
                    i1res[:, ic * 2048 + ns * 512: ic * 2048 + ns * 512 + 512],
                    start=(ic == 0), stop=(ic == 15)),
                    ev=(f"p4:mm{ocx}" if (ic == 15 and ns == 3) else None), waits=w)
        ctx.a(lambda a, par=par: a.activation(
            h4[:], poc4[par][:], AF.Relu, bias=0.0, scale=1.0),
            waits=[f"p4:mm{ocx}"], ev=f"p4:h{ocx}")
        w = [f"p4:gs{ocx-2}", f"p4:h{ocx}"] if ocx >= 2 else [f"p4:h{ocx}"]
        ctx.v(lambda v, par=par: v.tensor_mul(g4[par][:], h4[:], h4[:]),
              ev=f"p4:g{ocx}", waits=w)
        ctx.d(lambda s, ocx=ocx, par=par: s.dma_start(gsr[ocx, :, :], g4[par][:]),
              ev=f"p4:gs{ocx}", waits=[f"p4:g{ocx}"])

    ctx.bar("p4")
    freebag(pbag)
    freebag(bag)

    bag, pbag = [], []
    i2half = sb("i2half", [128, 32 * 1024], BF16, bag)
    big45 = sb("big45", [128, 8192], F32, bag)
    gsl = big45
    wt6 = [big45[:, 0:4096], big45[:, 4096:8192]]
    hia5 = sb("hia5", [128, 2048], BF16, bag)
    loa5 = sb("loa5", [128, 2048], BF16, bag)
    t2s = sb("t2s", [128, 4096], F32, bag)
    u5 = sb("u5", [128, 1024], F32, bag)
    i325 = sb("i325", [128, 1024], I32, bag)
    ga5 = sb("ga5", [128, 2048], F32, bag)
    gb5 = sb("gb5", [128, 2048], F32, bag)
    ints5 = sb("ints5", [128, 4096], BF16, bag)
    rq5 = sb("rq5", [128, 8], F32, bag)
    tern6 = [sb(f"tern6{i}", [128, 32 * 128], BF16, bag) for i in range(2)]
    ffn6 = sb("ffn6", [128, 1024], F32, bag)
    ap6 = [sb(f"ap6{i}", [128, 8 * 128], F32, bag) for i in range(2)]
    xo6 = [sb(f"xo6{i}", [128, 8 * 128], F32, bag) for i in range(2)]
    apx6 = [sb(f"apx6{i}", [128, 8 * 128], F32, bag) for i in range(2)]
    ob6 = [sb(f"ob6{i}", [128, 8 * 128], F32, bag) for i in range(2)]
    for half in range(2):
        p5ps = []
        ptm5 = [ps(f"ptm5_{half}_{i}", [128, 1024], F32, p5ps) for i in range(2)]
        pbtr5 = [ps(f"pbtr5{i}_{half}", [128, 128], BF16, p5ps) for i in range(2)]
        for slq in range(4):
            sl = half * 4 + slq
            if sl == 0:
                w = [f"p4:d{i}" for i in range(16)] + ["p4:t", "p4:v"]
            elif slq == 0:
                w = [f"p6:od0_15"]
            else:
                w = [f"p5:ld{sl-1}"]
            ctx.d(lambda s, sl=sl: s.dma_start(
                gsl[:].rearrange("p (c q) -> p c q", q=256),
                gslr[:, :, sl * 256:(sl + 1) * 256]),
                ev=f"p5:gl{sl}", waits=w)
            bit = 1
            k = 0
            while bit < 32:
                for c in range(32):
                    if c & bit:
                        continue
                    a0, b0 = c * 256, (c | bit) * 256
                    ctx.v(lambda v, a0=a0, b0=b0: v.tensor_sub(
                        gsl[:, b0:b0 + 256], gsl[:, a0:a0 + 256], gsl[:, b0:b0 + 256]),
                        waits=[f"p5:gl{sl}"] if k == 0 else ())
                    ctx.v(lambda v, a0=a0, b0=b0: v.scalar_tensor_tensor(
                        gsl[:, a0:a0 + 256], gsl[:, a0:a0 + 256], 2.0,
                        gsl[:, b0:b0 + 256], OP.mult, OP.subtract),
                        ev=f"p5:bf{sl}_{k}")
                    k += 1
                bit *= 2
            bfd = f"p5:bf{sl}_{k-1}"
            for tt in range(2):
                tg = sl * 2 + tt
                for fh in range(2):
                    gv = gsl[:].rearrange("p (c q) -> p c q", q=256)[
                        :, fh * 16:(fh + 1) * 16, tt * 128:tt * 128 + 128]
                    w = [bfd]
                    if fh == 1:
                        w.append(f"p5:fm{tg}_1_7")
                    elif tg > 0:
                        w.append(f"p5:fm{tg-1}_3_7")
                    ctx.v(lambda v, gv=gv: v.tensor_copy(hia5[:], gv),
                          ev=f"p5:hi{tg}_{fh}", waits=w)
                    ctx.v(lambda v, gv=gv: v.tensor_tensor(
                        loa5[:].rearrange("p (c q) -> p c q", q=128), gv,
                        hia5[:].rearrange("p (c q) -> p c q", q=128), OP.subtract),
                        ev=f"p5:lo{tg}_{fh}")
                    for fqh in range(2):
                        fq = fh * 2 + fqh
                        par5 = fq % 2
                        for cq in range(8):
                            ch5 = fqh * 8 + cq
                            ctx.t(lambda t, cq=cq, ch5=ch5, par5=par5: t.matmul(
                                ptm5[par5][:, cq * 128:(cq + 1) * 128],
                                hia5[:, ch5 * 128:(ch5 + 1) * 128],
                                h128[:], start=(cq % 4 == 0), stop=False),
                                waits=[f"p5:lo{tg}_{fh}"] +
                                      ([f"p5:t2e{tg}_{fq-2}"] if (cq == 0 and fq >= 2) else
                                       ([f"p5:t2e{tg-1}_{fq+2}"] if (cq == 0 and tg > 0) else [])))
                            ctx.t(lambda t, cq=cq, ch5=ch5, par5=par5: t.matmul(
                                ptm5[par5][:, cq * 128:(cq + 1) * 128],
                                loa5[:, ch5 * 128:(ch5 + 1) * 128],
                                h128[:], start=False, stop=(cq % 4 == 3)),
                                ev=f"p5:fm{tg}_{fq}_{cq}")
                        ctx.a(lambda a, fq=fq, par5=par5: a.activation(
                            t2s[:, fq * 1024:(fq + 1) * 1024], ptm5[par5][:],
                            AF.Copy, bias=0.0, scale=1.0),
                            ev=f"p5:t2e{tg}_{fq}",
                            waits=[f"p5:fm{tg}_{fq}_3", f"p5:fm{tg}_{fq}_7"])
                ctx.v(lambda v: v.tensor_reduce(rq5[:, 0:1], t2s[:], AX, OP.max,
                                                apply_absolute_value=True),
                      waits=[f"p5:t2e{tg}_{fq}" for fq in range(4)], flush=True)
                ctx.v(lambda v, tg=tg: v.tensor_scalar(
                    rq5[:, 3:4], mxc1b[:, tg:tg + 1], svals[:, 0:1],
                    float(1.0 / (127.0 * math.sqrt(2048.0))), OP.mult, OP.mult),
                    waits=["a:sv"], flush=True)
                ctx.v(lambda v, tg=tg: v.tensor_mul(
                    alpsq[:, tg:tg + 1], rq5[:, 3:4], rq5[:, 3:4]), ev=f"p5:alp{tg}",
                    flush=True)
                ctx.v(lambda v, tg=tg: v.reciprocal(rq5[:, 4:5], alpsq[:, tg:tg + 1]),
                      flush=True)
                ctx.v(lambda v: v.tensor_single_scalar(rq5[:, 5:6], rq5[:, 4:5],
                                                       float(1e-5 * 64.0), OP.mult),
                      flush=True)
                ctx.v(lambda v: v.tensor_max(rq5[:, 1:2], rq5[:, 0:1], rq5[:, 5:6]),
                      flush=True)
                ctx.v(lambda v: v.reciprocal(rq5[:, 2:3], rq5[:, 1:2]), flush=True)
                ctx.v(lambda v: v.tensor_single_scalar(rq5[:, 7:8], rq5[:, 2:3],
                                                       127.0, OP.mult),
                      ev=f"p5:inv{tg}", flush=True)
                ctx.v(lambda v, tg=tg: v.tensor_scalar(
                    rq5[:, 6:7], rq5[:, 1:2], alpsq[:, tg:tg + 1], svals[:, 2:3],
                    OP.mult, OP.mult), flush=True)
                ctx.v(lambda v, tg=tg: v.tensor_single_scalar(
                    betab[:, tg:tg + 1], rq5[:, 6:7],
                    float(1.0 / (127.0 * 64.0)), OP.mult), ev=f"p5:beta{tg}",
                    flush=True)
                for quar in range(4):
                    q0 = quar * 1024
                    gh = ga5[:, (quar % 2) * 1024:(quar % 2) * 1024 + 1024]
                    gh2 = gb5[:, (quar % 2) * 1024:(quar % 2) * 1024 + 1024]
                    w = [f"p5:inv{tg}"]
                    if quar >= 1:
                        w.append(f"p5:iq{tg}_{quar-1}a")
                    elif tg > 0:
                        w.append(f"p5:iq{tg-1}_3a")
                    ctx.a(lambda a, q0=q0: a.activation(
                        u5[:], t2s[:, q0:q0 + 1024], AF.Copy, bias=0.0,
                        scale=rq5[:, 7:8]), waits=w, ev=f"p5:u{tg}_{quar}")
                    ctx.v(lambda v: v.tensor_copy(i325[:], u5[:]),
                          waits=[f"p5:u{tg}_{quar}"], ev=f"p5:i32{tg}_{quar}")
                    ctx.v(lambda v, gh2=gh2: v.tensor_sub(gh2, u5[:], i325[:]))
                    ctx.v(lambda v, gh=gh, gh2=gh2: v.tensor_single_scalar(
                        gh, gh2, 0.5, OP.is_gt))
                    ctx.v(lambda v, gh2=gh2: v.tensor_single_scalar(
                        gh2, gh2, -0.5, OP.is_lt))
                    ctx.v(lambda v, gh=gh: v.tensor_add(gh, gh, i325[:]),
                          ev=f"p5:iq{tg}_{quar}a")
                    w = [f"p5:itr{tg-1}_31"] if (quar == 0 and tg > 0) else []
                    ctx.v(lambda v, q0=q0, gh=gh, gh2=gh2: v.tensor_sub(
                        ints5[:, q0:q0 + 1024], gh, gh2),
                        ev=f"p5:iq{tg}_{quar}", waits=w)
                for c in range(32):
                    par = c % 2
                    w = [f"p5:iq{tg}_{c // 8}"]
                    if c >= 2:
                        w.append(f"p5:ie{tg}_{c-2}")
                    ctx.t(lambda t, c=c, par=par: t.transpose(
                        pbtr5[par][:], ints5[:, c * 128:(c + 1) * 128], idb[:]),
                        ev=f"p5:itr{tg}_{c}", waits=w)
                    toff = slq * 256 + tt * 128
                    w2 = [f"p5:itr{tg}_{c}"]
                    if tg == half * 8 and c == 0:
                        w2.append(f"p6:mm{1 - half}_15" if half == 1 else None)
                    ctx.a(lambda a, c=c, par=par, toff=toff: a.activation(
                        i2half[:, c * 1024 + toff: c * 1024 + toff + 128],
                        pbtr5[par][:], AF.Copy, bias=0.0, scale=1.0),
                        ev=f"p5:ie{tg}_{c}", waits=[x for x in w2 if x])
            ctx.events[f"p5:ld{sl}"] = ctx.events[f"p5:lo{sl * 2 + 1}_1"]
        i2done = [f"p5:ie{(half * 4 + slq) * 2 + tt}_{c}"
                  for slq in range(4) for tt in range(2) for c in range(32)]
        freebag(p5ps)

        p6ps = []
        pdn = [ps(f"pdn{i}_{half}", [128, 1024], F32, p6ps) for i in range(2)]
        ptr6 = [ps(f"ptr6{i}_{half}", [128, 128], F32, p6ps) for i in range(2)]

        def p6_load_quant(oc, extra):
            par = oc % 2
            if half == 0:
                ctx.d(lambda s, oc=oc, par=par: s.dma_start(
                    wt6[par].rearrange("p (i c) -> p i c", c=128),
                    wdn_r[:, :, oc * 128:(oc + 1) * 128]),
                    ev=f"p6:wl{oc}", waits=[f"p5:ld{3}"] + extra)
                ctx.v(lambda v, par=par: v.tensor_scalar(
                    ga5[:], wt6[par][:, 0:2048], svals[:, 3:4], 0.5,
                    OP.mult, OP.is_gt), waits=[f"p6:wl{oc}", "a:sv"] + extra)
                ctx.v(lambda v, par=par: v.tensor_scalar(
                    gb5[:], wt6[par][:, 0:2048], svals[:, 3:4], -0.5,
                    OP.mult, OP.is_lt))
                ctx.v(lambda v, par=par: v.tensor_sub(
                    tern6[par][:, 0:2048], ga5[:], gb5[:]))
                ctx.v(lambda v, par=par: v.tensor_scalar(
                    ga5[:], wt6[par][:, 2048:4096], svals[:, 3:4], 0.5,
                    OP.mult, OP.is_gt))
                ctx.v(lambda v, par=par: v.tensor_scalar(
                    gb5[:], wt6[par][:, 2048:4096], svals[:, 3:4], -0.5,
                    OP.mult, OP.is_lt))
                ctx.v(lambda v, par=par: v.tensor_sub(
                    tern6[par][:, 2048:4096], ga5[:], gb5[:]), ev=f"p6:tq0_{oc}")
                ctx.d(lambda s, oc=oc, par=par: s.dma_start(
                    tdsr[:, :, oc * 128:(oc + 1) * 128],
                    tern6[par][:].rearrange("p (i c) -> p i c", c=128)),
                    ev=f"p6:tds{oc}", waits=[f"p6:tq0_{oc}"])
            else:
                ctx.d(lambda s, oc=oc, par=par: s.dma_start(
                    tern6[par][:].rearrange("p (i c) -> p i c", c=128),
                    tdsr[:, :, oc * 128:(oc + 1) * 128]),
                    ev=f"p6:tq1_{oc}", waits=[f"p6:tds{oc}"] + extra)

        for ocx in range(16):
            par = ocx % 2
            if ocx == 0:
                fw = [] if half == 0 else [f"p6:od0_15"]
                p6_load_quant(0, fw)
                p6_load_quant(1, fw)
            for ic in range(32):
                for ns in range(2):
                    w = []
                    if ic == 0 and ns == 0:
                        w = [f"p6:tq{half}_{ocx}"] + (i2done if ocx == 0 else []) + \
                            ([f"p6:fe{half}_{ocx-2}"] if ocx >= 2 else [])
                    ctx.t(lambda t, ocx=ocx, ic=ic, ns=ns, par=par: t.matmul(
                        pdn[par][:, ns * 512:(ns + 1) * 512],
                        tern6[par][:, ic * 128:(ic + 1) * 128],
                        i2half[:, ic * 1024 + ns * 512: ic * 1024 + ns * 512 + 512],
                        start=(ic == 0), stop=(ic == 31)),
                        ev=(f"p6:mm{half}_{ocx}" if (ic == 31 and ns == 1) else None),
                        waits=w)
            if ocx + 2 < 16:
                ex = [f"p6:mm{half}_{ocx}"]
                if half == 0:
                    ex.append(f"p6:tds{ocx}")
                p6_load_quant(ocx + 2, ex)
            w = [f"p6:mm{half}_{ocx}"]
            if ocx > 0:
                w.append(f"p6:ftr{half}_{ocx-1}_7")
            ctx.a(lambda a, par=par: a.activation(
                ffn6[:], pdn[par][:], AF.Copy, bias=0.0, scale=1.0),
                ev=f"p6:fe{half}_{ocx}", waits=w)
            ctx.d(lambda s, ocx=ocx, par=par, half=half: s.dma_start(
                ap6[par][:].rearrange("p (i c) -> p i c", c=128),
                apsr[:, half * 8:(half + 1) * 8, ocx * 128:(ocx + 1) * 128]),
                ev=f"p6:apl{half}_{ocx}",
                waits=[f"p3:aps{ocx}"] +
                      ([f"p6:apx{half}_{ocx-2}"] if ocx >= 2 else
                       ([f"p6:apx0_{14 + ocx}"] if half == 1 else [])))
            ctx.d(lambda s, ocx=ocx, par=par, half=half: s.dma_start(
                xo6[par][:].rearrange("p (i c) -> p i c", c=128),
                xor_[:, half * 8:(half + 1) * 8, ocx * 128:(ocx + 1) * 128]),
                ev=f"p6:xl{half}_{ocx}")
            obp = ocx % 2
            ctx.v(lambda v, par=par, obp=obp: v.tensor_add(
                apx6[obp][:], ap6[par][:], xo6[par][:]),
                ev=f"p6:apx{half}_{ocx}",
                waits=[f"p6:apl{half}_{ocx}", f"p6:xl{half}_{ocx}"] +
                      ([f"p6:ob{half}_{ocx-2}_7"] if ocx >= 2 else []))
            for tt in range(8):
                tpar = tt % 2
                tgl = half * 8 + tt
                w = [f"p6:fe{half}_{ocx}"]
                if tt >= 2:
                    w.append(f"p6:ob{half}_{ocx}_{tt-2}")
                ctx.t(lambda t, tt=tt, tpar=tpar: t.transpose(
                    ptr6[tpar][:], ffn6[:, tt * 128:(tt + 1) * 128], idf[:]),
                    ev=f"p6:ftr{half}_{ocx}_{tt}", waits=w)
                w = [f"p6:ftr{half}_{ocx}_{tt}", f"p6:apx{half}_{ocx}",
                     f"p5:beta{tgl}"]
                if tt == 0:
                    w.append(f"p6:od{half}_{ocx-2}" if ocx > 1 else
                             (f"p6:od0_{14 + ocx}" if half == 1 else None))
                ctx.v(lambda v, tt=tt, tpar=tpar, tgl=tgl, obp=obp: v.scalar_tensor_tensor(
                    ob6[obp][:, tt * 128:(tt + 1) * 128], ptr6[tpar][:],
                    betab[:, tgl:tgl + 1], apx6[obp][:, tt * 128:(tt + 1) * 128],
                    OP.mult, OP.add),
                    ev=f"p6:ob{half}_{ocx}_{tt}", waits=[x for x in w if x])
            ctx.d(lambda s, ocx=ocx, half=half, obp=obp: s.dma_start(
                outr[:, half * 8:(half + 1) * 8, ocx * 128:(ocx + 1) * 128],
                ob6[obp][:].rearrange("p (i c) -> p i c", c=128)),
                ev=f"p6:od{half}_{ocx}",
                waits=[f"p6:ob{half}_{ocx}_{tt}" for tt in range(8)])
        freebag(p6ps)

    if DEBUG:
        ctx.v(lambda v: v.tensor_copy(ga5[:, 0:4], svals[:, 0:4]), waits=["a:sv"])
        ctx.v(lambda v: v.tensor_copy(ga5[:, 8:24], mxc1b[:]))
        ctx.v(lambda v: v.tensor_copy(ga5[:, 24:40], alpsq[:]))
        ctx.v(lambda v: v.tensor_copy(ga5[:, 40:56], betab[:]), ev="dbg:v")
        ctx.d(lambda s: s.dma_start(dbg_d[:, 0:56], ga5[:, 0:56]), waits=["dbg:v"])

    ctx.bar("fin")
    freebag(pbag)
    freebag(bag)
    freebag(perm)

    for i, val in enumerate(CSTVALS):
        try:
            nc.const_aps.register(F32, float(val), cst[:, i:i + 1])
        except Exception:
            pass
    sems = {s: nc.alloc_semaphore(f"sem_{s}") for s in ALLSEMS}

    def replay(engine, lst, inc):
        lastw = {}
        for fn, waits, sem in lst:
            for (we, wv) in waits:
                if lastw.get(we, -1) < wv:
                    engine.wait_ge(sems[we], wv)
                    lastw[we] = wv
            fn(engine).then_inc(sems[sem], inc)

    with nc.Block() as blk:
        @blk.tensor
        def _(t):
            replay(t, ctx.q["t"], 1)

        @blk.vector
        def _(v):
            replay(v, ctx.q["v"], 1)

        @blk.scalar
        def _(a):
            replay(a, ctx.q["a"], 1)

        @blk.gpsimd
        def _(g):
            replay(g, ctx.q["g"], 1)

        @blk.sync
        def _(s):
            replay(s, ctx.q["d"], 16)
            for ds in DSEMS:
                s.wait_ge(sems[ds], ctx.cnt[ds])

    return nc


_NC_CACHE = [None]


def kernel(L0_attn_x, L0_attn_Wq, L0_attn_Wkv, L0_attn_Wo, L0_attn_bo,
           L0_ffn_w_up, L0_ffn_w_down):
    X = np.ascontiguousarray(np.asarray(L0_attn_x, np.float32))
    Wq = np.ascontiguousarray(np.asarray(L0_attn_Wq, np.float32))
    Wkv = np.ascontiguousarray(np.asarray(L0_attn_Wkv, np.float32))
    Wo = np.ascontiguousarray(np.asarray(L0_attn_Wo, np.float32))
    bo = np.ascontiguousarray(np.asarray(L0_attn_bo, np.float32))
    WupT = np.ascontiguousarray(np.asarray(L0_ffn_w_up, np.float32).T)
    WdnT = np.ascontiguousarray(np.asarray(L0_ffn_w_down, np.float32).T)

    idf = np.eye(128, dtype=np.float32)
    idb = np.eye(128, dtype=np.float32).astype(BFNP)
    onesb = np.ones((128, 128), dtype=BFNP)
    onesf = np.ones((128, 128), np.float32)
    h128 = hmat(128).astype(BFNP)
    cstm = np.tile(np.array(CSTVALS, np.float32)[None, :], (128, 1))
    bo_t = np.ascontiguousarray(bo.reshape(16, 128).T)

    if _NC_CACHE[0] is None:
        _NC_CACHE[0] = build_program()
    nc = _NC_CACHE[0]

    in_maps = []
    for core in range(NCORE):
        b, qh = core // 2, core % 2
        rows = own_rows(qh)
        in_maps.append({
            "x": X[b],
            "xo": np.ascontiguousarray(X[b, rows]),
            "wq": Wq, "wkv": Wkv, "wo": Wo, "bo": bo_t,
            "wupT": WupT, "wdnT": WdnT,
            "idf_d": idf, "idb_d": idb, "onesb_d": onesb, "onesf_d": onesf,
            "h128_d": h128,
            "askew_d": build_askew(qh),
            "umask_d": np.full((128, 1), 1.0 if qh == 0 else 0.0, np.float32),
            "cst_d": cstm,
        })
    res = run_bass_kernel_spmd(nc, in_maps, core_ids=list(range(NCORE)))
    out = np.zeros((4, S, D), np.float32)
    for core in range(NCORE):
        b, qh = core // 2, core % 2
        out[b, own_rows(qh), :] = res.results[core]["out"]
    return out


if __name__ == "__main__":
    import time
    t0 = time.time()
    build_program()
    print(f"build ok in {time.time() - t0:.1f}s")


# version 14
# speedup vs baseline: 1.0254x; 1.0254x over previous
import math
import numpy as np
import ml_dtypes
import concourse.bass as bass
import concourse.mybir as mybir
from concourse.bass_utils import run_bass_kernel_spmd

F32 = mybir.dt.float32
BF16 = mybir.dt.bfloat16
I16 = mybir.dt.int16
I32 = mybir.dt.int32
OP = mybir.AluOpType
AF = mybir.ActivationFunctionType
AX = mybir.AxisListType.X

BFNP = ml_dtypes.bfloat16

D = 2048
S = 4096
TOWN = 2048
NCORE = 8
SKEW_W = 7680
SKEW_BASE = 3968

SLOPES = [float(np.float32((2.0 ** (-0.5)) ** (h + 1))) for h in range(16)]
MARGIN = 75.0
KEEPC = [257 + int(math.ceil(MARGIN / s)) for s in SLOPES]


def kept_js(hmax, qt):
    i0, i1 = qt * 1024, qt * 1024 + 1024
    C = KEEPC[hmax]
    out = []
    for j in range(32):
        j0, j1 = j * 128, j * 128 + 128
        dmin = max(0, j0 - (i1 - 1), i0 - (j1 - 1))
        if dmin <= C:
            out.append(j)
    return out

C_EXPB, C_ZERO, C_ONE, C_MONE, C_LNSC, C_EPS, C_QSC, C_VMSC, C_MHALF = range(9)
CSTVALS = [-15.0, 0.0, 1.0, -1.0, 1.0 / 2048.0, 1e-5, 1.0 / math.sqrt(128.0),
           1.0 / 4096.0, -0.5]

ENGS = ("t", "v", "a", "g", "d")
NDSEM = 16
DSEMS = tuple(f"d{i}" for i in range(NDSEM))
ALLSEMS = ("t", "v", "a", "g") + DSEMS


class Ctx:

    def __init__(self):
        self.q = {e: [] for e in ENGS}
        self.cnt = {s: 0 for s in ALLSEMS}
        self.events = {}
        self.dtotal = 0

    def op(self, eng, fn, ev=None, waits=(), flush=False):
        w = []
        for x in waits:
            if x is None:
                continue
            we, wv = self.events[x] if isinstance(x, str) else x
            if we != eng:
                w.append((we, wv))
        pf = getattr(self, "need_flush", None)
        if pf is None:
            self.need_flush = pf = {}
        if eng in pf and pf[eng] is not None:
            w.append((eng, pf[eng]))
            pf[eng] = None
        if eng == "d":
            sem = DSEMS[self.dtotal % NDSEM]
            self.dtotal += 1
            self.cnt[sem] += 16
            self.q[eng].append((fn, w, sem))
            if ev is not None:
                self.events[ev] = (sem, self.cnt[sem])
            return (sem, self.cnt[sem])
        self.q[eng].append((fn, w, eng))
        self.cnt[eng] += 1
        if flush:
            pf[eng] = self.cnt[eng]
        if ev is not None:
            self.events[ev] = (eng, self.cnt[eng])
        return (eng, self.cnt[eng])

    def t(self, fn, **kw):
        return self.op("t", fn, **kw)

    def v(self, fn, **kw):
        return self.op("v", fn, **kw)

    def a(self, fn, **kw):
        return self.op("a", fn, **kw)

    def g(self, fn, **kw):
        return self.op("g", fn, **kw)

    def d(self, fn, **kw):
        return self.op("d", fn, **kw)

    def bar(self, name):
        evs = []
        for e in ("t", "v", "a", "g"):
            self.events[f"{name}:{e}"] = (e, self.cnt[e])
            evs.append(f"{name}:{e}")
        for i, s in enumerate(DSEMS):
            self.events[f"{name}:d{i}"] = (s, self.cnt[s])
            evs.append(f"{name}:d{i}")
        self.events[f"{name}:d"] = (DSEMS[(self.dtotal - 1) % NDSEM],
                                    self.cnt[DSEMS[(self.dtotal - 1) % NDSEM]])
        return evs


def hmat(n):
    H = np.ones((1, 1), np.float32)
    while H.shape[0] < n:
        H = np.block([[H, H], [H, -H]])
    return H


def build_askew(qh):
    p = np.arange(128)[:, None]
    x = np.arange(SKEW_W)[None, :]
    d = (x - SKEW_BASE + qh * 512) - p
    g = np.where(np.abs(d) <= 256, -32768.0, 257.0 - np.abs(d))
    return g.astype(np.int16)


def own_rows(qh):
    return np.concatenate([np.arange(qh * 512 + k * 1024, qh * 512 + k * 1024 + 512)
                           for k in range(4)])


DEBUG = False
DBG_WA = False
DBG_KV = False
DBG_YF = False
DBG_XNF = False


def build_program():
    nc = bass.Bass("TRN2", target_bir_lowering=False)
    ctx = Ctx()

    x_full = nc.dram_tensor("x", [S, D], F32, kind="ExternalInput")
    x_own = nc.dram_tensor("xo", [TOWN, D], F32, kind="ExternalInput")
    wq_d = nc.dram_tensor("wq", [D, D], F32, kind="ExternalInput")
    wkv_d = nc.dram_tensor("wkv", [D, 1024], F32, kind="ExternalInput")
    wo_d = nc.dram_tensor("wo", [D, D], F32, kind="ExternalInput")
    bo_d = nc.dram_tensor("bo", [128, 16], F32, kind="ExternalInput")
    wup_d = nc.dram_tensor("wupT", [D, 4096], F32, kind="ExternalInput")
    wdn_d = nc.dram_tensor("wdnT", [4096, D], F32, kind="ExternalInput")
    idf_d = nc.dram_tensor("idf_d", [128, 128], F32, kind="ExternalInput")
    idb_d = nc.dram_tensor("idb_d", [128, 128], BF16, kind="ExternalInput")
    onesb_d = nc.dram_tensor("onesb_d", [128, 128], BF16, kind="ExternalInput")
    onesf_d = nc.dram_tensor("onesf_d", [128, 128], F32, kind="ExternalInput")
    h128_d = nc.dram_tensor("h128_d", [128, 128], BF16, kind="ExternalInput")
    askew_d = nc.dram_tensor("askew_d", [128, SKEW_W], I16, kind="ExternalInput")
    umask_d = nc.dram_tensor("umask_d", [128, 1], F32, kind="ExternalInput")
    cst_d = nc.dram_tensor("cst_d", [128, len(CSTVALS)], F32, kind="ExternalInput")
    out_d = nc.dram_tensor("out", [TOWN, D], F32, kind="ExternalOutput")

    kscr = "ExternalOutput"
    q_scr = nc.dram_tensor("q_scr", [D, TOWN], BF16, kind=kscr)
    ao_scr = nc.dram_tensor("ao_scr", [D, TOWN], BF16, kind=kscr)
    ap_scr = nc.dram_tensor("ap_scr", [TOWN, D], F32, kind=kscr)
    i1_scr = nc.dram_tensor("i1_scr", [D, TOWN], BF16, kind=kscr)
    g_scr = nc.dram_tensor("g_scr", [4096, TOWN], F32, kind=kscr)
    td_scr = nc.dram_tensor("td_scr", [4096, D], BF16, kind=kscr)
    if DEBUG:
        dbg_d = nc.dram_tensor("dbg", [128, 72], F32, kind="ExternalOutput")
    if DEBUG or DBG_KV:
        kv_dbg = nc.dram_tensor("kv_dbg", [128, 2 * 16384], BF16, kind="ExternalOutput")
    if DEBUG or DBG_YF:
        yf_dbg = nc.dram_tensor("yf_dbg", [128, 8192], F32, kind="ExternalOutput")
    if DEBUG or DBG_XNF:
        xnf_dbg = nc.dram_tensor("xnf_dbg", [128, 8192], BF16, kind="ExternalOutput")
    if DEBUG or DBG_WA:
        wa_dbg = nc.dram_tensor("wa_dbg", [128, 4], F32, kind="ExternalOutput")

    qsr = q_scr.rearrange("(o p) q -> p o q", p=128)
    aosr = ao_scr.rearrange("(o p) q -> p o q", p=128)
    apsr = ap_scr.rearrange("(i p) d -> p i d", p=128)
    i1sr = i1_scr.rearrange("(o p) q -> p o q", p=128)
    gsr = g_scr.rearrange("(o p) q -> o p q", p=128)
    gslr = g_scr.rearrange("(c p) q -> p c q", p=128)
    tdsr = td_scr.rearrange("(i p) d -> p i d", p=128)
    outr = out_d.rearrange("(i p) d -> p i d", p=128)
    xor_ = x_own.rearrange("(i p) d -> p i d", p=128)
    wup_r = wup_d.rearrange("(i p) d -> p i d", p=128)
    wdn_r = wdn_d.rearrange("(i p) d -> p i d", p=128)
    wkv_r = wkv_d.rearrange("(i p) d -> i p d", p=128)

    def sb(name, shape, dtype, bag):
        g = nc.sbuf_tensor(name, shape, dtype)
        t = g.__enter__()
        bag.append(g)
        return t

    def ps(name, shape, dtype, bag):
        g = nc.psum_tensor(name, shape, dtype)
        t = g.__enter__()
        bag.append(g)
        return t

    def freebag(bag):
        for g in reversed(bag):
            g.__exit__(None, None, None)
        bag.clear()

    perm = []
    idf = sb("idf", [128, 128], F32, perm)
    idb = sb("idb", [128, 128], BF16, perm)
    onesb = sb("onesb", [128, 128], BF16, perm)
    onesf = sb("onesf", [128, 128], F32, perm)
    h128 = sb("h128", [128, 128], BF16, perm)
    cst = sb("cst", [128, len(CSTVALS)], F32, perm)
    umask = sb("umask", [128, 1], F32, perm)
    bo_s = sb("bo_s", [128, 16], F32, perm)
    svals = sb("svals", [128, 8], F32, perm)
    mxc1b = sb("mxc1b", [128, 16], F32, perm)
    wacc = sb("wacc", [128, 4], F32, perm)
    alpsq = sb("alpsq", [128, 16], F32, perm)
    betab = sb("betab", [128, 16], F32, perm)

    def cc(i):
        return cst[:, i:i + 1]

    for dst, src in ((idf, idf_d), (idb, idb_d), (onesb, onesb_d), (onesf, onesf_d),
                     (h128, h128_d), (cst, cst_d), (umask, umask_d), (bo_s, bo_d)):
        ctx.d(lambda s, dst=dst, src=src: s.dma_start(dst[:], src[:, :]))
    ctx.bar("cb")

    def emit_ln(tag, prev_tag, xs, t0, xt, xcb, sqb, rb_full, rpar, xld_waits=(),
                xc_waits=(), want_cw=False):
        for stage in range(3):
            emit_ln_stage(stage, tag, prev_tag, xs, t0, xt, xcb, sqb, rb_full, rpar,
                          xld_waits, xc_waits, want_cw)

    def emit_ln_stage(stage, tag, prev_tag, xs, t0, xt, xcb, sqb, rb_full, rpar,
                      xld_waits=(), xc_waits=(), want_cw=False):
        rb = rb_full[:, rpar * 8:(rpar + 1) * 8]
        if stage == 0:
            ctx.d(lambda s: s.dma_start(xt[:], xs[t0:t0 + 128, :]), ev=f"{tag}:xld",
                  waits=xld_waits)
            ctx.v(lambda v: v.tensor_reduce(rb[:, 0:1], xt[:], AX, OP.add),
                  waits=[f"{tag}:xld"], flush=True)
            ctx.v(lambda v: v.tensor_single_scalar(rb[:, 1:2], rb[:, 0:1],
                                                   -1.0 / 2048.0, OP.mult), flush=True)
            return
        if stage == 1:
            ctx.v(lambda v: v.tensor_scalar(xcb[:], xt[:], rb[:, 1:2], None, OP.add),
                  ev=f"{tag}:xc", waits=xc_waits)
            ctx.v(lambda v: v.tensor_mul(sqb[:], xcb[:], xcb[:]))
            ctx.v(lambda v: v.tensor_reduce(rb[:, 2:3], sqb[:], AX, OP.add),
                  ev=f"{tag}:ssq",
                  waits=[f"{prev_tag}:rstd"] if prev_tag else ())
            ctx.a(lambda a: a.activation(rb[:, 3:4], rb[:, 2:3], AF.Ln,
                                         bias=cc(C_EPS), scale=cc(C_LNSC)),
                  waits=[f"{tag}:ssq"], flush=True)
            ctx.a(lambda a: a.activation(rb[:, 4:5], rb[:, 3:4], AF.Exp,
                                         bias=cc(C_ZERO), scale=cc(C_MHALF)),
                  ev=f"{tag}:rstd")
            return
        ctx.v(lambda v: v.tensor_scalar(xcb[:], xcb[:], rb[:, 4:5], None, OP.mult),
              ev=f"{tag}:xn", waits=[f"{tag}:rstd"])
        if want_cw:
            ctx.v(lambda v: v.tensor_reduce(rb[:, 5:6], xcb[:], AX, OP.add,
                                            apply_absolute_value=True),
                  ev=f"{tag}:asum",
                  waits=[f"{prev_tag}:cw"] if prev_tag else ())
            ctx.a(lambda a: a.activation(rb[:, 6:7], rb[:, 5:6], AF.Exp,
                                         bias=cc(C_ZERO), scale=cc(C_MONE)),
                  waits=[f"{tag}:asum"], flush=True)
            ctx.a(lambda a: a.activation(rb[:, 6:7], rb[:, 6:7], AF.Ln,
                                         bias=cc(C_ONE), scale=cc(C_ONE)), flush=True)
            ctx.a(lambda a: a.activation(rb[:, 7:8], rb[:, 6:7], AF.Exp,
                                         bias=cc(C_ZERO), scale=cc(C_MONE)),
                  ev=f"{tag}:cw")
            ctx.v(lambda v: v.tensor_scalar(xcb[:], xcb[:], rb[:, 7:8], None, OP.mult),
                  ev=f"{tag}:xn", waits=[f"{tag}:cw"])

    def emit_tr(tag, src, dst, dst_col, ptr, nch, out_stride, evac, dtw=(), ident=None):
        last = [None, None]
        for c in range(nch):
            par = c % 2
            w = list(dtw) if c < 2 else []
            if last[par] is not None:
                w.append(last[par])
            idm = ident if ident is not None else idf
            ctx.t(lambda t, c=c, par=par, idm=idm: t.transpose(
                ptr[par][:], src[:, c * 128:(c + 1) * 128], idm[:]),
                ev=f"{tag}:tr{c}", waits=w)
            o0 = c * out_stride + dst_col
            if evac == "a":
                ctx.a(lambda a, c=c, par=par, o0=o0: a.activation(
                    dst[:, o0:o0 + 128], ptr[par][:], AF.Copy,
                    bias=0.0, scale=1.0),
                    ev=f"{tag}:ev{c}", waits=[f"{tag}:tr{c}"])
            else:
                ctx.v(lambda v, c=c, par=par, o0=o0: v.tensor_copy(
                    dst[:, o0:o0 + 128], ptr[par][:]),
                    ev=f"{tag}:ev{c}", waits=[f"{tag}:tr{c}"])
            last[par] = f"{tag}:ev{c}"
        ctx.events[f"{tag}:T"] = ctx.events[f"{tag}:tr{nch-1}"]
        ctx.events[f"{tag}:E"] = ctx.events[f"{tag}:ev{nch-1}"]

    bag, pbag = [], []
    wqb = sb("wqb", [128, 16 * 2048], BF16, bag)
    wstr = [sb(f"wstr{i}", [128, 2048], F32, bag) for i in range(2)]
    wsb = [sb(f"wsb{i}", [128, 2048], F32, bag) for i in range(2)]
    xt_b = [sb(f"xt{i}", [128, 2048], F32, bag) for i in range(2)]
    xc2_b = [sb(f"xc{i}", [128, 2048], F32, bag) for i in range(2)]
    sq_b = sb("sq", [128, 2048], F32, bag)
    r_b = sb("r", [128, 16], F32, bag)
    xncwfm = sb("xncwfm", [128, 16 * 512], BF16, bag)
    qfm = sb("qfm", [128, 16 * 512], BF16, bag)
    ptrA = [ps(f"ptrA{i}", [128, 128], F32, pbag) for i in range(2)]
    pq = [ps(f"pq{i}", [128, 512], F32, pbag) for i in range(2)]
    psml = ps("psml", [128, 4], F32, pbag)

    for ic in range(16):
        par = ic % 2
        w = [f"cb:d{i}" for i in range(16)] if ic < 2 else [f"q:wc{ic-2}"]
        ctx.d(lambda s, ic=ic, par=par: s.dma_start(
            wstr[par][:], wq_d[ic * 128:(ic + 1) * 128, :]), ev=f"q:wl{ic}", waits=w)
        ctx.v(lambda v, ic=ic, par=par: v.tensor_copy(
            wqb[:, ic * 2048:(ic + 1) * 2048], wstr[par][:]),
            ev=f"q:wc{ic}", waits=[f"q:wl{ic}"])

    wchunks = []
    for k in range(32):
        wchunks.append((wup_d[(k % 16) * 128:(k % 16 + 1) * 128,
                              (k // 16) * 2048:(k // 16 + 1) * 2048], 0))
    for k in range(32):
        wchunks.append((wdn_d[(k % 32) * 128:(k % 32 + 1) * 128,
                              (k // 32) * 2048:(k // 32 + 1) * 2048], 1))
    ctx.v(lambda v: v.memset(wacc[:, 0:4], 0.0), ev="q:wz", waits=["cb:d0"])

    def emit_wsum(k):
        srcw, col = wchunks[k]
        par = k % 2
        ctx.d(lambda s, srcw=srcw, par=par: s.dma_start(wsb[par][:], srcw),
              ev=f"q:wl{k}", waits=["q:wz"] if k < 2 else [f"q:wr{k-2}"])
        ctx.v(lambda v, par=par: v.tensor_reduce(
            wacc[:, 2:3], wsb[par][:], AX, OP.add, apply_absolute_value=True),
            waits=[f"q:wl{k}"], flush=True)
        ctx.v(lambda v, col=col: v.tensor_add(
            wacc[:, col:col + 1], wacc[:, col:col + 1], wacc[:, 2:3]),
            ev=f"q:wr{k}", flush=True)

    for qt in range(4):
        for k in range(qt * 16, qt * 16 + 16):
            emit_wsum(k)
        for tp_ in range(2):
            tgs = [qt * 4 + tp_ * 2, qt * 4 + tp_ * 2 + 1]
            for st_ in range(3):
                for i_, tg in enumerate(tgs):
                    tag = f"q:t{tg}"
                    prevtag = f"q:t{tg-2}" if tg >= 2 else None
                    emit_ln_stage(st_, tag, prevtag, x_own, tg * 128, xt_b[i_],
                                  xc2_b[i_], sq_b, r_b, tg % 2,
                                  xld_waits=[f"q:t{tg-2}:xc"] if tg >= 2 else
                                  [f"cb:d{i}" for i in range(16)],
                                  xc_waits=[f"q:r{tg-2}:T"] if tg >= 2 else (),
                                  want_cw=True)
            for i_, tg in enumerate(tgs):
                tt = tg - qt * 4
                emit_tr(f"q:r{tg}", xc2_b[i_], xncwfm, tt * 128, ptrA, 16, 512, "a",
                        dtw=[f"q:t{tg}:xn"] +
                            ([f"q:r{tg-1}:E"] if tg >= 1 else []) +
                            ([f"q:mm{qt-1}_15_last"] if (tt == 0 and qt > 0) else []))
        trall = [f"q:r{qt * 4 + tt}:E" for tt in range(4)]
        for oc in range(16):
            par = oc % 2
            for ic in range(16):
                w = []
                if ic == 0:
                    w = list(trall)
                    if oc >= 2:
                        w.append(f"q:qe{qt}_{oc-2}")
                    elif qt > 0:
                        w.append(f"q:qe{qt-1}_{oc+14}")
                    if oc < 2:
                        w.append(f"q:qs{qt-1}" if qt > 0 else None)
                ctx.t(lambda t, oc=oc, ic=ic, par=par: t.matmul(
                    pq[par][:], wqb[:, ic * 2048 + oc * 128: ic * 2048 + oc * 128 + 128],
                    xncwfm[:, ic * 512:(ic + 1) * 512],
                    start=(ic == 0), stop=(ic == 15)),
                    ev=(f"q:mm{qt}_{oc}" if ic == 15 else None), waits=w)
            ctx.a(lambda a, oc=oc, par=par: a.activation(
                qfm[:, oc * 512:(oc + 1) * 512], pq[par][:], AF.Copy,
                bias=0.0, scale=CSTVALS[C_QSC]),
                ev=f"q:qe{qt}_{oc}", waits=[f"q:mm{qt}_{oc}"] +
                ([f"q:qs{qt-1}"] if (qt > 0 and oc < 2) else []))
        ctx.events[f"q:mm{qt}_15_last"] = ctx.events[f"q:mm{qt}_15"]
        ctx.d(lambda s, qt=qt: s.dma_start(
            qsr[:, :, qt * 512:(qt + 1) * 512],
            qfm[:].rearrange("p (o c) -> p o c", c=512)),
            ev=f"q:qs{qt}", waits=[f"q:qe{qt}_{o}" for o in range(16)])

    ctx.t(lambda t: t.matmul(psml[:, 0:2], onesf[:], wacc[:, 0:2],
                             start=True, stop=True), ev="q:ws", waits=["q:wr63"])
    ctx.v(lambda v: v.tensor_scalar(svals[:, 0:1], psml[:, 0:1],
                                    1.0 / 8388608.0, 1e-5, OP.mult, OP.max),
          waits=["q:ws"], flush=True)
    ctx.v(lambda v: v.tensor_scalar(svals[:, 2:3], psml[:, 1:2],
                                    1.0 / 8388608.0, 1e-5, OP.mult, OP.max), flush=True)
    ctx.v(lambda v: v.reciprocal(svals[:, 1:2], svals[:, 0:1]), flush=True)
    ctx.v(lambda v: v.reciprocal(svals[:, 3:4], svals[:, 2:3]), ev="a:sv", flush=True)
    if DEBUG or DBG_WA:
        ctx.d(lambda s: s.dma_start(wa_dbg[:, :], wacc[:]), waits=["a:sv"])

    ctx.bar("q")
    freebag(pbag)
    freebag(bag)

    bag, pbag = [], []
    xt_b = [sb(f"fxt{i}", [128, 2048], F32, bag) for i in range(2)]
    xc2_b = [sb(f"fxc{i}", [128, 2048], F32, bag) for i in range(2)]
    sq_b = sb("fsq", [128, 2048], F32, bag)
    r_b = sb("fr", [128, 16], F32, bag)
    yfm = sb("yfm", [128, 16 * 512], F32, bag)
    hia = sb("hia", [128, 2048], BF16, bag)
    loa = sb("loa", [128, 2048], BF16, bag)
    ubuf = sb("ubuf", [128, 2048], F32, bag)
    i32b = sb("i32b", [128, 2048], I32, bag)
    gab = sb("gab", [128, 2048], F32, bag)
    gbb = sb("gbb", [128, 2048], F32, bag)
    intstm = sb("intstm", [128, 2048], BF16, bag)
    intsfm = sb("intsfm", [128, 16 * 512], BF16, bag)
    rq = sb("rq", [128, 4], F32, bag)
    ptrA = [ps(f"fptr{i}", [128, 128], F32, pbag) for i in range(2)]
    ptm = ps("ptm", [128, 2048], F32, pbag)
    pbtr = [ps(f"fbtr{i}", [128, 128], BF16, pbag) for i in range(2)]

    for qt in range(4):
        for tp_ in range(2):
            tgs = [qt * 4 + tp_ * 2, qt * 4 + tp_ * 2 + 1]
            for st_ in range(3):
                for i_, tg in enumerate(tgs):
                    tag = f"f:t{tg}"
                    prevtag = f"f:t{tg-2}" if tg >= 2 else None
                    emit_ln_stage(st_, tag, prevtag, x_own, tg * 128, xt_b[i_],
                                  xc2_b[i_], sq_b, r_b, tg % 2,
                                  xld_waits=[f"f:t{tg-2}:xc"] if tg >= 2 else
                                  [f"q:d{i}" for i in range(16)] + ["q:t", "q:v", "q:a"],
                                  xc_waits=[f"f:r{tg-2}:T"] if tg >= 2 else ())
            for i_, tg in enumerate(tgs):
                tt = tg - qt * 4
                emit_tr(f"f:r{tg}", xc2_b[i_], yfm, tt * 128, ptrA, 16, 512, "v",
                        dtw=[f"f:t{tg}:xn"] +
                            ([f"f:r{tg-1}:E"] if tg >= 1 else []) +
                            ([f"f:losplit{qt-1}"] if (tt == 0 and qt > 0) else []))
        ydone = [f"f:r{qt * 4 + tt}:E" for tt in range(4)]
        bit = 1
        k = 0
        while bit < 16:
            for c in range(16):
                if c & bit:
                    continue
                a0, b0 = c * 512, (c | bit) * 512
                ctx.v(lambda v, a0=a0, b0=b0: v.tensor_sub(
                    yfm[:, b0:b0 + 512], yfm[:, a0:a0 + 512], yfm[:, b0:b0 + 512]),
                    waits=ydone if k == 0 else ())
                ctx.v(lambda v, a0=a0, b0=b0: v.scalar_tensor_tensor(
                    yfm[:, a0:a0 + 512], yfm[:, a0:a0 + 512], 2.0,
                    yfm[:, b0:b0 + 512], OP.mult, OP.subtract),
                    ev=f"f:bf{qt}_{k}")
                k += 1
            bit *= 2
        bfd = f"f:bf{qt}_{k-1}"
        if (DEBUG or DBG_YF) and qt == 0:
            ctx.d(lambda s: s.dma_start(yf_dbg[:, :], yfm[:]), waits=[bfd])
        for tt in range(4):
            tg = qt * 4 + tt
            yv = yfm[:].rearrange("p (c q) -> p c q", q=512)[:, :, tt * 128:tt * 128 + 128]
            w = [bfd]
            if tg > 0:
                w.append(f"f:fm{tg-1}_15")
            ctx.v(lambda v, yv=yv: v.tensor_copy(hia[:], yv), ev=f"f:hi{tg}", waits=w)
            ctx.v(lambda v, yv=yv: v.tensor_tensor(
                loa[:].rearrange("p (c q) -> p c q", q=128), yv,
                hia[:].rearrange("p (c q) -> p c q", q=128), OP.subtract),
                ev=f"f:lo{tg}")
            for c in range(16):
                ctx.t(lambda t, c=c: t.matmul(
                    ptm[:, c * 128:(c + 1) * 128], hia[:, c * 128:(c + 1) * 128],
                    h128[:], start=(c % 4 == 0), stop=False),
                    waits=[f"f:lo{tg}"] +
                          ([f"f:u{tg-1}"] if (c == 0 and tg > 0) else []))
                ctx.t(lambda t, c=c: t.matmul(
                    ptm[:, c * 128:(c + 1) * 128], loa[:, c * 128:(c + 1) * 128],
                    h128[:], start=False, stop=(c % 4 == 3)),
                    ev=f"f:fm{tg}_{c}")
            mmd = [f"f:fm{tg}_{c}" for c in range(3, 16, 4)]
            ctx.v(lambda v: v.tensor_reduce(rq[:, 0:1], ptm[:], AX, OP.max,
                                            apply_absolute_value=True), waits=mmd,
                  flush=True)
            ctx.v(lambda v, tg=tg: v.tensor_single_scalar(
                mxc1b[:, tg:tg + 1], rq[:, 0:1],
                float(np.float32(1e-5) * np.float32(math.sqrt(2048.0))), OP.max),
                flush=True)
            ctx.v(lambda v, tg=tg: v.reciprocal(rq[:, 2:3], mxc1b[:, tg:tg + 1]),
                  flush=True)
            ctx.v(lambda v: v.tensor_scalar(ubuf[:], ptm[:], rq[:, 2:3], 127.0,
                                            OP.mult, OP.mult),
                  ev=f"f:u{tg}")
            ctx.v(lambda v: v.tensor_copy(i32b[:], ubuf[:]), ev=f"f:i32{tg}")
            ctx.v(lambda v: v.tensor_sub(gbb[:], ubuf[:], i32b[:]), ev=f"f:ib{tg}")
            ctx.v(lambda v: v.tensor_single_scalar(gab[:], gbb[:], 0.5, OP.is_gt))
            ctx.v(lambda v: v.tensor_single_scalar(gbb[:], gbb[:], -0.5, OP.is_lt))
            ctx.v(lambda v: v.tensor_add(gab[:], gab[:], i32b[:]), ev=f"f:ia{tg}")
            ctx.v(lambda v: v.tensor_sub(intstm[:], gab[:], gbb[:]),
                  ev=f"f:ints{tg}",
                  waits=[f"f:i1r{tg-1}:T"] if tg > 0 else ())
            emit_tr(f"f:i1r{tg}", intstm, intsfm, tt * 128, pbtr, 16, 512, "v",
                    dtw=[f"f:ints{tg}"] +
                        ([f"f:is{qt-1}"] if (tt == 0 and qt > 0) else []), ident=idb)
        ctx.events[f"f:losplit{qt}"] = ctx.events[f"f:lo{qt * 4 + 3}"]
        ctx.d(lambda s, qt=qt: s.dma_start(
            i1sr[:, :, qt * 512:(qt + 1) * 512],
            intsfm[:].rearrange("p (o c) -> p o c", c=512)),
            ev=f"f:is{qt}", waits=[f"f:i1r{qt * 4 + tt}:E" for tt in range(4)])

    ctx.bar("f")
    freebag(pbag)
    freebag(bag)

    kvbag = []
    k_res = sb("k_res", [128, 4 * 4096], BF16, kvbag)
    v_res = sb("v_res", [128, 4 * 32 * 128], BF16, kvbag)
    askew = sb("askew", [128, SKEW_W], I16, kvbag)

    bag, pbag = [], []
    wkvb = sb("wkvb", [128, 16 * 1024], BF16, bag)
    kstr = [sb(f"kstr{i}", [128, 1024], F32, bag) for i in range(2)]
    xt_b = [sb(f"axt{i}", [128, 2048], F32, bag) for i in range(2)]
    xc2_b = [sb(f"axc{i}", [128, 2048], F32, bag) for i in range(4)]
    sq_b = sb("asq", [128, 2048], F32, bag)
    r_b = sb("ar", [128, 32], F32, bag)
    xnfm = sb("axnfm", [128, 16 * 512], BF16, bag)
    vtmp = sb("vtmp", [128, 512], BF16, bag)

    ptrA = [ps(f"aptr{i}", [128, 128], F32, pbag) for i in range(2)]
    pkv = [ps(f"pkv{i}", [128, 512], F32, pbag) for i in range(2)]
    pvtr = [ps(f"avtr{i}", [128, 128], BF16, pbag) for i in range(2)]

    ctx.d(lambda s: s.dma_start(askew[:], askew_d[:, :]), waits=[f"f:d{i}" for i in range(16)])
    for ic in range(16):
        par = ic % 2
        w = [f"f:d{i}" for i in range(16)] if ic < 2 else [f"a:wkc{ic-2}"]
        ctx.d(lambda s, ic=ic, par=par: s.dma_start(kstr[par][:], wkv_r[ic, :, :]),
              ev=f"a:wkl{ic}", waits=w)
        ctx.v(lambda v, ic=ic, par=par: v.tensor_copy(
            wkvb[:, ic * 1024:(ic + 1) * 1024], kstr[par][:]),
            ev=f"a:wkc{ic}", waits=[f"a:wkl{ic}"])
    for grp in range(8):
        for tp_ in range(2):
            tgs = [grp * 4 + tp_ * 2, grp * 4 + tp_ * 2 + 1]
            for st_ in range(3):
                for i_, tg in enumerate(tgs):
                    tag = f"a:t{tg}"
                    prevtag = f"a:t{tg-2}" if tg >= 2 else None
                    emit_ln_stage(st_, tag, prevtag, x_full, tg * 128, xt_b[tg % 2],
                                  xc2_b[tg % 4], sq_b, r_b, tg % 4,
                                  xld_waits=[f"a:t{tg-2}:xc"] if tg >= 2 else
                                  ["f:v", "f:t", "f:a", "f:g"],
                                  xc_waits=[f"a:r{tg-4}:T"] if tg >= 4 else ())
            for i_, tg in enumerate(tgs):
                tt = tg - grp * 4
                emit_tr(f"a:r{tg}", xc2_b[tg % 4], xnfm, tt * 128, ptrA, 16, 512, "a",
                        dtw=[f"a:t{tg}:xn"] +
                            ([f"a:r{tg-1}:E"] if tg >= 1 else []) +
                            ([f"a:kv{grp-1}_7mm"] if (tt == 0 and grp > 0) else []))
        trall = [f"a:r{grp * 4 + tt}:E" for tt in range(4)]
        if (DEBUG or DBG_XNF) and grp == 0:
            ctx.d(lambda s: s.dma_start(xnf_dbg[:, :], xnfm[:]), waits=trall)
        for oc in range(8):
            par = oc % 2
            for ic in range(16):
                w = []
                if ic == 0:
                    w = list(trall) + ["a:wkc15"]
                    if oc >= 2:
                        w.append(f"a:kve{grp}_{oc-2}")
                    elif grp > 0:
                        w.append(f"a:kve{grp-1}_{oc+6}")
                ctx.t(lambda t, oc=oc, ic=ic, par=par: t.matmul(
                    pkv[par][:], wkvb[:, ic * 1024 + oc * 128: ic * 1024 + oc * 128 + 128],
                    xnfm[:, ic * 512:(ic + 1) * 512], start=(ic == 0), stop=(ic == 15)),
                    ev=(f"a:kvm{grp}_{oc}" if ic == 15 else None), waits=w)
            if oc < 4:
                ctx.a(lambda a, oc=oc, grp=grp, par=par: a.activation(
                    k_res[:, oc * 4096 + grp * 512: oc * 4096 + grp * 512 + 512],
                    pkv[par][:], AF.Copy, bias=0.0, scale=1.0),
                    ev=f"a:kve{grp}_{oc}", waits=[f"a:kvm{grp}_{oc}"])
            else:
                gkv = oc - 4
                w = [f"a:kvm{grp}_{oc}"]
                if oc > 4:
                    w.append(f"a:vtr{grp}_{oc-1}_3")
                elif grp > 0:
                    w.append(f"a:vtr{grp-1}_7_3")
                ctx.a(lambda a, par=par: a.activation(
                    vtmp[:], pkv[par][:], AF.Copy, bias=0.0, scale=1.0),
                    ev=f"a:vt{grp}_{oc}", waits=w)
                for jl in range(4):
                    tpar = jl % 2
                    w2 = [f"a:vt{grp}_{oc}"]
                    if jl >= 2:
                        w2.append(f"a:vev{grp}_{oc}_{jl-2}")
                    ctx.t(lambda t, jl=jl, tpar=tpar: t.transpose(
                        pvtr[tpar][:], vtmp[:, jl * 128:(jl + 1) * 128], idb[:]),
                        ev=f"a:vtr{grp}_{oc}_{jl}", waits=w2)
                    col = (gkv * 32 + grp * 4 + jl) * 128
                    ctx.v(lambda v, col=col, tpar=tpar: v.tensor_copy(
                        v_res[:, col:col + 128], pvtr[tpar][:]),
                        ev=f"a:vev{grp}_{oc}_{jl}", waits=[f"a:vtr{grp}_{oc}_{jl}"])
                ctx.events[f"a:kve{grp}_{oc}"] = ctx.events[f"a:vev{grp}_{oc}_3"]
            if oc == 7:
                ctx.events[f"a:kv{grp}_7mm"] = ctx.events[f"a:kvm{grp}_{oc}"]

    if DEBUG or DBG_KV:
        ctx.d(lambda s: s.dma_start(kv_dbg[:, 0:16384], k_res[:]),
              waits=[f"a:kve7_{o}" for o in range(8)])
        ctx.d(lambda s: s.dma_start(kv_dbg[:, 16384:32768], v_res[:]))

    ctx.bar("pa")
    freebag(pbag)
    freebag(bag)

    bag, pbag = [], []
    qt_b = [sb(f"qtl{i}", [128, 16 * 512], BF16, bag) for i in range(2)]
    ao_b = sb("ao", [128, 16 * 512], BF16, bag)
    t_b = [sb(f"tb{i}", [128, 512], F32, bag) for i in range(4)]
    p_b = [sb(f"pb{i}", [128, 512], BF16, bag) for i in range(4)]
    zg_b = [sb(f"zg{i}", [128, 512], F32, bag) for i in range(2)]
    rz_b = [sb(f"rz{i}", [128, 512], F32, bag) for i in range(2)]
    vme_b = sb("vme", [128, 4 * 64], BF16, bag)
    dt_b = sb("dt", [128, 64], F32, bag)

    vmbag = []
    pvm = ps("pvm", [128, 64], F32, vmbag)
    for gkv in range(4):
        for j in range(32):
            w = []
            if gkv == 0 and j == 0:
                w = ["pa:t", "pa:v", "pa:a"]
            elif j == 0:
                w = [f"p2:vme{gkv-1}"]
            ctx.t(lambda t, gkv=gkv, j=j: t.matmul(
                pvm[:], v_res[:, (gkv * 32 + j) * 128:(gkv * 32 + j) * 128 + 128],
                onesb[:, 0:64], start=(j == 0), stop=(j == 31)),
                ev=(f"p2:vmm{gkv}" if j == 31 else None), waits=w)
        ctx.a(lambda a, gkv=gkv: a.activation(
            vme_b[:, gkv * 64:(gkv + 1) * 64], pvm[:], AF.Copy,
            bias=0.0, scale=CSTVALS[C_VMSC]),
            ev=f"p2:vme{gkv}", waits=[f"p2:vmm{gkv}"])
    freebag(vmbag)

    ppv = [ps(f"ppv{i}", [128, 512], F32, pbag) for i in range(2)]
    pz = [ps(f"pz{i}", [128, 512], F32, pbag) for i in range(2)]
    psc = [ps(f"psc{i}", [128, 512], F32, pbag) for i in range(4)]

    def njof(qt, pr):
        return len(kept_js(2 * pr + 1, qt))

    def _lastevt(kind, qt, pr, hi_, k):
        if k >= 0:
            return f"p2:{kind}{qt}_{pr}_{2 * pr + hi_}_{k}"
        if pr > 0:
            n = njof(qt, pr - 1)
            return f"p2:{kind}{qt}_{pr-1}_{2 * (pr - 1) + hi_}_{n + k}"
        if qt > 0:
            n = njof(qt - 1, 7)
            return f"p2:{kind}{qt-1}_7_{14 + hi_}_{n + k}"
        return None

    def lastap(qt, pr, hi_, k):
        return _lastevt("ap", qt, pr, hi_, k)

    def lastex(qt, pr, hi_, k):
        return _lastevt("ex", qt, pr, hi_, k)

    def lastpv(qt, pr, hi_, k):
        return _lastevt("pz", qt, pr, hi_, k)

    ctx.d(lambda s: s.dma_start(
        qt_b[0][:].rearrange("p (o c) -> p o c", c=512), qsr[:, :, 0:512]),
        ev="p2:ql0", waits=["q:qs0", "pa:t", "pa:v", "pa:a"] +
        [f"pa:d{i}" for i in range(16)])

    for qt in range(4):
        qpar = qt % 2
        if qt + 1 < 4:
            ctx.d(lambda s, qt=qt: s.dma_start(
                qt_b[(qt + 1) % 2][:].rearrange("p (o c) -> p o c", c=512),
                qsr[:, :, (qt + 1) * 512:(qt + 2) * 512]),
                ev=f"p2:ql{qt+1}",
                waits=[f"q:qs{qt+1}"] +
                      ([f"p2:sc{qt-1}_7_15_{njof(qt-1, 7) - 1}"] if qt >= 1 else []))
        for pr in range(8):
            h0, h1 = 2 * pr, 2 * pr + 1
            gkv = h0 // 4
            jset = kept_js(h1, qt)
            nj = len(jset)
            for k in range(nj + 1):
                if k < nj:
                    j = jset[k]
                    for hi_, h in enumerate((h0, h1)):
                        sbk = 2 * hi_ + (k % 2)
                        w = [f"p2:ql{qt}"]
                        ap = lastap(qt, pr, hi_, k - 2)
                        if ap:
                            w.append(ap)
                        if qt == 0 and pr == 0 and k == 0:
                            w += ["pa:t", "pa:a", "pa:v", f"p2:vme3"]
                        ctx.t(lambda t, gkv=gkv, j=j, h=h, sbk=sbk, qpar=qpar: t.matmul(
                            psc[sbk][:],
                            k_res[:, gkv * 4096 + j * 128: gkv * 4096 + j * 128 + 128],
                            qt_b[qpar][:, h * 512:(h + 1) * 512],
                            start=True, stop=True),
                            ev=f"p2:sc{qt}_{pr}_{h}_{k}", waits=w)
                kk = k - 1
                if kk < 0 or kk > nj - 1:
                    continue
                jj = jset[kk]
                for hi_, h in enumerate((h0, h1)):
                    sbk = 2 * hi_ + (kk % 2)
                    xoff = qt * 1024 - jj * 128 + SKEW_BASE
                    w = [f"p2:sc{qt}_{pr}_{h}_{kk}"]
                    ex = lastex(qt, pr, hi_, kk - 2)
                    if ex:
                        w.append(ex)
                    if qt == 0 and pr == 0 and kk == 0:
                        w.append("pa:d")
                    ctx.v(lambda v, h=h, sbk=sbk, xoff=xoff: v.scalar_tensor_tensor(
                        t_b[sbk][:], askew[:, xoff:xoff + 512], SLOPES[h], psc[sbk][:],
                        OP.mult, OP.add),
                        ev=(None if jj == 0 else f"p2:ap{qt}_{pr}_{h}_{kk}"), waits=w)
                    if jj == 0:
                        ctx.v(lambda v, sbk=sbk: v.memset(t_b[sbk][0:64, :], -100000.0),
                              ev=f"p2:ap{qt}_{pr}_{h}_{kk}")
                    w = [f"p2:ap{qt}_{pr}_{h}_{kk}"]
                    pv = lastpv(qt, pr, hi_, kk - 2)
                    if pv:
                        w.append(pv)
                    ctx.a(lambda a, sbk=sbk: a.activation(
                        p_b[sbk][:], t_b[sbk][:], AF.Exp,
                        bias=cc(C_EXPB), scale=cc(C_ONE)),
                        ev=f"p2:ex{qt}_{pr}_{h}_{kk}", waits=w)
                    w = [f"p2:ex{qt}_{pr}_{h}_{kk}"]
                    if kk == 0:
                        if pr > 0:
                            w.append(f"p2:aow{qt}_{pr-1}_{2 * (pr - 1) + hi_}")
                        elif qt > 0:
                            w.append(f"p2:aow{qt-1}_7_{14 + hi_}")
                    ctx.t(lambda t, gkv=gkv, jj=jj, hi_=hi_, sbk=sbk, kk=kk, nj=nj: t.matmul(
                        ppv[hi_][:],
                        v_res[:, (gkv * 32 + jj) * 128:(gkv * 32 + jj) * 128 + 128],
                        p_b[sbk][:], start=(kk == 0), stop=(kk == nj - 1)),
                        ev=(f"p2:pvm{qt}_{pr}_{h}" if kk == nj - 1 else None), waits=w)
                    ctx.t(lambda t, hi_=hi_, sbk=sbk, kk=kk, nj=nj: t.matmul(
                        pz[hi_][:], onesb[:], p_b[sbk][:],
                        start=(kk == 0), stop=(kk == nj - 1)),
                        ev=f"p2:pz{qt}_{pr}_{h}_{kk}")
            for hi_, h in enumerate((h0, h1)):
                ctx.v(lambda v, hi_=hi_: v.tensor_single_scalar(
                    zg_b[hi_][:], pz[hi_][:], 1e-35, OP.max),
                    ev=f"p2:zg{qt}_{pr}_{h}", waits=[f"p2:pz{qt}_{pr}_{h}_{nj - 1}"])
                ctx.a(lambda a, hi_=hi_: a.activation(
                    rz_b[hi_][:], zg_b[hi_][:], AF.Ln,
                    bias=cc(C_ZERO), scale=cc(C_ONE)),
                    waits=[f"p2:zg{qt}_{pr}_{h}"])
                ctx.a(lambda a, hi_=hi_: a.activation(
                    rz_b[hi_][:], rz_b[hi_][:], AF.Exp,
                    bias=cc(C_ZERO), scale=cc(C_MONE)),
                    ev=f"p2:rz{qt}_{pr}_{h}")
                w = [f"p2:rz{qt}_{pr}_{h}", f"p2:pvm{qt}_{pr}_{h}"]
                if qt > 0 and pr == 0:
                    w.append(f"p2:aod{qt-1}")
                ctx.v(lambda v, h=h, hi_=hi_: v.tensor_mul(
                    ao_b[:, h * 512:(h + 1) * 512], ppv[hi_][:], rz_b[hi_][:]),
                    ev=f"p2:ao0{qt}_{pr}_{h}", waits=w)
                if qt == 0:
                    ctx.v(lambda v, h=h, gkv=gkv: v.tensor_sub(
                        dt_b[:], vme_b[:, gkv * 64:(gkv + 1) * 64],
                        ao_b[:, h * 512:h * 512 + 64]),
                        waits=[f"p2:ao0{qt}_{pr}_{h}", f"p2:vme{gkv}"], flush=True)
                    ctx.v(lambda v, h=h: v.scalar_tensor_tensor(
                        ao_b[:, h * 512:h * 512 + 64], dt_b[:], umask[:, 0:1],
                        ao_b[:, h * 512:h * 512 + 64], OP.mult, OP.add),
                        ev=f"p2:aow{qt}_{pr}_{h}")
                else:
                    ctx.events[f"p2:aow{qt}_{pr}_{h}"] = ctx.events[f"p2:ao0{qt}_{pr}_{h}"]
        ctx.d(lambda s, qt=qt: s.dma_start(
            aosr[:, :, qt * 512:(qt + 1) * 512],
            ao_b[:].rearrange("p (o c) -> p o c", c=512)),
            ev=f"p2:aod{qt}",
            waits=[f"p2:aow{qt}_{pr}_{h}" for pr in range(8)
                   for h in (2 * pr, 2 * pr + 1)])

    ctx.bar("p2")
    freebag(pbag)
    freebag(bag)
    freebag(kvbag)

    bag, pbag = [], []
    aores = sb("aores", [128, 16 * 2048], BF16, bag)
    wob = sb("wob", [128, 16 * 2048], BF16, bag)
    wstr3 = [sb(f"wstr3{i}", [128, 2048], F32, bag) for i in range(2)]
    apfm = sb("apfm", [128, 2048], F32, bag)
    aptm = sb("aptm", [128, 16 * 128], F32, bag)
    poc3 = ps("poc3", [128, 2048], F32, pbag)
    ptr3 = [ps(f"ptr3{i}", [128, 128], F32, pbag) for i in range(2)]

    for ic in range(16):
        par = ic % 2
        w = [f"p2:d{i}" for i in range(16)] if ic < 2 else [f"p3:wc{ic-2}"]
        ctx.d(lambda s, ic=ic, par=par: s.dma_start(
            wstr3[par][:], wo_d[ic * 128:(ic + 1) * 128, :]),
            ev=f"p3:wl{ic}", waits=w)
        ctx.v(lambda v, ic=ic, par=par: v.tensor_copy(
            wob[:, ic * 2048:(ic + 1) * 2048], wstr3[par][:]),
            ev=f"p3:wc{ic}", waits=[f"p3:wl{ic}"])
    ctx.d(lambda s: s.dma_start(
        aores[:].rearrange("p (o c) -> p o c", c=2048), aosr[:, :, :]),
        ev="p3:aol", waits=[f"p2:d{i}" for i in range(16)])

    for oc in range(16):
        for ic in range(16):
            for ns in range(4):
                w = []
                if ic == 0 and ns == 0:
                    w = (["p3:aol", "p3:wc15"] if oc == 0 else [f"p3:ape{oc-1}"])
                ctx.t(lambda t, oc=oc, ic=ic, ns=ns: t.matmul(
                    poc3[:, ns * 512:(ns + 1) * 512],
                    wob[:, ic * 2048 + oc * 128: ic * 2048 + oc * 128 + 128],
                    aores[:, ic * 2048 + ns * 512: ic * 2048 + ns * 512 + 512],
                    start=(ic == 0), stop=(ic == 15)),
                    ev=(f"p3:mm{oc}" if (ic == 15 and ns == 3) else None), waits=w)
        w = [f"p3:mm{oc}"]
        if oc > 0:
            w.append(f"p3:tr{oc-1}:T")
        ctx.v(lambda v, oc=oc: v.tensor_scalar(
            apfm[:], poc3[:], bo_s[:, oc:oc + 1], None, OP.add),
            ev=f"p3:ape{oc}", waits=w)
        emit_tr(f"p3:tr{oc}", apfm, aptm, 0, ptr3, 16, 128, "v",
                dtw=[f"p3:ape{oc}"] + ([f"p3:aps{oc-1}"] if oc > 0 else []))
        ctx.d(lambda s, oc=oc: s.dma_start(
            apsr[:, :, oc * 128:(oc + 1) * 128],
            aptm[:].rearrange("p (i c) -> p i c", c=128)),
            ev=f"p3:aps{oc}", waits=[f"p3:tr{oc}:E"])

    ctx.bar("p3")
    freebag(pbag)
    freebag(bag)

    bag, pbag = [], []
    i1res = sb("i1res", [128, 16 * 2048], BF16, bag)
    wt4 = [sb(f"wt4{i}", [128, 16 * 128], F32, bag) for i in range(2)]
    tern4 = [sb(f"tern4{i}", [128, 16 * 128], BF16, bag) for i in range(2)]
    ga4 = sb("ga4", [128, 2048], F32, bag)
    gb4 = sb("gb4", [128, 2048], F32, bag)
    h4 = sb("h4", [128, 2048], F32, bag)
    g4 = [sb(f"g4{i}", [128, 2048], F32, bag) for i in range(2)]
    poc4 = [ps(f"poc4{i}", [128, 2048], F32, pbag) for i in range(2)]

    ctx.d(lambda s: s.dma_start(
        i1res[:].rearrange("p (o c) -> p o c", c=2048), i1sr[:, :, :]),
        ev="p4:i1l", waits=[f"p3:d{i}" for i in range(16)])
    for oc in range(2):
        ctx.d(lambda s, oc=oc: s.dma_start(
            wt4[oc % 2][:].rearrange("p (i c) -> p i c", c=128),
            wup_r[:, :, oc * 128:(oc + 1) * 128]),
            ev=f"p4:wl{oc}", waits=[f"p3:d{i}" for i in range(16)])
    for ocx in range(32):
        par = ocx % 2
        w = [f"p4:wl{ocx}", "a:sv"]
        if ocx >= 2:
            w.append(f"p4:mm{ocx-2}")
        ctx.v(lambda v, par=par: v.tensor_scalar(
            ga4[:], wt4[par][:], svals[:, 1:2], 0.5, OP.mult, OP.is_gt), waits=w)
        ctx.v(lambda v, par=par: v.tensor_scalar(
            gb4[:], wt4[par][:], svals[:, 1:2], -0.5, OP.mult, OP.is_lt))
        ctx.v(lambda v, par=par: v.tensor_sub(tern4[par][:], ga4[:], gb4[:]),
              ev=f"p4:tq{ocx}")
        if ocx + 2 < 32:
            ctx.d(lambda s, ocx=ocx, par=par: s.dma_start(
                wt4[par][:].rearrange("p (i c) -> p i c", c=128),
                wup_r[:, :, (ocx + 2) * 128:(ocx + 3) * 128]),
                ev=f"p4:wl{ocx+2}", waits=[f"p4:tq{ocx}"])
        for ic in range(16):
            for ns in range(4):
                w = []
                if ic == 0 and ns == 0:
                    w = [f"p4:tq{ocx}"] + (["p4:i1l"] if ocx == 0 else []) + \
                        ([f"p4:gs{ocx-2}"] if ocx >= 2 else [])
                ctx.t(lambda t, ocx=ocx, ic=ic, ns=ns, par=par: t.matmul(
                    poc4[par][:, ns * 512:(ns + 1) * 512],
                    tern4[par][:, ic * 128:(ic + 1) * 128],
                    i1res[:, ic * 2048 + ns * 512: ic * 2048 + ns * 512 + 512],
                    start=(ic == 0), stop=(ic == 15)),
                    ev=(f"p4:mm{ocx}" if (ic == 15 and ns == 3) else None), waits=w)
        ctx.a(lambda a, par=par: a.activation(
            h4[:], poc4[par][:], AF.Relu, bias=0.0, scale=1.0),
            waits=[f"p4:mm{ocx}"], ev=f"p4:h{ocx}")
        w = [f"p4:gs{ocx-2}", f"p4:h{ocx}"] if ocx >= 2 else [f"p4:h{ocx}"]
        ctx.v(lambda v, par=par: v.tensor_mul(g4[par][:], h4[:], h4[:]),
              ev=f"p4:g{ocx}", waits=w)
        ctx.d(lambda s, ocx=ocx, par=par: s.dma_start(gsr[ocx, :, :], g4[par][:]),
              ev=f"p4:gs{ocx}", waits=[f"p4:g{ocx}"])

    ctx.bar("p4")
    freebag(pbag)
    freebag(bag)

    bag, pbag = [], []
    i2half = sb("i2half", [128, 32 * 1024], BF16, bag)
    big45 = sb("big45", [128, 8192], F32, bag)
    gsl = big45
    wt6 = [big45[:, 0:4096], big45[:, 4096:8192]]
    hia5 = sb("hia5", [128, 2048], BF16, bag)
    loa5 = sb("loa5", [128, 2048], BF16, bag)
    t2s = sb("t2s", [128, 4096], F32, bag)
    u5 = sb("u5", [128, 1024], F32, bag)
    i325 = sb("i325", [128, 1024], I32, bag)
    ga5 = sb("ga5", [128, 2048], F32, bag)
    gb5 = sb("gb5", [128, 2048], F32, bag)
    ints5 = sb("ints5", [128, 4096], BF16, bag)
    rq5 = sb("rq5", [128, 8], F32, bag)
    tern6 = [sb(f"tern6{i}", [128, 32 * 128], BF16, bag) for i in range(2)]
    ffn6 = sb("ffn6", [128, 1024], F32, bag)
    ap6 = [sb(f"ap6{i}", [128, 8 * 128], F32, bag) for i in range(2)]
    xo6 = [sb(f"xo6{i}", [128, 8 * 128], F32, bag) for i in range(2)]
    apx6 = [sb(f"apx6{i}", [128, 8 * 128], F32, bag) for i in range(2)]
    ob6 = [sb(f"ob6{i}", [128, 8 * 128], F32, bag) for i in range(2)]
    for half in range(2):
        p5ps = []
        ptm5 = [ps(f"ptm5_{half}_{i}", [128, 1024], F32, p5ps) for i in range(2)]
        pbtr5 = [ps(f"pbtr5{i}_{half}", [128, 128], BF16, p5ps) for i in range(2)]
        for slq in range(4):
            sl = half * 4 + slq
            if sl == 0:
                w = [f"p4:d{i}" for i in range(16)] + ["p4:t", "p4:v"]
            elif slq == 0:
                w = [f"p6:od0_15"]
            else:
                w = [f"p5:ld{sl-1}"]
            ctx.d(lambda s, sl=sl: s.dma_start(
                gsl[:].rearrange("p (c q) -> p c q", q=256),
                gslr[:, :, sl * 256:(sl + 1) * 256]),
                ev=f"p5:gl{sl}", waits=w)
            bit = 1
            k = 0
            while bit < 32:
                for c in range(32):
                    if c & bit:
                        continue
                    a0, b0 = c * 256, (c | bit) * 256
                    ctx.v(lambda v, a0=a0, b0=b0: v.tensor_sub(
                        gsl[:, b0:b0 + 256], gsl[:, a0:a0 + 256], gsl[:, b0:b0 + 256]),
                        waits=[f"p5:gl{sl}"] if k == 0 else ())
                    ctx.v(lambda v, a0=a0, b0=b0: v.scalar_tensor_tensor(
                        gsl[:, a0:a0 + 256], gsl[:, a0:a0 + 256], 2.0,
                        gsl[:, b0:b0 + 256], OP.mult, OP.subtract),
                        ev=f"p5:bf{sl}_{k}")
                    k += 1
                bit *= 2
            bfd = f"p5:bf{sl}_{k-1}"
            for tt in range(2):
                tg = sl * 2 + tt
                for fh in range(2):
                    gv = gsl[:].rearrange("p (c q) -> p c q", q=256)[
                        :, fh * 16:(fh + 1) * 16, tt * 128:tt * 128 + 128]
                    w = [bfd]
                    if fh == 1:
                        w.append(f"p5:fm{tg}_1_7")
                    elif tg > 0:
                        w.append(f"p5:fm{tg-1}_3_7")
                    ctx.v(lambda v, gv=gv: v.tensor_copy(hia5[:], gv),
                          ev=f"p5:hi{tg}_{fh}", waits=w)
                    ctx.v(lambda v, gv=gv: v.tensor_tensor(
                        loa5[:].rearrange("p (c q) -> p c q", q=128), gv,
                        hia5[:].rearrange("p (c q) -> p c q", q=128), OP.subtract),
                        ev=f"p5:lo{tg}_{fh}")
                    for fqh in range(2):
                        fq = fh * 2 + fqh
                        par5 = fq % 2
                        for cq in range(8):
                            ch5 = fqh * 8 + cq
                            ctx.t(lambda t, cq=cq, ch5=ch5, par5=par5: t.matmul(
                                ptm5[par5][:, cq * 128:(cq + 1) * 128],
                                hia5[:, ch5 * 128:(ch5 + 1) * 128],
                                h128[:], start=(cq % 4 == 0), stop=False),
                                waits=[f"p5:lo{tg}_{fh}"] +
                                      ([f"p5:t2e{tg}_{fq-2}"] if (cq == 0 and fq >= 2) else
                                       ([f"p5:t2e{tg-1}_{fq+2}"] if (cq == 0 and tg > 0) else [])))
                            ctx.t(lambda t, cq=cq, ch5=ch5, par5=par5: t.matmul(
                                ptm5[par5][:, cq * 128:(cq + 1) * 128],
                                loa5[:, ch5 * 128:(ch5 + 1) * 128],
                                h128[:], start=False, stop=(cq % 4 == 3)),
                                ev=f"p5:fm{tg}_{fq}_{cq}")
                        ctx.a(lambda a, fq=fq, par5=par5: a.activation(
                            t2s[:, fq * 1024:(fq + 1) * 1024], ptm5[par5][:],
                            AF.Copy, bias=0.0, scale=1.0),
                            ev=f"p5:t2e{tg}_{fq}",
                            waits=[f"p5:fm{tg}_{fq}_3", f"p5:fm{tg}_{fq}_7"])
                ctx.v(lambda v: v.tensor_reduce(rq5[:, 0:1], t2s[:], AX, OP.max,
                                                apply_absolute_value=True),
                      waits=[f"p5:t2e{tg}_{fq}" for fq in range(4)], flush=True)
                ctx.v(lambda v, tg=tg: v.tensor_scalar(
                    rq5[:, 3:4], mxc1b[:, tg:tg + 1], svals[:, 0:1],
                    float(1.0 / (127.0 * math.sqrt(2048.0))), OP.mult, OP.mult),
                    waits=["a:sv"], flush=True)
                ctx.v(lambda v, tg=tg: v.tensor_mul(
                    alpsq[:, tg:tg + 1], rq5[:, 3:4], rq5[:, 3:4]), ev=f"p5:alp{tg}",
                    flush=True)
                ctx.v(lambda v, tg=tg: v.reciprocal(rq5[:, 4:5], alpsq[:, tg:tg + 1]),
                      flush=True)
                ctx.v(lambda v: v.tensor_single_scalar(rq5[:, 5:6], rq5[:, 4:5],
                                                       float(1e-5 * 64.0), OP.mult),
                      flush=True)
                ctx.v(lambda v: v.tensor_max(rq5[:, 1:2], rq5[:, 0:1], rq5[:, 5:6]),
                      flush=True)
                ctx.v(lambda v: v.reciprocal(rq5[:, 2:3], rq5[:, 1:2]), flush=True)
                ctx.v(lambda v: v.tensor_single_scalar(rq5[:, 7:8], rq5[:, 2:3],
                                                       127.0, OP.mult),
                      ev=f"p5:inv{tg}", flush=True)
                ctx.v(lambda v, tg=tg: v.tensor_scalar(
                    rq5[:, 6:7], rq5[:, 1:2], alpsq[:, tg:tg + 1], svals[:, 2:3],
                    OP.mult, OP.mult), flush=True)
                ctx.v(lambda v, tg=tg: v.tensor_single_scalar(
                    betab[:, tg:tg + 1], rq5[:, 6:7],
                    float(1.0 / (127.0 * 64.0)), OP.mult), ev=f"p5:beta{tg}",
                    flush=True)
                for quar in range(4):
                    q0 = quar * 1024
                    gh = ga5[:, (quar % 2) * 1024:(quar % 2) * 1024 + 1024]
                    gh2 = gb5[:, (quar % 2) * 1024:(quar % 2) * 1024 + 1024]
                    w = [f"p5:inv{tg}"]
                    if quar >= 1:
                        w.append(f"p5:iq{tg}_{quar-1}a")
                    elif tg > 0:
                        w.append(f"p5:iq{tg-1}_3a")
                    ctx.a(lambda a, q0=q0: a.activation(
                        u5[:], t2s[:, q0:q0 + 1024], AF.Copy, bias=0.0,
                        scale=rq5[:, 7:8]), waits=w, ev=f"p5:u{tg}_{quar}")
                    ctx.v(lambda v: v.tensor_copy(i325[:], u5[:]),
                          waits=[f"p5:u{tg}_{quar}"], ev=f"p5:i32{tg}_{quar}")
                    ctx.v(lambda v, gh2=gh2: v.tensor_sub(gh2, u5[:], i325[:]))
                    ctx.v(lambda v, gh=gh, gh2=gh2: v.tensor_single_scalar(
                        gh, gh2, 0.5, OP.is_gt))
                    ctx.v(lambda v, gh2=gh2: v.tensor_single_scalar(
                        gh2, gh2, -0.5, OP.is_lt))
                    ctx.v(lambda v, gh=gh: v.tensor_add(gh, gh, i325[:]),
                          ev=f"p5:iq{tg}_{quar}a")
                    w = [f"p5:itr{tg-1}_31"] if (quar == 0 and tg > 0) else []
                    ctx.v(lambda v, q0=q0, gh=gh, gh2=gh2: v.tensor_sub(
                        ints5[:, q0:q0 + 1024], gh, gh2),
                        ev=f"p5:iq{tg}_{quar}", waits=w)
                for c in range(32):
                    par = c % 2
                    w = [f"p5:iq{tg}_{c // 8}"]
                    if c >= 2:
                        w.append(f"p5:ie{tg}_{c-2}")
                    ctx.t(lambda t, c=c, par=par: t.transpose(
                        pbtr5[par][:], ints5[:, c * 128:(c + 1) * 128], idb[:]),
                        ev=f"p5:itr{tg}_{c}", waits=w)
                    toff = slq * 256 + tt * 128
                    w2 = [f"p5:itr{tg}_{c}"]
                    if tg == half * 8 and c == 0:
                        w2.append(f"p6:mm{1 - half}_15" if half == 1 else None)
                    ctx.a(lambda a, c=c, par=par, toff=toff: a.activation(
                        i2half[:, c * 1024 + toff: c * 1024 + toff + 128],
                        pbtr5[par][:], AF.Copy, bias=0.0, scale=1.0),
                        ev=f"p5:ie{tg}_{c}", waits=[x for x in w2 if x])
            ctx.events[f"p5:ld{sl}"] = ctx.events[f"p5:lo{sl * 2 + 1}_1"]
        i2done = [f"p5:ie{(half * 4 + slq) * 2 + tt}_{c}"
                  for slq in range(4) for tt in range(2) for c in range(32)]
        freebag(p5ps)

        p6ps = []
        pdn = [ps(f"pdn{i}_{half}", [128, 1024], F32, p6ps) for i in range(2)]
        ptr6 = [ps(f"ptr6{i}_{half}", [128, 128], F32, p6ps) for i in range(2)]

        def p6_load_quant(oc, extra):
            par = oc % 2
            if half == 0:
                ctx.d(lambda s, oc=oc, par=par: s.dma_start(
                    wt6[par].rearrange("p (i c) -> p i c", c=128),
                    wdn_r[:, :, oc * 128:(oc + 1) * 128]),
                    ev=f"p6:wl{oc}", waits=[f"p5:ld{3}"] + extra)
                ctx.v(lambda v, par=par: v.tensor_scalar(
                    ga5[:], wt6[par][:, 0:2048], svals[:, 3:4], 0.5,
                    OP.mult, OP.is_gt), waits=[f"p6:wl{oc}", "a:sv"] + extra)
                ctx.v(lambda v, par=par: v.tensor_scalar(
                    gb5[:], wt6[par][:, 0:2048], svals[:, 3:4], -0.5,
                    OP.mult, OP.is_lt))
                ctx.v(lambda v, par=par: v.tensor_sub(
                    tern6[par][:, 0:2048], ga5[:], gb5[:]))
                ctx.v(lambda v, par=par: v.tensor_scalar(
                    ga5[:], wt6[par][:, 2048:4096], svals[:, 3:4], 0.5,
                    OP.mult, OP.is_gt))
                ctx.v(lambda v, par=par: v.tensor_scalar(
                    gb5[:], wt6[par][:, 2048:4096], svals[:, 3:4], -0.5,
                    OP.mult, OP.is_lt))
                ctx.v(lambda v, par=par: v.tensor_sub(
                    tern6[par][:, 2048:4096], ga5[:], gb5[:]), ev=f"p6:tq0_{oc}")
                ctx.d(lambda s, oc=oc, par=par: s.dma_start(
                    tdsr[:, :, oc * 128:(oc + 1) * 128],
                    tern6[par][:].rearrange("p (i c) -> p i c", c=128)),
                    ev=f"p6:tds{oc}", waits=[f"p6:tq0_{oc}"])
            else:
                ctx.d(lambda s, oc=oc, par=par: s.dma_start(
                    tern6[par][:].rearrange("p (i c) -> p i c", c=128),
                    tdsr[:, :, oc * 128:(oc + 1) * 128]),
                    ev=f"p6:tq1_{oc}", waits=[f"p6:tds{oc}"] + extra)

        for ocx in range(16):
            par = ocx % 2
            if ocx == 0:
                fw = [] if half == 0 else [f"p6:od0_15"]
                p6_load_quant(0, fw)
                p6_load_quant(1, fw)
            for ic in range(32):
                for ns in range(2):
                    w = []
                    if ic == 0 and ns == 0:
                        w = [f"p6:tq{half}_{ocx}"] + (i2done if ocx == 0 else []) + \
                            ([f"p6:fe{half}_{ocx-2}"] if ocx >= 2 else [])
                    ctx.t(lambda t, ocx=ocx, ic=ic, ns=ns, par=par: t.matmul(
                        pdn[par][:, ns * 512:(ns + 1) * 512],
                        tern6[par][:, ic * 128:(ic + 1) * 128],
                        i2half[:, ic * 1024 + ns * 512: ic * 1024 + ns * 512 + 512],
                        start=(ic == 0), stop=(ic == 31)),
                        ev=(f"p6:mm{half}_{ocx}" if (ic == 31 and ns == 1) else None),
                        waits=w)
            if ocx + 2 < 16:
                ex = [f"p6:mm{half}_{ocx}"]
                if half == 0:
                    ex.append(f"p6:tds{ocx}")
                p6_load_quant(ocx + 2, ex)
            w = [f"p6:mm{half}_{ocx}"]
            if ocx > 0:
                w.append(f"p6:ftr{half}_{ocx-1}_7")
            ctx.a(lambda a, par=par: a.activation(
                ffn6[:], pdn[par][:], AF.Copy, bias=0.0, scale=1.0),
                ev=f"p6:fe{half}_{ocx}", waits=w)
            ctx.d(lambda s, ocx=ocx, par=par, half=half: s.dma_start(
                ap6[par][:].rearrange("p (i c) -> p i c", c=128),
                apsr[:, half * 8:(half + 1) * 8, ocx * 128:(ocx + 1) * 128]),
                ev=f"p6:apl{half}_{ocx}",
                waits=[f"p3:aps{ocx}"] +
                      ([f"p6:apx{half}_{ocx-2}"] if ocx >= 2 else
                       ([f"p6:apx0_{14 + ocx}"] if half == 1 else [])))
            ctx.d(lambda s, ocx=ocx, par=par, half=half: s.dma_start(
                xo6[par][:].rearrange("p (i c) -> p i c", c=128),
                xor_[:, half * 8:(half + 1) * 8, ocx * 128:(ocx + 1) * 128]),
                ev=f"p6:xl{half}_{ocx}")
            obp = ocx % 2
            ctx.v(lambda v, par=par, obp=obp: v.tensor_add(
                apx6[obp][:], ap6[par][:], xo6[par][:]),
                ev=f"p6:apx{half}_{ocx}",
                waits=[f"p6:apl{half}_{ocx}", f"p6:xl{half}_{ocx}"] +
                      ([f"p6:ob{half}_{ocx-2}_7"] if ocx >= 2 else []))
            for tt in range(8):
                tpar = tt % 2
                tgl = half * 8 + tt
                w = [f"p6:fe{half}_{ocx}"]
                if tt >= 2:
                    w.append(f"p6:ob{half}_{ocx}_{tt-2}")
                elif ocx > 0:
                    w.append(f"p6:ob{half}_{ocx-1}_{tt+6}")
                elif half == 1:
                    w.append(f"p6:ob0_15_{tt+6}")
                ctx.t(lambda t, tt=tt, tpar=tpar: t.transpose(
                    ptr6[tpar][:], ffn6[:, tt * 128:(tt + 1) * 128], idf[:]),
                    ev=f"p6:ftr{half}_{ocx}_{tt}", waits=w)
                w = [f"p6:ftr{half}_{ocx}_{tt}", f"p6:apx{half}_{ocx}",
                     f"p5:beta{tgl}"]
                if tt == 0:
                    w.append(f"p6:od{half}_{ocx-2}" if ocx > 1 else
                             (f"p6:od0_{14 + ocx}" if half == 1 else None))
                ctx.v(lambda v, tt=tt, tpar=tpar, tgl=tgl, obp=obp: v.scalar_tensor_tensor(
                    ob6[obp][:, tt * 128:(tt + 1) * 128], ptr6[tpar][:],
                    betab[:, tgl:tgl + 1], apx6[obp][:, tt * 128:(tt + 1) * 128],
                    OP.mult, OP.add),
                    ev=f"p6:ob{half}_{ocx}_{tt}", waits=[x for x in w if x])
            ctx.d(lambda s, ocx=ocx, half=half, obp=obp: s.dma_start(
                outr[:, half * 8:(half + 1) * 8, ocx * 128:(ocx + 1) * 128],
                ob6[obp][:].rearrange("p (i c) -> p i c", c=128)),
                ev=f"p6:od{half}_{ocx}",
                waits=[f"p6:ob{half}_{ocx}_{tt}" for tt in range(8)])
        freebag(p6ps)

    if DEBUG:
        ctx.v(lambda v: v.tensor_copy(ga5[:, 0:4], svals[:, 0:4]), waits=["a:sv"])
        ctx.v(lambda v: v.tensor_copy(ga5[:, 8:24], mxc1b[:]))
        ctx.v(lambda v: v.tensor_copy(ga5[:, 24:40], alpsq[:]))
        ctx.v(lambda v: v.tensor_copy(ga5[:, 40:56], betab[:]), ev="dbg:v")
        ctx.d(lambda s: s.dma_start(dbg_d[:, 0:56], ga5[:, 0:56]), waits=["dbg:v"])

    ctx.bar("fin")
    freebag(pbag)
    freebag(bag)
    freebag(perm)

    for i, val in enumerate(CSTVALS):
        try:
            nc.const_aps.register(F32, float(val), cst[:, i:i + 1])
        except Exception:
            pass
    sems = {s: nc.alloc_semaphore(f"sem_{s}") for s in ALLSEMS}

    def replay(engine, lst, inc):
        lastw = {}
        for fn, waits, sem in lst:
            for (we, wv) in waits:
                if lastw.get(we, -1) < wv:
                    engine.wait_ge(sems[we], wv)
                    lastw[we] = wv
            fn(engine).then_inc(sems[sem], inc)

    with nc.Block() as blk:
        @blk.tensor
        def _(t):
            replay(t, ctx.q["t"], 1)

        @blk.vector
        def _(v):
            replay(v, ctx.q["v"], 1)

        @blk.scalar
        def _(a):
            replay(a, ctx.q["a"], 1)

        @blk.gpsimd
        def _(g):
            replay(g, ctx.q["g"], 1)

        @blk.sync
        def _(s):
            replay(s, ctx.q["d"], 16)
            for ds in DSEMS:
                s.wait_ge(sems[ds], ctx.cnt[ds])

    return nc


_NC_CACHE = [None]


def kernel(L0_attn_x, L0_attn_Wq, L0_attn_Wkv, L0_attn_Wo, L0_attn_bo,
           L0_ffn_w_up, L0_ffn_w_down):
    X = np.ascontiguousarray(np.asarray(L0_attn_x, np.float32))
    Wq = np.ascontiguousarray(np.asarray(L0_attn_Wq, np.float32))
    Wkv = np.ascontiguousarray(np.asarray(L0_attn_Wkv, np.float32))
    Wo = np.ascontiguousarray(np.asarray(L0_attn_Wo, np.float32))
    bo = np.ascontiguousarray(np.asarray(L0_attn_bo, np.float32))
    WupT = np.ascontiguousarray(np.asarray(L0_ffn_w_up, np.float32).T)
    WdnT = np.ascontiguousarray(np.asarray(L0_ffn_w_down, np.float32).T)

    idf = np.eye(128, dtype=np.float32)
    idb = np.eye(128, dtype=np.float32).astype(BFNP)
    onesb = np.ones((128, 128), dtype=BFNP)
    onesf = np.ones((128, 128), np.float32)
    h128 = hmat(128).astype(BFNP)
    cstm = np.tile(np.array(CSTVALS, np.float32)[None, :], (128, 1))
    bo_t = np.ascontiguousarray(bo.reshape(16, 128).T)

    if _NC_CACHE[0] is None:
        _NC_CACHE[0] = build_program()
    nc = _NC_CACHE[0]

    in_maps = []
    for core in range(NCORE):
        b, qh = core // 2, core % 2
        rows = own_rows(qh)
        in_maps.append({
            "x": X[b],
            "xo": np.ascontiguousarray(X[b, rows]),
            "wq": Wq, "wkv": Wkv, "wo": Wo, "bo": bo_t,
            "wupT": WupT, "wdnT": WdnT,
            "idf_d": idf, "idb_d": idb, "onesb_d": onesb, "onesf_d": onesf,
            "h128_d": h128,
            "askew_d": build_askew(qh),
            "umask_d": np.full((128, 1), 1.0 if qh == 0 else 0.0, np.float32),
            "cst_d": cstm,
        })
    res = run_bass_kernel_spmd(nc, in_maps, core_ids=list(range(NCORE)))
    out = np.zeros((4, S, D), np.float32)
    for core in range(NCORE):
        b, qh = core // 2, core % 2
        out[b, own_rows(qh), :] = res.results[core]["out"]
    return out


if __name__ == "__main__":
    import time
    t0 = time.time()
    build_program()
    print(f"build ok in {time.time() - t0:.1f}s")


# version 15 (final)
# speedup vs baseline: 1.0341x; 1.0085x over previous
import math
import numpy as np
import ml_dtypes
import concourse.bass as bass
import concourse.mybir as mybir
from concourse.bass_utils import run_bass_kernel_spmd

F32 = mybir.dt.float32
BF16 = mybir.dt.bfloat16
I16 = mybir.dt.int16
I32 = mybir.dt.int32
OP = mybir.AluOpType
AF = mybir.ActivationFunctionType
AX = mybir.AxisListType.X

BFNP = ml_dtypes.bfloat16

D = 2048
S = 4096
TOWN = 2048
NCORE = 8
SKEW_W = 7680
SKEW_BASE = 3968

SLOPES = [float(np.float32((2.0 ** (-0.5)) ** (h + 1))) for h in range(16)]
MARGIN = 75.0
KEEPC = [257 + int(math.ceil(MARGIN / s)) for s in SLOPES]


def kept_js(hmax, qt):
    i0, i1 = qt * 1024, qt * 1024 + 1024
    C = KEEPC[hmax]
    out = []
    for j in range(32):
        j0, j1 = j * 128, j * 128 + 128
        dmin = max(0, j0 - (i1 - 1), i0 - (j1 - 1))
        if dmin <= C:
            out.append(j)
    return out

C_EXPB, C_ZERO, C_ONE, C_MONE, C_LNSC, C_EPS, C_QSC, C_VMSC, C_MHALF = range(9)
CSTVALS = [-15.0, 0.0, 1.0, -1.0, 1.0 / 2048.0, 1e-5, 1.0 / math.sqrt(128.0),
           1.0 / 4096.0, -0.5]

ENGS = ("t", "v", "a", "g", "d")
NDSEM = 16
DSEMS = tuple(f"d{i}" for i in range(NDSEM))
ALLSEMS = ("t", "v", "a", "g") + DSEMS


class Ctx:

    def __init__(self):
        self.q = {e: [] for e in ENGS}
        self.cnt = {s: 0 for s in ALLSEMS}
        self.events = {}
        self.dtotal = 0

    def op(self, eng, fn, ev=None, waits=(), flush=False):
        w = []
        for x in waits:
            if x is None:
                continue
            we, wv = self.events[x] if isinstance(x, str) else x
            if we != eng:
                w.append((we, wv))
        pf = getattr(self, "need_flush", None)
        if pf is None:
            self.need_flush = pf = {}
        if eng in pf and pf[eng] is not None:
            w.append((eng, pf[eng]))
            pf[eng] = None
        if eng == "d":
            sem = DSEMS[self.dtotal % NDSEM]
            self.dtotal += 1
            self.cnt[sem] += 16
            self.q[eng].append((fn, w, sem))
            if ev is not None:
                self.events[ev] = (sem, self.cnt[sem])
            return (sem, self.cnt[sem])
        self.q[eng].append((fn, w, eng))
        self.cnt[eng] += 1
        if flush:
            pf[eng] = self.cnt[eng]
        if ev is not None:
            self.events[ev] = (eng, self.cnt[eng])
        return (eng, self.cnt[eng])

    def t(self, fn, **kw):
        return self.op("t", fn, **kw)

    def v(self, fn, **kw):
        return self.op("v", fn, **kw)

    def a(self, fn, **kw):
        return self.op("a", fn, **kw)

    def g(self, fn, **kw):
        return self.op("g", fn, **kw)

    def d(self, fn, **kw):
        return self.op("d", fn, **kw)

    def bar(self, name):
        evs = []
        for e in ("t", "v", "a", "g"):
            self.events[f"{name}:{e}"] = (e, self.cnt[e])
            evs.append(f"{name}:{e}")
        for i, s in enumerate(DSEMS):
            self.events[f"{name}:d{i}"] = (s, self.cnt[s])
            evs.append(f"{name}:d{i}")
        self.events[f"{name}:d"] = (DSEMS[(self.dtotal - 1) % NDSEM],
                                    self.cnt[DSEMS[(self.dtotal - 1) % NDSEM]])
        return evs


def hmat(n):
    H = np.ones((1, 1), np.float32)
    while H.shape[0] < n:
        H = np.block([[H, H], [H, -H]])
    return H


def build_askew(qh):
    p = np.arange(128)[:, None]
    x = np.arange(SKEW_W)[None, :]
    d = (x - SKEW_BASE + qh * 512) - p
    g = np.where(np.abs(d) <= 256, -32768.0, 257.0 - np.abs(d))
    return g.astype(np.int16)


def own_rows(qh):
    return np.concatenate([np.arange(qh * 512 + k * 1024, qh * 512 + k * 1024 + 512)
                           for k in range(4)])


DEBUG = False
DBG_WA = False
DBG_KV = False
DBG_YF = False
DBG_XNF = False


def build_program():
    nc = bass.Bass("TRN2", target_bir_lowering=False)
    ctx = Ctx()

    x_full = nc.dram_tensor("x", [S, D], F32, kind="ExternalInput")
    x_own = nc.dram_tensor("xo", [TOWN, D], F32, kind="ExternalInput")
    wq_d = nc.dram_tensor("wq", [D, D], F32, kind="ExternalInput")
    wkv_d = nc.dram_tensor("wkv", [D, 1024], F32, kind="ExternalInput")
    wo_d = nc.dram_tensor("wo", [D, D], F32, kind="ExternalInput")
    bo_d = nc.dram_tensor("bo", [128, 16], F32, kind="ExternalInput")
    wup_d = nc.dram_tensor("wupT", [D, 4096], F32, kind="ExternalInput")
    wdn_d = nc.dram_tensor("wdnT", [4096, D], F32, kind="ExternalInput")
    idf_d = nc.dram_tensor("idf_d", [128, 128], F32, kind="ExternalInput")
    idb_d = nc.dram_tensor("idb_d", [128, 128], BF16, kind="ExternalInput")
    onesb_d = nc.dram_tensor("onesb_d", [128, 128], BF16, kind="ExternalInput")
    onesf_d = nc.dram_tensor("onesf_d", [128, 128], F32, kind="ExternalInput")
    h128_d = nc.dram_tensor("h128_d", [128, 128], BF16, kind="ExternalInput")
    askew_d = nc.dram_tensor("askew_d", [128, SKEW_W], I16, kind="ExternalInput")
    umask_d = nc.dram_tensor("umask_d", [128, 1], F32, kind="ExternalInput")
    cst_d = nc.dram_tensor("cst_d", [128, len(CSTVALS)], F32, kind="ExternalInput")
    out_d = nc.dram_tensor("out", [TOWN, D], F32, kind="ExternalOutput")

    kscr = "ExternalOutput"
    q_scr = nc.dram_tensor("q_scr", [D, TOWN], BF16, kind=kscr)
    ao_scr = nc.dram_tensor("ao_scr", [D, TOWN], BF16, kind=kscr)
    ap_scr = nc.dram_tensor("ap_scr", [TOWN, D], F32, kind=kscr)
    i1_scr = nc.dram_tensor("i1_scr", [D, TOWN], BF16, kind=kscr)
    g_scr = nc.dram_tensor("g_scr", [4096, TOWN], F32, kind=kscr)
    td_scr = nc.dram_tensor("td_scr", [4096, D], BF16, kind=kscr)
    if DEBUG:
        dbg_d = nc.dram_tensor("dbg", [128, 72], F32, kind="ExternalOutput")
    if DEBUG or DBG_KV:
        kv_dbg = nc.dram_tensor("kv_dbg", [128, 2 * 16384], BF16, kind="ExternalOutput")
    if DEBUG or DBG_YF:
        yf_dbg = nc.dram_tensor("yf_dbg", [128, 8192], F32, kind="ExternalOutput")
    if DEBUG or DBG_XNF:
        xnf_dbg = nc.dram_tensor("xnf_dbg", [128, 8192], BF16, kind="ExternalOutput")
    if DEBUG or DBG_WA:
        wa_dbg = nc.dram_tensor("wa_dbg", [128, 4], F32, kind="ExternalOutput")

    qsr = q_scr.rearrange("(o p) q -> p o q", p=128)
    aosr = ao_scr.rearrange("(o p) q -> p o q", p=128)
    apsr = ap_scr.rearrange("(i p) d -> p i d", p=128)
    i1sr = i1_scr.rearrange("(o p) q -> p o q", p=128)
    gsr = g_scr.rearrange("(o p) q -> o p q", p=128)
    gslr = g_scr.rearrange("(c p) q -> p c q", p=128)
    tdsr = td_scr.rearrange("(i p) d -> p i d", p=128)
    outr = out_d.rearrange("(i p) d -> p i d", p=128)
    xor_ = x_own.rearrange("(i p) d -> p i d", p=128)
    wup_r = wup_d.rearrange("(i p) d -> p i d", p=128)
    wdn_r = wdn_d.rearrange("(i p) d -> p i d", p=128)
    wkv_r = wkv_d.rearrange("(i p) d -> i p d", p=128)

    def sb(name, shape, dtype, bag):
        g = nc.sbuf_tensor(name, shape, dtype)
        t = g.__enter__()
        bag.append(g)
        return t

    def ps(name, shape, dtype, bag):
        g = nc.psum_tensor(name, shape, dtype)
        t = g.__enter__()
        bag.append(g)
        return t

    def freebag(bag):
        for g in reversed(bag):
            g.__exit__(None, None, None)
        bag.clear()

    perm = []
    idf = sb("idf", [128, 128], F32, perm)
    idb = sb("idb", [128, 128], BF16, perm)
    onesb = sb("onesb", [128, 128], BF16, perm)
    onesf = sb("onesf", [128, 128], F32, perm)
    h128 = sb("h128", [128, 128], BF16, perm)
    cst = sb("cst", [128, len(CSTVALS)], F32, perm)
    umask = sb("umask", [128, 1], F32, perm)
    bo_s = sb("bo_s", [128, 16], F32, perm)
    svals = sb("svals", [128, 8], F32, perm)
    mxc1b = sb("mxc1b", [128, 16], F32, perm)
    wacc = sb("wacc", [128, 4], F32, perm)
    alpsq = sb("alpsq", [128, 16], F32, perm)
    betab = sb("betab", [128, 16], F32, perm)

    def cc(i):
        return cst[:, i:i + 1]

    for dst, src in ((idf, idf_d), (idb, idb_d), (onesb, onesb_d), (onesf, onesf_d),
                     (h128, h128_d), (cst, cst_d), (umask, umask_d), (bo_s, bo_d)):
        ctx.d(lambda s, dst=dst, src=src: s.dma_start(dst[:], src[:, :]))
    ctx.bar("cb")

    def emit_ln(tag, prev_tag, xs, t0, xt, xcb, sqb, rb_full, rpar, xld_waits=(),
                xc_waits=(), want_cw=False):
        for stage in range(3):
            emit_ln_stage(stage, tag, prev_tag, xs, t0, xt, xcb, sqb, rb_full, rpar,
                          xld_waits, xc_waits, want_cw)

    def emit_ln_stage(stage, tag, prev_tag, xs, t0, xt, xcb, sqb, rb_full, rpar,
                      xld_waits=(), xc_waits=(), want_cw=False):
        rb = rb_full[:, rpar * 8:(rpar + 1) * 8]
        if stage == 0:
            ctx.d(lambda s: s.dma_start(xt[:], xs[t0:t0 + 128, :]), ev=f"{tag}:xld",
                  waits=xld_waits)
            ctx.v(lambda v: v.tensor_reduce(rb[:, 0:1], xt[:], AX, OP.add),
                  waits=[f"{tag}:xld"], flush=True)
            ctx.v(lambda v: v.tensor_single_scalar(rb[:, 1:2], rb[:, 0:1],
                                                   -1.0 / 2048.0, OP.mult), flush=True)
            return
        if stage == 1:
            ctx.v(lambda v: v.tensor_scalar(xcb[:], xt[:], rb[:, 1:2], None, OP.add),
                  ev=f"{tag}:xc", waits=xc_waits)
            ctx.v(lambda v: v.tensor_mul(sqb[:], xcb[:], xcb[:]))
            ctx.v(lambda v: v.tensor_reduce(rb[:, 2:3], sqb[:], AX, OP.add),
                  ev=f"{tag}:ssq",
                  waits=[f"{prev_tag}:rstd"] if prev_tag else ())
            ctx.a(lambda a: a.activation(rb[:, 3:4], rb[:, 2:3], AF.Ln,
                                         bias=cc(C_EPS), scale=cc(C_LNSC)),
                  waits=[f"{tag}:ssq"], flush=True)
            ctx.a(lambda a: a.activation(rb[:, 4:5], rb[:, 3:4], AF.Exp,
                                         bias=cc(C_ZERO), scale=cc(C_MHALF)),
                  ev=f"{tag}:rstd")
            return
        ctx.v(lambda v: v.tensor_scalar(xcb[:], xcb[:], rb[:, 4:5], None, OP.mult),
              ev=f"{tag}:xn", waits=[f"{tag}:rstd"])
        if want_cw:
            ctx.v(lambda v: v.tensor_reduce(rb[:, 5:6], xcb[:], AX, OP.add,
                                            apply_absolute_value=True),
                  ev=f"{tag}:asum",
                  waits=[f"{prev_tag}:cw"] if prev_tag else ())
            ctx.a(lambda a: a.activation(rb[:, 6:7], rb[:, 5:6], AF.Exp,
                                         bias=cc(C_ZERO), scale=cc(C_MONE)),
                  waits=[f"{tag}:asum"], flush=True)
            ctx.a(lambda a: a.activation(rb[:, 6:7], rb[:, 6:7], AF.Ln,
                                         bias=cc(C_ONE), scale=cc(C_ONE)), flush=True)
            ctx.a(lambda a: a.activation(rb[:, 7:8], rb[:, 6:7], AF.Exp,
                                         bias=cc(C_ZERO), scale=cc(C_MONE)),
                  ev=f"{tag}:cw")
            ctx.v(lambda v: v.tensor_scalar(xcb[:], xcb[:], rb[:, 7:8], None, OP.mult),
                  ev=f"{tag}:xn", waits=[f"{tag}:cw"])

    def emit_tr(tag, src, dst, dst_col, ptr, nch, out_stride, evac, dtw=(), ident=None):
        last = [None, None]
        for c in range(nch):
            par = c % 2
            w = list(dtw) if c < 2 else []
            if last[par] is not None:
                w.append(last[par])
            idm = ident if ident is not None else idf
            ctx.t(lambda t, c=c, par=par, idm=idm: t.transpose(
                ptr[par][:], src[:, c * 128:(c + 1) * 128], idm[:]),
                ev=f"{tag}:tr{c}", waits=w)
            o0 = c * out_stride + dst_col
            if evac == "a":
                ctx.a(lambda a, c=c, par=par, o0=o0: a.activation(
                    dst[:, o0:o0 + 128], ptr[par][:], AF.Copy,
                    bias=0.0, scale=1.0),
                    ev=f"{tag}:ev{c}", waits=[f"{tag}:tr{c}"])
            else:
                ctx.v(lambda v, c=c, par=par, o0=o0: v.tensor_copy(
                    dst[:, o0:o0 + 128], ptr[par][:]),
                    ev=f"{tag}:ev{c}", waits=[f"{tag}:tr{c}"])
            last[par] = f"{tag}:ev{c}"
        ctx.events[f"{tag}:T"] = ctx.events[f"{tag}:tr{nch-1}"]
        ctx.events[f"{tag}:E"] = ctx.events[f"{tag}:ev{nch-1}"]

    bag, pbag = [], []
    wqb = sb("wqb", [128, 16 * 2048], BF16, bag)
    wstr = [sb(f"wstr{i}", [128, 2048], F32, bag) for i in range(2)]
    wsb = [sb(f"wsb{i}", [128, 2048], F32, bag) for i in range(2)]
    xt_b = [sb(f"xt{i}", [128, 2048], F32, bag) for i in range(2)]
    xc2_b = [sb(f"xc{i}", [128, 2048], F32, bag) for i in range(2)]
    sq_b = sb("sq", [128, 2048], F32, bag)
    r_b = sb("r", [128, 16], F32, bag)
    xncwfm = sb("xncwfm", [128, 16 * 512], BF16, bag)
    qfm = sb("qfm", [128, 16 * 512], BF16, bag)
    ptrA = [ps(f"ptrA{i}", [128, 128], F32, pbag) for i in range(2)]
    pq = [ps(f"pq{i}", [128, 512], F32, pbag) for i in range(2)]
    psml = ps("psml", [128, 4], F32, pbag)

    for ic in range(16):
        par = ic % 2
        w = [f"cb:d{i}" for i in range(16)] if ic < 2 else [f"q:wc{ic-2}"]
        ctx.d(lambda s, ic=ic, par=par: s.dma_start(
            wstr[par][:], wq_d[ic * 128:(ic + 1) * 128, :]), ev=f"q:wl{ic}", waits=w)
        ctx.v(lambda v, ic=ic, par=par: v.tensor_copy(
            wqb[:, ic * 2048:(ic + 1) * 2048], wstr[par][:]),
            ev=f"q:wc{ic}", waits=[f"q:wl{ic}"])

    wchunks = []
    for k in range(32):
        wchunks.append((wup_d[(k % 16) * 128:(k % 16 + 1) * 128,
                              (k // 16) * 2048:(k // 16 + 1) * 2048], 0))
    for k in range(32):
        wchunks.append((wdn_d[(k % 32) * 128:(k % 32 + 1) * 128,
                              (k // 32) * 2048:(k // 32 + 1) * 2048], 1))
    ctx.v(lambda v: v.memset(wacc[:, 0:4], 0.0), ev="q:wz", waits=["cb:d0"])

    def emit_wsum(k):
        srcw, col = wchunks[k]
        par = k % 2
        ctx.d(lambda s, srcw=srcw, par=par: s.dma_start(wsb[par][:], srcw),
              ev=f"q:wl{k}", waits=["q:wz"] if k < 2 else [f"q:wr{k-2}"])
        ctx.v(lambda v, par=par: v.tensor_reduce(
            wacc[:, 2:3], wsb[par][:], AX, OP.add, apply_absolute_value=True),
            waits=[f"q:wl{k}"], flush=True)
        ctx.v(lambda v, col=col: v.tensor_add(
            wacc[:, col:col + 1], wacc[:, col:col + 1], wacc[:, 2:3]),
            ev=f"q:wr{k}", flush=True)

    for qt in range(4):
        for k in range(qt * 16, qt * 16 + 16):
            emit_wsum(k)
        for tp_ in range(2):
            tgs = [qt * 4 + tp_ * 2, qt * 4 + tp_ * 2 + 1]
            for st_ in range(3):
                for i_, tg in enumerate(tgs):
                    tag = f"q:t{tg}"
                    prevtag = f"q:t{tg-2}" if tg >= 2 else None
                    emit_ln_stage(st_, tag, prevtag, x_own, tg * 128, xt_b[i_],
                                  xc2_b[i_], sq_b, r_b, tg % 2,
                                  xld_waits=[f"q:t{tg-2}:xc"] if tg >= 2 else
                                  [f"cb:d{i}" for i in range(16)],
                                  xc_waits=[f"q:r{tg-2}:T"] if tg >= 2 else (),
                                  want_cw=True)
            for i_, tg in enumerate(tgs):
                tt = tg - qt * 4
                emit_tr(f"q:r{tg}", xc2_b[i_], xncwfm, tt * 128, ptrA, 16, 512, "a",
                        dtw=[f"q:t{tg}:xn"] +
                            ([f"q:r{tg-1}:E"] if tg >= 1 else []) +
                            ([f"q:mm{qt-1}_15_last"] if (tt == 0 and qt > 0) else []))
        trall = [f"q:r{qt * 4 + tt}:E" for tt in range(4)]
        for oc in range(16):
            par = oc % 2
            for ic in range(16):
                w = []
                if ic == 0:
                    w = list(trall)
                    if oc >= 2:
                        w.append(f"q:qe{qt}_{oc-2}")
                    elif qt > 0:
                        w.append(f"q:qe{qt-1}_{oc+14}")
                    if oc < 2:
                        w.append(f"q:qs{qt-1}" if qt > 0 else None)
                ctx.t(lambda t, oc=oc, ic=ic, par=par: t.matmul(
                    pq[par][:], wqb[:, ic * 2048 + oc * 128: ic * 2048 + oc * 128 + 128],
                    xncwfm[:, ic * 512:(ic + 1) * 512],
                    start=(ic == 0), stop=(ic == 15)),
                    ev=(f"q:mm{qt}_{oc}" if ic == 15 else None), waits=w)
            ctx.a(lambda a, oc=oc, par=par: a.activation(
                qfm[:, oc * 512:(oc + 1) * 512], pq[par][:], AF.Copy,
                bias=0.0, scale=CSTVALS[C_QSC]),
                ev=f"q:qe{qt}_{oc}", waits=[f"q:mm{qt}_{oc}"] +
                ([f"q:qs{qt-1}"] if (qt > 0 and oc < 2) else []))
        ctx.events[f"q:mm{qt}_15_last"] = ctx.events[f"q:mm{qt}_15"]
        ctx.d(lambda s, qt=qt: s.dma_start(
            qsr[:, :, qt * 512:(qt + 1) * 512],
            qfm[:].rearrange("p (o c) -> p o c", c=512)),
            ev=f"q:qs{qt}", waits=[f"q:qe{qt}_{o}" for o in range(16)])

    ctx.t(lambda t: t.matmul(psml[:, 0:2], onesf[:], wacc[:, 0:2],
                             start=True, stop=True), ev="q:ws", waits=["q:wr63"])
    ctx.v(lambda v: v.tensor_scalar(svals[:, 0:1], psml[:, 0:1],
                                    1.0 / 8388608.0, 1e-5, OP.mult, OP.max),
          waits=["q:ws"], flush=True)
    ctx.v(lambda v: v.tensor_scalar(svals[:, 2:3], psml[:, 1:2],
                                    1.0 / 8388608.0, 1e-5, OP.mult, OP.max), flush=True)
    ctx.v(lambda v: v.reciprocal(svals[:, 1:2], svals[:, 0:1]), flush=True)
    ctx.v(lambda v: v.reciprocal(svals[:, 3:4], svals[:, 2:3]), ev="a:sv", flush=True)
    if DEBUG or DBG_WA:
        ctx.d(lambda s: s.dma_start(wa_dbg[:, :], wacc[:]), waits=["a:sv"])

    ctx.bar("q")
    freebag(pbag)
    freebag(bag)

    bag, pbag = [], []
    xt_b = [sb(f"fxt{i}", [128, 2048], F32, bag) for i in range(2)]
    xc2_b = [sb(f"fxc{i}", [128, 2048], F32, bag) for i in range(2)]
    sq_b = sb("fsq", [128, 2048], F32, bag)
    r_b = sb("fr", [128, 16], F32, bag)
    yfm = sb("yfm", [128, 16 * 512], F32, bag)
    hia = sb("hia", [128, 2048], BF16, bag)
    loa = sb("loa", [128, 2048], BF16, bag)
    ubuf = sb("ubuf", [128, 2048], F32, bag)
    i32b = sb("i32b", [128, 2048], I32, bag)
    gab = sb("gab", [128, 2048], F32, bag)
    gbb = sb("gbb", [128, 2048], F32, bag)
    intstm = sb("intstm", [128, 2048], BF16, bag)
    intsfm = sb("intsfm", [128, 16 * 512], BF16, bag)
    rq = sb("rq", [128, 4], F32, bag)
    ptrA = [ps(f"fptr{i}", [128, 128], F32, pbag) for i in range(2)]
    ptm = ps("ptm", [128, 2048], F32, pbag)
    pbtr = [ps(f"fbtr{i}", [128, 128], BF16, pbag) for i in range(2)]

    for qt in range(4):
        for tp_ in range(2):
            tgs = [qt * 4 + tp_ * 2, qt * 4 + tp_ * 2 + 1]
            for st_ in range(3):
                for i_, tg in enumerate(tgs):
                    tag = f"f:t{tg}"
                    prevtag = f"f:t{tg-2}" if tg >= 2 else None
                    emit_ln_stage(st_, tag, prevtag, x_own, tg * 128, xt_b[i_],
                                  xc2_b[i_], sq_b, r_b, tg % 2,
                                  xld_waits=[f"f:t{tg-2}:xc"] if tg >= 2 else
                                  [f"q:d{i}" for i in range(16)] + ["q:t", "q:v", "q:a"],
                                  xc_waits=[f"f:r{tg-2}:T"] if tg >= 2 else ())
            for i_, tg in enumerate(tgs):
                tt = tg - qt * 4
                emit_tr(f"f:r{tg}", xc2_b[i_], yfm, tt * 128, ptrA, 16, 512, "v",
                        dtw=[f"f:t{tg}:xn"] +
                            ([f"f:r{tg-1}:E"] if tg >= 1 else []) +
                            ([f"f:losplit{qt-1}"] if (tt == 0 and qt > 0) else []))
        ydone = [f"f:r{qt * 4 + tt}:E" for tt in range(4)]
        bit = 1
        k = 0
        while bit < 16:
            for c in range(16):
                if c & bit:
                    continue
                a0, b0 = c * 512, (c | bit) * 512
                ctx.v(lambda v, a0=a0, b0=b0: v.tensor_sub(
                    yfm[:, b0:b0 + 512], yfm[:, a0:a0 + 512], yfm[:, b0:b0 + 512]),
                    waits=ydone if k == 0 else ())
                ctx.v(lambda v, a0=a0, b0=b0: v.scalar_tensor_tensor(
                    yfm[:, a0:a0 + 512], yfm[:, a0:a0 + 512], 2.0,
                    yfm[:, b0:b0 + 512], OP.mult, OP.subtract),
                    ev=f"f:bf{qt}_{k}")
                k += 1
            bit *= 2
        bfd = f"f:bf{qt}_{k-1}"
        if (DEBUG or DBG_YF) and qt == 0:
            ctx.d(lambda s: s.dma_start(yf_dbg[:, :], yfm[:]), waits=[bfd])
        for tt in range(4):
            tg = qt * 4 + tt
            yv = yfm[:].rearrange("p (c q) -> p c q", q=512)[:, :, tt * 128:tt * 128 + 128]
            w = [bfd]
            if tg > 0:
                w.append(f"f:fm{tg-1}_15")
            ctx.v(lambda v, yv=yv: v.tensor_copy(hia[:], yv), ev=f"f:hi{tg}", waits=w)
            ctx.v(lambda v, yv=yv: v.tensor_tensor(
                loa[:].rearrange("p (c q) -> p c q", q=128), yv,
                hia[:].rearrange("p (c q) -> p c q", q=128), OP.subtract),
                ev=f"f:lo{tg}")
            for c in range(16):
                ctx.t(lambda t, c=c: t.matmul(
                    ptm[:, c * 128:(c + 1) * 128], hia[:, c * 128:(c + 1) * 128],
                    h128[:], start=(c % 4 == 0), stop=False),
                    waits=[f"f:lo{tg}"] +
                          ([f"f:u{tg-1}"] if (c == 0 and tg > 0) else []))
                ctx.t(lambda t, c=c: t.matmul(
                    ptm[:, c * 128:(c + 1) * 128], loa[:, c * 128:(c + 1) * 128],
                    h128[:], start=False, stop=(c % 4 == 3)),
                    ev=f"f:fm{tg}_{c}")
            mmd = [f"f:fm{tg}_{c}" for c in range(3, 16, 4)]
            ctx.v(lambda v: v.tensor_reduce(rq[:, 0:1], ptm[:], AX, OP.max,
                                            apply_absolute_value=True), waits=mmd,
                  flush=True)
            ctx.v(lambda v, tg=tg: v.tensor_single_scalar(
                mxc1b[:, tg:tg + 1], rq[:, 0:1],
                float(np.float32(1e-5) * np.float32(math.sqrt(2048.0))), OP.max),
                flush=True)
            ctx.v(lambda v, tg=tg: v.reciprocal(rq[:, 2:3], mxc1b[:, tg:tg + 1]),
                  flush=True)
            ctx.v(lambda v: v.tensor_scalar(ubuf[:], ptm[:], rq[:, 2:3], 127.0,
                                            OP.mult, OP.mult),
                  ev=f"f:u{tg}")
            ctx.v(lambda v: v.tensor_copy(i32b[:], ubuf[:]), ev=f"f:i32{tg}")
            ctx.v(lambda v: v.tensor_sub(gbb[:], ubuf[:], i32b[:]), ev=f"f:ib{tg}")
            ctx.v(lambda v: v.tensor_single_scalar(gab[:], gbb[:], 0.5, OP.is_gt))
            ctx.v(lambda v: v.tensor_single_scalar(gbb[:], gbb[:], -0.5, OP.is_lt))
            ctx.v(lambda v: v.tensor_add(gab[:], gab[:], i32b[:]), ev=f"f:ia{tg}")
            ctx.v(lambda v: v.tensor_sub(intstm[:], gab[:], gbb[:]),
                  ev=f"f:ints{tg}",
                  waits=[f"f:i1r{tg-1}:T"] if tg > 0 else ())
            emit_tr(f"f:i1r{tg}", intstm, intsfm, tt * 128, pbtr, 16, 512, "v",
                    dtw=[f"f:ints{tg}"] +
                        ([f"f:is{qt-1}"] if (tt == 0 and qt > 0) else []), ident=idb)
        ctx.events[f"f:losplit{qt}"] = ctx.events[f"f:lo{qt * 4 + 3}"]
        ctx.d(lambda s, qt=qt: s.dma_start(
            i1sr[:, :, qt * 512:(qt + 1) * 512],
            intsfm[:].rearrange("p (o c) -> p o c", c=512)),
            ev=f"f:is{qt}", waits=[f"f:i1r{qt * 4 + tt}:E" for tt in range(4)])

    ctx.bar("f")
    freebag(pbag)
    freebag(bag)

    kvbag = []
    k_res = sb("k_res", [128, 4 * 4096], BF16, kvbag)
    v_res = sb("v_res", [128, 4 * 32 * 128], BF16, kvbag)
    askew = sb("askew", [128, SKEW_W], I16, kvbag)

    bag, pbag = [], []
    wkvb = sb("wkvb", [128, 16 * 1024], BF16, bag)
    kstr = [sb(f"kstr{i}", [128, 1024], F32, bag) for i in range(2)]
    xt_b = [sb(f"axt{i}", [128, 2048], F32, bag) for i in range(2)]
    xc2_b = [sb(f"axc{i}", [128, 2048], F32, bag) for i in range(4)]
    sq_b = sb("asq", [128, 2048], F32, bag)
    r_b = sb("ar", [128, 32], F32, bag)
    xnfm = sb("axnfm", [128, 16 * 512], BF16, bag)
    vtmp = sb("vtmp", [128, 512], BF16, bag)

    ptrA = [ps(f"aptr{i}", [128, 128], F32, pbag) for i in range(2)]
    pkv = [ps(f"pkv{i}", [128, 512], F32, pbag) for i in range(2)]
    pvtr = [ps(f"avtr{i}", [128, 128], BF16, pbag) for i in range(2)]

    ctx.d(lambda s: s.dma_start(askew[:], askew_d[:, :]), waits=[f"f:d{i}" for i in range(16)])
    for ic in range(16):
        par = ic % 2
        w = [f"f:d{i}" for i in range(16)] if ic < 2 else [f"a:wkc{ic-2}"]
        ctx.d(lambda s, ic=ic, par=par: s.dma_start(kstr[par][:], wkv_r[ic, :, :]),
              ev=f"a:wkl{ic}", waits=w)
        ctx.v(lambda v, ic=ic, par=par: v.tensor_copy(
            wkvb[:, ic * 1024:(ic + 1) * 1024], kstr[par][:]),
            ev=f"a:wkc{ic}", waits=[f"a:wkl{ic}"])
    for grp in range(8):
        for tp_ in range(2):
            tgs = [grp * 4 + tp_ * 2, grp * 4 + tp_ * 2 + 1]
            for st_ in range(3):
                for i_, tg in enumerate(tgs):
                    tag = f"a:t{tg}"
                    prevtag = f"a:t{tg-2}" if tg >= 2 else None
                    emit_ln_stage(st_, tag, prevtag, x_full, tg * 128, xt_b[tg % 2],
                                  xc2_b[tg % 4], sq_b, r_b, tg % 4,
                                  xld_waits=[f"a:t{tg-2}:xc"] if tg >= 2 else
                                  ["f:v", "f:t", "f:a", "f:g"],
                                  xc_waits=[f"a:r{tg-4}:T"] if tg >= 4 else ())
            for i_, tg in enumerate(tgs):
                tt = tg - grp * 4
                emit_tr(f"a:r{tg}", xc2_b[tg % 4], xnfm, tt * 128, ptrA, 16, 512, "a",
                        dtw=[f"a:t{tg}:xn"] +
                            ([f"a:r{tg-1}:E"] if tg >= 1 else []) +
                            ([f"a:kv{grp-1}_7mm"] if (tt == 0 and grp > 0) else []))
        trall = [f"a:r{grp * 4 + tt}:E" for tt in range(4)]
        if (DEBUG or DBG_XNF) and grp == 0:
            ctx.d(lambda s: s.dma_start(xnf_dbg[:, :], xnfm[:]), waits=trall)
        for oc in range(8):
            par = oc % 2
            for ic in range(16):
                w = []
                if ic == 0:
                    w = list(trall) + ["a:wkc15"]
                    if oc >= 2:
                        w.append(f"a:kve{grp}_{oc-2}")
                    elif grp > 0:
                        w.append(f"a:kve{grp-1}_{oc+6}")
                ctx.t(lambda t, oc=oc, ic=ic, par=par: t.matmul(
                    pkv[par][:], wkvb[:, ic * 1024 + oc * 128: ic * 1024 + oc * 128 + 128],
                    xnfm[:, ic * 512:(ic + 1) * 512], start=(ic == 0), stop=(ic == 15)),
                    ev=(f"a:kvm{grp}_{oc}" if ic == 15 else None), waits=w)
            if oc < 4:
                ctx.a(lambda a, oc=oc, grp=grp, par=par: a.activation(
                    k_res[:, oc * 4096 + grp * 512: oc * 4096 + grp * 512 + 512],
                    pkv[par][:], AF.Copy, bias=0.0, scale=1.0),
                    ev=f"a:kve{grp}_{oc}", waits=[f"a:kvm{grp}_{oc}"])
            else:
                gkv = oc - 4
                w = [f"a:kvm{grp}_{oc}"]
                if oc > 4:
                    w.append(f"a:vtr{grp}_{oc-1}_3")
                elif grp > 0:
                    w.append(f"a:vtr{grp-1}_7_3")
                ctx.a(lambda a, par=par: a.activation(
                    vtmp[:], pkv[par][:], AF.Copy, bias=0.0, scale=1.0),
                    ev=f"a:vt{grp}_{oc}", waits=w)
                for jl in range(4):
                    tpar = jl % 2
                    w2 = [f"a:vt{grp}_{oc}"]
                    if jl >= 2:
                        w2.append(f"a:vev{grp}_{oc}_{jl-2}")
                    ctx.t(lambda t, jl=jl, tpar=tpar: t.transpose(
                        pvtr[tpar][:], vtmp[:, jl * 128:(jl + 1) * 128], idb[:]),
                        ev=f"a:vtr{grp}_{oc}_{jl}", waits=w2)
                    col = (gkv * 32 + grp * 4 + jl) * 128
                    ctx.v(lambda v, col=col, tpar=tpar: v.tensor_copy(
                        v_res[:, col:col + 128], pvtr[tpar][:]),
                        ev=f"a:vev{grp}_{oc}_{jl}", waits=[f"a:vtr{grp}_{oc}_{jl}"])
                ctx.events[f"a:kve{grp}_{oc}"] = ctx.events[f"a:vev{grp}_{oc}_3"]
            if oc == 7:
                ctx.events[f"a:kv{grp}_7mm"] = ctx.events[f"a:kvm{grp}_{oc}"]

    if DEBUG or DBG_KV:
        ctx.d(lambda s: s.dma_start(kv_dbg[:, 0:16384], k_res[:]),
              waits=[f"a:kve7_{o}" for o in range(8)])
        ctx.d(lambda s: s.dma_start(kv_dbg[:, 16384:32768], v_res[:]))

    ctx.bar("pa")
    freebag(pbag)
    freebag(bag)

    bag, pbag = [], []
    qt_b = [sb(f"qtl{i}", [128, 16 * 512], BF16, bag) for i in range(2)]
    ao_b = sb("ao", [128, 16 * 512], BF16, bag)
    t_b = [sb(f"tb{i}", [128, 512], F32, bag) for i in range(4)]
    p_b = [sb(f"pb{i}", [128, 512], BF16, bag) for i in range(4)]
    zg_b = [sb(f"zg{i}", [128, 512], F32, bag) for i in range(2)]
    rz_b = [sb(f"rz{i}", [128, 512], F32, bag) for i in range(2)]
    vme_b = sb("vme", [128, 4 * 64], BF16, bag)
    dt_b = sb("dt", [128, 64], F32, bag)

    vmbag = []
    pvm = ps("pvm", [128, 64], F32, vmbag)
    for gkv in range(4):
        for j in range(32):
            w = []
            if gkv == 0 and j == 0:
                w = ["pa:t", "pa:v", "pa:a"]
            elif j == 0:
                w = [f"p2:vme{gkv-1}"]
            ctx.t(lambda t, gkv=gkv, j=j: t.matmul(
                pvm[:], v_res[:, (gkv * 32 + j) * 128:(gkv * 32 + j) * 128 + 128],
                onesb[:, 0:64], start=(j == 0), stop=(j == 31)),
                ev=(f"p2:vmm{gkv}" if j == 31 else None), waits=w)
        ctx.a(lambda a, gkv=gkv: a.activation(
            vme_b[:, gkv * 64:(gkv + 1) * 64], pvm[:], AF.Copy,
            bias=0.0, scale=CSTVALS[C_VMSC]),
            ev=f"p2:vme{gkv}", waits=[f"p2:vmm{gkv}"])
    freebag(vmbag)

    ppv = [ps(f"ppv{i}", [128, 512], F32, pbag) for i in range(2)]
    pz = [ps(f"pz{i}", [128, 512], F32, pbag) for i in range(2)]
    psc = [ps(f"psc{i}", [128, 512], F32, pbag) for i in range(4)]

    def njof(qt, pr):
        return len(kept_js(2 * pr + 1, qt))

    def _lastevt(kind, qt, pr, hi_, k):
        if k >= 0:
            return f"p2:{kind}{qt}_{pr}_{2 * pr + hi_}_{k}"
        if pr > 0:
            n = njof(qt, pr - 1)
            return f"p2:{kind}{qt}_{pr-1}_{2 * (pr - 1) + hi_}_{n + k}"
        if qt > 0:
            n = njof(qt - 1, 7)
            return f"p2:{kind}{qt-1}_7_{14 + hi_}_{n + k}"
        return None

    def lastap(qt, pr, hi_, k):
        return _lastevt("ap", qt, pr, hi_, k)

    def lastex(qt, pr, hi_, k):
        return _lastevt("ex", qt, pr, hi_, k)

    def lastpv(qt, pr, hi_, k):
        return _lastevt("pz", qt, pr, hi_, k)

    ctx.d(lambda s: s.dma_start(
        qt_b[0][:].rearrange("p (o c) -> p o c", c=512), qsr[:, :, 0:512]),
        ev="p2:ql0", waits=["q:qs0", "pa:t", "pa:v", "pa:a"] +
        [f"pa:d{i}" for i in range(16)])

    for qt in range(4):
        qpar = qt % 2
        if qt + 1 < 4:
            ctx.d(lambda s, qt=qt: s.dma_start(
                qt_b[(qt + 1) % 2][:].rearrange("p (o c) -> p o c", c=512),
                qsr[:, :, (qt + 1) * 512:(qt + 2) * 512]),
                ev=f"p2:ql{qt+1}",
                waits=[f"q:qs{qt+1}"] +
                      ([f"p2:sc{qt-1}_7_15_{njof(qt-1, 7) - 1}"] if qt >= 1 else []))
        for pr in range(8):
            h0, h1 = 2 * pr, 2 * pr + 1
            gkv = h0 // 4
            jset = kept_js(h1, qt)
            nj = len(jset)
            for k in range(nj + 1):
                if k < nj:
                    j = jset[k]
                    for hi_, h in enumerate((h0, h1)):
                        sbk = 2 * hi_ + (k % 2)
                        w = [f"p2:ql{qt}"]
                        ap = lastap(qt, pr, hi_, k - 2)
                        if ap:
                            w.append(ap)
                        if qt == 0 and pr == 0 and k == 0:
                            w += ["pa:t", "pa:a", "pa:v", f"p2:vme3"]
                        ctx.t(lambda t, gkv=gkv, j=j, h=h, sbk=sbk, qpar=qpar: t.matmul(
                            psc[sbk][:],
                            k_res[:, gkv * 4096 + j * 128: gkv * 4096 + j * 128 + 128],
                            qt_b[qpar][:, h * 512:(h + 1) * 512],
                            start=True, stop=True),
                            ev=f"p2:sc{qt}_{pr}_{h}_{k}", waits=w)
                kk = k - 1
                if kk < 0 or kk > nj - 1:
                    continue
                jj = jset[kk]
                for hi_, h in enumerate((h0, h1)):
                    sbk = 2 * hi_ + (kk % 2)
                    xoff = qt * 1024 - jj * 128 + SKEW_BASE
                    w = [f"p2:sc{qt}_{pr}_{h}_{kk}"]
                    ex = lastex(qt, pr, hi_, kk - 2)
                    if ex:
                        w.append(ex)
                    if qt == 0 and pr == 0 and kk == 0:
                        w.append("pa:d")
                    ctx.v(lambda v, h=h, sbk=sbk, xoff=xoff: v.scalar_tensor_tensor(
                        t_b[sbk][:], askew[:, xoff:xoff + 512], SLOPES[h], psc[sbk][:],
                        OP.mult, OP.add),
                        ev=(None if jj == 0 else f"p2:ap{qt}_{pr}_{h}_{kk}"), waits=w)
                    if jj == 0:
                        ctx.v(lambda v, sbk=sbk: v.memset(t_b[sbk][0:64, :], -100000.0),
                              ev=f"p2:ap{qt}_{pr}_{h}_{kk}")
                    w = [f"p2:ap{qt}_{pr}_{h}_{kk}"]
                    pv = lastpv(qt, pr, hi_, kk - 2)
                    if pv:
                        w.append(pv)
                    ctx.a(lambda a, sbk=sbk: a.activation(
                        p_b[sbk][:], t_b[sbk][:], AF.Exp,
                        bias=cc(C_EXPB), scale=cc(C_ONE)),
                        ev=f"p2:ex{qt}_{pr}_{h}_{kk}", waits=w)
                    w = [f"p2:ex{qt}_{pr}_{h}_{kk}"]
                    if kk == 0:
                        if pr > 0:
                            w.append(f"p2:aow{qt}_{pr-1}_{2 * (pr - 1) + hi_}")
                        elif qt > 0:
                            w.append(f"p2:aow{qt-1}_7_{14 + hi_}")
                    ctx.t(lambda t, gkv=gkv, jj=jj, hi_=hi_, sbk=sbk, kk=kk, nj=nj: t.matmul(
                        ppv[hi_][:],
                        v_res[:, (gkv * 32 + jj) * 128:(gkv * 32 + jj) * 128 + 128],
                        p_b[sbk][:], start=(kk == 0), stop=(kk == nj - 1)),
                        ev=(f"p2:pvm{qt}_{pr}_{h}" if kk == nj - 1 else None), waits=w)
                    ctx.t(lambda t, hi_=hi_, sbk=sbk, kk=kk, nj=nj: t.matmul(
                        pz[hi_][:], onesb[:], p_b[sbk][:],
                        start=(kk == 0), stop=(kk == nj - 1)),
                        ev=f"p2:pz{qt}_{pr}_{h}_{kk}")
            for hi_, h in enumerate((h0, h1)):
                ctx.v(lambda v, hi_=hi_: v.tensor_single_scalar(
                    zg_b[hi_][:], pz[hi_][:], 1e-35, OP.max),
                    ev=f"p2:zg{qt}_{pr}_{h}", waits=[f"p2:pz{qt}_{pr}_{h}_{nj - 1}"])
                ctx.a(lambda a, hi_=hi_: a.activation(
                    rz_b[hi_][:], zg_b[hi_][:], AF.Ln,
                    bias=cc(C_ZERO), scale=cc(C_ONE)),
                    waits=[f"p2:zg{qt}_{pr}_{h}"])
                ctx.a(lambda a, hi_=hi_: a.activation(
                    rz_b[hi_][:], rz_b[hi_][:], AF.Exp,
                    bias=cc(C_ZERO), scale=cc(C_MONE)),
                    ev=f"p2:rz{qt}_{pr}_{h}")
                w = [f"p2:rz{qt}_{pr}_{h}", f"p2:pvm{qt}_{pr}_{h}"]
                if qt > 0 and pr == 0:
                    w.append(f"p2:aod{qt-1}")
                ctx.v(lambda v, h=h, hi_=hi_: v.tensor_mul(
                    ao_b[:, h * 512:(h + 1) * 512], ppv[hi_][:], rz_b[hi_][:]),
                    ev=f"p2:ao0{qt}_{pr}_{h}", waits=w)
                if qt == 0:
                    ctx.v(lambda v, h=h, gkv=gkv: v.tensor_sub(
                        dt_b[:], vme_b[:, gkv * 64:(gkv + 1) * 64],
                        ao_b[:, h * 512:h * 512 + 64]),
                        waits=[f"p2:ao0{qt}_{pr}_{h}", f"p2:vme{gkv}"], flush=True)
                    ctx.v(lambda v, h=h: v.scalar_tensor_tensor(
                        ao_b[:, h * 512:h * 512 + 64], dt_b[:], umask[:, 0:1],
                        ao_b[:, h * 512:h * 512 + 64], OP.mult, OP.add),
                        ev=f"p2:aow{qt}_{pr}_{h}")
                else:
                    ctx.events[f"p2:aow{qt}_{pr}_{h}"] = ctx.events[f"p2:ao0{qt}_{pr}_{h}"]
        ctx.d(lambda s, qt=qt: s.dma_start(
            aosr[:, :, qt * 512:(qt + 1) * 512],
            ao_b[:].rearrange("p (o c) -> p o c", c=512)),
            ev=f"p2:aod{qt}",
            waits=[f"p2:aow{qt}_{pr}_{h}" for pr in range(8)
                   for h in (2 * pr, 2 * pr + 1)])

    ctx.bar("p2")
    freebag(pbag)
    freebag(bag)
    freebag(kvbag)

    bag, pbag = [], []
    aores = sb("aores", [128, 16 * 2048], BF16, bag)
    wob = sb("wob", [128, 16 * 2048], BF16, bag)
    wstr3 = [sb(f"wstr3{i}", [128, 2048], F32, bag) for i in range(2)]
    apfm = sb("apfm", [128, 2048], F32, bag)
    aptm = sb("aptm", [128, 16 * 128], F32, bag)
    poc3 = ps("poc3", [128, 2048], F32, pbag)
    ptr3 = [ps(f"ptr3{i}", [128, 128], F32, pbag) for i in range(2)]

    for ic in range(16):
        par = ic % 2
        w = [f"p2:d{i}" for i in range(16)] if ic < 2 else [f"p3:wc{ic-2}"]
        ctx.d(lambda s, ic=ic, par=par: s.dma_start(
            wstr3[par][:], wo_d[ic * 128:(ic + 1) * 128, :]),
            ev=f"p3:wl{ic}", waits=w)
        ctx.v(lambda v, ic=ic, par=par: v.tensor_copy(
            wob[:, ic * 2048:(ic + 1) * 2048], wstr3[par][:]),
            ev=f"p3:wc{ic}", waits=[f"p3:wl{ic}"])
    ctx.d(lambda s: s.dma_start(
        aores[:].rearrange("p (o c) -> p o c", c=2048), aosr[:, :, :]),
        ev="p3:aol", waits=[f"p2:d{i}" for i in range(16)])

    for oc in range(16):
        for ic in range(16):
            for ns in range(4):
                w = []
                if ic == 0 and ns == 0:
                    w = (["p3:aol", "p3:wc15"] if oc == 0 else [f"p3:ape{oc-1}"])
                ctx.t(lambda t, oc=oc, ic=ic, ns=ns: t.matmul(
                    poc3[:, ns * 512:(ns + 1) * 512],
                    wob[:, ic * 2048 + oc * 128: ic * 2048 + oc * 128 + 128],
                    aores[:, ic * 2048 + ns * 512: ic * 2048 + ns * 512 + 512],
                    start=(ic == 0), stop=(ic == 15)),
                    ev=(f"p3:mm{oc}" if (ic == 15 and ns == 3) else None), waits=w)
        w = [f"p3:mm{oc}"]
        if oc > 0:
            w.append(f"p3:tr{oc-1}:T")
        ctx.v(lambda v, oc=oc: v.tensor_scalar(
            apfm[:], poc3[:], bo_s[:, oc:oc + 1], None, OP.add),
            ev=f"p3:ape{oc}", waits=w)
        emit_tr(f"p3:tr{oc}", apfm, aptm, 0, ptr3, 16, 128, "v",
                dtw=[f"p3:ape{oc}"] + ([f"p3:aps{oc-1}"] if oc > 0 else []))
        ctx.d(lambda s, oc=oc: s.dma_start(
            apsr[:, :, oc * 128:(oc + 1) * 128],
            aptm[:].rearrange("p (i c) -> p i c", c=128)),
            ev=f"p3:aps{oc}", waits=[f"p3:tr{oc}:E"])

    ctx.bar("p3")
    freebag(pbag)
    freebag(bag)

    bag, pbag = [], []
    i1res = sb("i1res", [128, 16 * 2048], BF16, bag)
    wt4 = [sb(f"wt4{i}", [128, 16 * 128], F32, bag) for i in range(2)]
    tern4 = [sb(f"tern4{i}", [128, 16 * 128], BF16, bag) for i in range(2)]
    ga4 = sb("ga4", [128, 2048], F32, bag)
    gb4 = sb("gb4", [128, 2048], F32, bag)
    h4 = sb("h4", [128, 2048], F32, bag)
    g4 = [sb(f"g4{i}", [128, 2048], F32, bag) for i in range(2)]
    poc4 = [ps(f"poc4{i}", [128, 2048], F32, pbag) for i in range(2)]

    ctx.d(lambda s: s.dma_start(
        i1res[:].rearrange("p (o c) -> p o c", c=2048), i1sr[:, :, :]),
        ev="p4:i1l", waits=[f"p3:d{i}" for i in range(16)])
    for oc in range(2):
        ctx.d(lambda s, oc=oc: s.dma_start(
            wt4[oc % 2][:].rearrange("p (i c) -> p i c", c=128),
            wup_r[:, :, oc * 128:(oc + 1) * 128]),
            ev=f"p4:wl{oc}", waits=[f"p3:d{i}" for i in range(16)])
    for ocx in range(32):
        par = ocx % 2
        w = [f"p4:wl{ocx}", "a:sv"]
        if ocx >= 2:
            w.append(f"p4:mm{ocx-2}")
        ctx.v(lambda v, par=par: v.tensor_scalar(
            ga4[:], wt4[par][:], svals[:, 1:2], 0.5, OP.mult, OP.is_gt), waits=w)
        ctx.v(lambda v, par=par: v.tensor_scalar(
            gb4[:], wt4[par][:], svals[:, 1:2], -0.5, OP.mult, OP.is_lt))
        ctx.v(lambda v, par=par: v.tensor_sub(tern4[par][:], ga4[:], gb4[:]),
              ev=f"p4:tq{ocx}")
        if ocx + 2 < 32:
            ctx.d(lambda s, ocx=ocx, par=par: s.dma_start(
                wt4[par][:].rearrange("p (i c) -> p i c", c=128),
                wup_r[:, :, (ocx + 2) * 128:(ocx + 3) * 128]),
                ev=f"p4:wl{ocx+2}", waits=[f"p4:tq{ocx}"])
        for ic in range(16):
            for ns in range(4):
                w = []
                if ic == 0 and ns == 0:
                    w = [f"p4:tq{ocx}"] + (["p4:i1l"] if ocx == 0 else []) + \
                        ([f"p4:gs{ocx-2}"] if ocx >= 2 else [])
                ctx.t(lambda t, ocx=ocx, ic=ic, ns=ns, par=par: t.matmul(
                    poc4[par][:, ns * 512:(ns + 1) * 512],
                    tern4[par][:, ic * 128:(ic + 1) * 128],
                    i1res[:, ic * 2048 + ns * 512: ic * 2048 + ns * 512 + 512],
                    start=(ic == 0), stop=(ic == 15)),
                    ev=(f"p4:mm{ocx}" if (ic == 15 and ns == 3) else None), waits=w)
        ctx.a(lambda a, par=par: a.activation(
            h4[:], poc4[par][:], AF.Relu, bias=0.0, scale=1.0),
            waits=[f"p4:mm{ocx}"], ev=f"p4:h{ocx}")
        w = [f"p4:gs{ocx-2}", f"p4:h{ocx}"] if ocx >= 2 else [f"p4:h{ocx}"]
        ctx.v(lambda v, par=par: v.tensor_mul(g4[par][:], h4[:], h4[:]),
              ev=f"p4:g{ocx}", waits=w)
        ctx.d(lambda s, ocx=ocx, par=par: s.dma_start(gsr[ocx, :, :], g4[par][:]),
              ev=f"p4:gs{ocx}", waits=[f"p4:g{ocx}"])

    ctx.bar("p4")
    freebag(pbag)
    freebag(bag)

    bag, pbag = [], []
    i2half = sb("i2half", [128, 32 * 1024], BF16, bag)
    big45 = sb("big45", [128, 8192], F32, bag)
    gsl = big45
    wt6 = [big45[:, 0:4096], big45[:, 4096:8192]]
    hia5 = sb("hia5", [128, 2048], BF16, bag)
    loa5 = sb("loa5", [128, 2048], BF16, bag)
    t2s = sb("t2s", [128, 4096], F32, bag)
    u5 = sb("u5", [128, 1024], F32, bag)
    i325 = sb("i325", [128, 1024], I32, bag)
    ga5 = sb("ga5", [128, 2048], F32, bag)
    gb5 = sb("gb5", [128, 2048], F32, bag)
    ints5 = sb("ints5", [128, 4096], BF16, bag)
    rq5 = sb("rq5", [128, 8], F32, bag)
    tern6 = [sb(f"tern6{i}", [128, 32 * 128], BF16, bag) for i in range(2)]
    ffn6 = sb("ffn6", [128, 1024], F32, bag)
    ap6 = [sb(f"ap6{i}", [128, 8 * 128], F32, bag) for i in range(2)]
    xo6 = [sb(f"xo6{i}", [128, 8 * 128], F32, bag) for i in range(2)]
    apx6 = [sb(f"apx6{i}", [128, 8 * 128], F32, bag) for i in range(2)]
    ob6 = [sb(f"ob6{i}", [128, 8 * 128], F32, bag) for i in range(2)]
    pre5 = sb("pre5", [128, 48], F32, bag)

    ctx.v(lambda v: v.tensor_scalar(
        pre5[:, 0:16], mxc1b[:], svals[:, 0:1],
        float(1.0 / (127.0 * math.sqrt(2048.0))), OP.mult, OP.mult),
        waits=["a:sv"], flush=True)
    ctx.v(lambda v: v.tensor_mul(alpsq[:], pre5[:, 0:16], pre5[:, 0:16]),
          flush=True)
    ctx.v(lambda v: v.reciprocal(pre5[:, 16:32], alpsq[:]), flush=True)
    ctx.v(lambda v: v.tensor_single_scalar(pre5[:, 32:48], pre5[:, 16:32],
                                           float(1e-5 * 64.0), OP.mult),
          ev="p5:pre", flush=True)
    for half in range(2):
        p5ps = []
        ptm5 = [ps(f"ptm5_{half}_{i}", [128, 1024], F32, p5ps) for i in range(2)]
        pbtr5 = [ps(f"pbtr5{i}_{half}", [128, 128], BF16, p5ps) for i in range(2)]
        for slq in range(4):
            sl = half * 4 + slq
            if sl == 0:
                w = [f"p4:d{i}" for i in range(16)] + ["p4:t", "p4:v"]
            elif slq == 0:
                w = [f"p6:od0_15"]
            else:
                w = [f"p5:ld{sl-1}"]
            ctx.d(lambda s, sl=sl: s.dma_start(
                gsl[:].rearrange("p (c q) -> p c q", q=256),
                gslr[:, :, sl * 256:(sl + 1) * 256]),
                ev=f"p5:gl{sl}", waits=w)
            bit = 1
            k = 0
            while bit < 32:
                for c in range(32):
                    if c & bit:
                        continue
                    a0, b0 = c * 256, (c | bit) * 256
                    ctx.v(lambda v, a0=a0, b0=b0: v.tensor_sub(
                        gsl[:, b0:b0 + 256], gsl[:, a0:a0 + 256], gsl[:, b0:b0 + 256]),
                        waits=[f"p5:gl{sl}"] if k == 0 else ())
                    ctx.v(lambda v, a0=a0, b0=b0: v.scalar_tensor_tensor(
                        gsl[:, a0:a0 + 256], gsl[:, a0:a0 + 256], 2.0,
                        gsl[:, b0:b0 + 256], OP.mult, OP.subtract),
                        ev=f"p5:bf{sl}_{k}")
                    k += 1
                bit *= 2
            bfd = f"p5:bf{sl}_{k-1}"
            for tt in range(2):
                tg = sl * 2 + tt
                for fh in range(2):
                    gv = gsl[:].rearrange("p (c q) -> p c q", q=256)[
                        :, fh * 16:(fh + 1) * 16, tt * 128:tt * 128 + 128]
                    w = [bfd]
                    if fh == 1:
                        w.append(f"p5:fm{tg}_1_7")
                    elif tg > 0:
                        w.append(f"p5:fm{tg-1}_3_7")
                    ctx.v(lambda v, gv=gv: v.tensor_copy(hia5[:], gv),
                          ev=f"p5:hi{tg}_{fh}", waits=w)
                    ctx.v(lambda v, gv=gv: v.tensor_tensor(
                        loa5[:].rearrange("p (c q) -> p c q", q=128), gv,
                        hia5[:].rearrange("p (c q) -> p c q", q=128), OP.subtract),
                        ev=f"p5:lo{tg}_{fh}")
                    for fqh in range(2):
                        fq = fh * 2 + fqh
                        par5 = fq % 2
                        for cq in range(8):
                            ch5 = fqh * 8 + cq
                            ctx.t(lambda t, cq=cq, ch5=ch5, par5=par5: t.matmul(
                                ptm5[par5][:, cq * 128:(cq + 1) * 128],
                                hia5[:, ch5 * 128:(ch5 + 1) * 128],
                                h128[:], start=(cq % 4 == 0), stop=False),
                                waits=[f"p5:lo{tg}_{fh}"] +
                                      ([f"p5:t2e{tg}_{fq-2}"] if (cq == 0 and fq >= 2) else
                                       ([f"p5:t2e{tg-1}_{fq+2}"] if (cq == 0 and tg > 0) else [])))
                            ctx.t(lambda t, cq=cq, ch5=ch5, par5=par5: t.matmul(
                                ptm5[par5][:, cq * 128:(cq + 1) * 128],
                                loa5[:, ch5 * 128:(ch5 + 1) * 128],
                                h128[:], start=False, stop=(cq % 4 == 3)),
                                ev=f"p5:fm{tg}_{fq}_{cq}")
                        ctx.a(lambda a, fq=fq, par5=par5: a.activation(
                            t2s[:, fq * 1024:(fq + 1) * 1024], ptm5[par5][:],
                            AF.Copy, bias=0.0, scale=1.0),
                            ev=f"p5:t2e{tg}_{fq}",
                            waits=[f"p5:fm{tg}_{fq}_3", f"p5:fm{tg}_{fq}_7"])
                ctx.v(lambda v: v.tensor_reduce(rq5[:, 0:1], t2s[:], AX, OP.max,
                                                apply_absolute_value=True),
                      waits=[f"p5:t2e{tg}_{fq}" for fq in range(4)], flush=True)
                ctx.v(lambda v, tg=tg: v.tensor_max(
                    rq5[:, 1:2], rq5[:, 0:1], pre5[:, 32 + tg:33 + tg]),
                    flush=True)
                ctx.v(lambda v: v.reciprocal(rq5[:, 2:3], rq5[:, 1:2]), flush=True)
                ctx.v(lambda v: v.tensor_single_scalar(rq5[:, 7:8], rq5[:, 2:3],
                                                       127.0, OP.mult),
                      ev=f"p5:inv{tg}", flush=True)
                ctx.v(lambda v, tg=tg: v.tensor_scalar(
                    rq5[:, 6:7], rq5[:, 1:2], alpsq[:, tg:tg + 1], svals[:, 2:3],
                    OP.mult, OP.mult), flush=True)
                ctx.v(lambda v, tg=tg: v.tensor_single_scalar(
                    betab[:, tg:tg + 1], rq5[:, 6:7],
                    float(1.0 / (127.0 * 64.0)), OP.mult), ev=f"p5:beta{tg}",
                    flush=True)
                for quar in range(4):
                    q0 = quar * 1024
                    gh = ga5[:, (quar % 2) * 1024:(quar % 2) * 1024 + 1024]
                    gh2 = gb5[:, (quar % 2) * 1024:(quar % 2) * 1024 + 1024]
                    w = [f"p5:inv{tg}"]
                    if quar >= 1:
                        w.append(f"p5:iq{tg}_{quar-1}a")
                    elif tg > 0:
                        w.append(f"p5:iq{tg-1}_3a")
                    ctx.a(lambda a, q0=q0: a.activation(
                        u5[:], t2s[:, q0:q0 + 1024], AF.Copy, bias=0.0,
                        scale=rq5[:, 7:8]), waits=w, ev=f"p5:u{tg}_{quar}")
                    ctx.v(lambda v: v.tensor_copy(i325[:], u5[:]),
                          waits=[f"p5:u{tg}_{quar}"], ev=f"p5:i32{tg}_{quar}")
                    ctx.v(lambda v, gh2=gh2: v.tensor_sub(gh2, u5[:], i325[:]))
                    ctx.v(lambda v, gh=gh, gh2=gh2: v.tensor_single_scalar(
                        gh, gh2, 0.5, OP.is_gt))
                    ctx.v(lambda v, gh2=gh2: v.tensor_single_scalar(
                        gh2, gh2, -0.5, OP.is_lt))
                    ctx.v(lambda v, gh=gh: v.tensor_add(gh, gh, i325[:]),
                          ev=f"p5:iq{tg}_{quar}a")
                    w = [f"p5:itr{tg-1}_31"] if (quar == 0 and tg > 0) else []
                    ctx.v(lambda v, q0=q0, gh=gh, gh2=gh2: v.tensor_sub(
                        ints5[:, q0:q0 + 1024], gh, gh2),
                        ev=f"p5:iq{tg}_{quar}", waits=w)
                for c in range(32):
                    par = c % 2
                    w = [f"p5:iq{tg}_{c // 8}"]
                    if c >= 2:
                        w.append(f"p5:ie{tg}_{c-2}")
                    ctx.t(lambda t, c=c, par=par: t.transpose(
                        pbtr5[par][:], ints5[:, c * 128:(c + 1) * 128], idb[:]),
                        ev=f"p5:itr{tg}_{c}", waits=w)
                    toff = slq * 256 + tt * 128
                    w2 = [f"p5:itr{tg}_{c}"]
                    if tg == half * 8 and c == 0:
                        w2.append(f"p6:mm{1 - half}_15" if half == 1 else None)
                    ctx.a(lambda a, c=c, par=par, toff=toff: a.activation(
                        i2half[:, c * 1024 + toff: c * 1024 + toff + 128],
                        pbtr5[par][:], AF.Copy, bias=0.0, scale=1.0),
                        ev=f"p5:ie{tg}_{c}", waits=[x for x in w2 if x])
            ctx.events[f"p5:ld{sl}"] = ctx.events[f"p5:lo{sl * 2 + 1}_1"]
        i2done = [f"p5:ie{(half * 4 + slq) * 2 + tt}_{c}"
                  for slq in range(4) for tt in range(2) for c in range(32)]
        freebag(p5ps)

        p6ps = []
        pdn = [ps(f"pdn{i}_{half}", [128, 1024], F32, p6ps) for i in range(2)]
        ptr6 = [ps(f"ptr6{i}_{half}", [128, 128], F32, p6ps) for i in range(2)]

        def p6_load_quant(oc, extra):
            par = oc % 2
            if half == 0:
                ctx.d(lambda s, oc=oc, par=par: s.dma_start(
                    wt6[par].rearrange("p (i c) -> p i c", c=128),
                    wdn_r[:, :, oc * 128:(oc + 1) * 128]),
                    ev=f"p6:wl{oc}", waits=[f"p5:ld{3}"] + extra)
                ctx.v(lambda v, par=par: v.tensor_scalar(
                    ga5[:], wt6[par][:, 0:2048], svals[:, 3:4], 0.5,
                    OP.mult, OP.is_gt), waits=[f"p6:wl{oc}", "a:sv"] + extra)
                ctx.v(lambda v, par=par: v.tensor_scalar(
                    gb5[:], wt6[par][:, 0:2048], svals[:, 3:4], -0.5,
                    OP.mult, OP.is_lt))
                ctx.v(lambda v, par=par: v.tensor_sub(
                    tern6[par][:, 0:2048], ga5[:], gb5[:]))
                ctx.v(lambda v, par=par: v.tensor_scalar(
                    ga5[:], wt6[par][:, 2048:4096], svals[:, 3:4], 0.5,
                    OP.mult, OP.is_gt))
                ctx.v(lambda v, par=par: v.tensor_scalar(
                    gb5[:], wt6[par][:, 2048:4096], svals[:, 3:4], -0.5,
                    OP.mult, OP.is_lt))
                ctx.v(lambda v, par=par: v.tensor_sub(
                    tern6[par][:, 2048:4096], ga5[:], gb5[:]), ev=f"p6:tq0_{oc}")
                ctx.d(lambda s, oc=oc, par=par: s.dma_start(
                    tdsr[:, :, oc * 128:(oc + 1) * 128],
                    tern6[par][:].rearrange("p (i c) -> p i c", c=128)),
                    ev=f"p6:tds{oc}", waits=[f"p6:tq0_{oc}"])
            else:
                ctx.d(lambda s, oc=oc, par=par: s.dma_start(
                    tern6[par][:].rearrange("p (i c) -> p i c", c=128),
                    tdsr[:, :, oc * 128:(oc + 1) * 128]),
                    ev=f"p6:tq1_{oc}", waits=[f"p6:tds{oc}"] + extra)

        for ocx in range(16):
            par = ocx % 2
            if ocx == 0:
                fw = [] if half == 0 else [f"p6:od0_15"]
                p6_load_quant(0, fw)
                p6_load_quant(1, fw)
            for ic in range(32):
                for ns in range(2):
                    w = []
                    if ic == 0 and ns == 0:
                        w = [f"p6:tq{half}_{ocx}"] + (i2done if ocx == 0 else []) + \
                            ([f"p6:fe{half}_{ocx-2}"] if ocx >= 2 else [])
                    ctx.t(lambda t, ocx=ocx, ic=ic, ns=ns, par=par: t.matmul(
                        pdn[par][:, ns * 512:(ns + 1) * 512],
                        tern6[par][:, ic * 128:(ic + 1) * 128],
                        i2half[:, ic * 1024 + ns * 512: ic * 1024 + ns * 512 + 512],
                        start=(ic == 0), stop=(ic == 31)),
                        ev=(f"p6:mm{half}_{ocx}" if (ic == 31 and ns == 1) else None),
                        waits=w)
            if ocx + 2 < 16:
                ex = [f"p6:mm{half}_{ocx}"]
                if half == 0:
                    ex.append(f"p6:tds{ocx}")
                p6_load_quant(ocx + 2, ex)
            w = [f"p6:mm{half}_{ocx}"]
            if ocx > 0:
                w.append(f"p6:ftr{half}_{ocx-1}_7")
            ctx.a(lambda a, par=par: a.activation(
                ffn6[:], pdn[par][:], AF.Copy, bias=0.0, scale=1.0),
                ev=f"p6:fe{half}_{ocx}", waits=w)
            ctx.d(lambda s, ocx=ocx, par=par, half=half: s.dma_start(
                ap6[par][:].rearrange("p (i c) -> p i c", c=128),
                apsr[:, half * 8:(half + 1) * 8, ocx * 128:(ocx + 1) * 128]),
                ev=f"p6:apl{half}_{ocx}",
                waits=[f"p3:aps{ocx}"] +
                      ([f"p6:apx{half}_{ocx-2}"] if ocx >= 2 else
                       ([f"p6:apx0_{14 + ocx}"] if half == 1 else [])))
            ctx.d(lambda s, ocx=ocx, par=par, half=half: s.dma_start(
                xo6[par][:].rearrange("p (i c) -> p i c", c=128),
                xor_[:, half * 8:(half + 1) * 8, ocx * 128:(ocx + 1) * 128]),
                ev=f"p6:xl{half}_{ocx}")
            obp = ocx % 2
            ctx.v(lambda v, par=par, obp=obp: v.tensor_add(
                apx6[obp][:], ap6[par][:], xo6[par][:]),
                ev=f"p6:apx{half}_{ocx}",
                waits=[f"p6:apl{half}_{ocx}", f"p6:xl{half}_{ocx}"] +
                      ([f"p6:ob{half}_{ocx-2}_7"] if ocx >= 2 else []))
            for tt in range(8):
                tpar = tt % 2
                tgl = half * 8 + tt
                w = [f"p6:fe{half}_{ocx}"]
                if tt >= 2:
                    w.append(f"p6:ob{half}_{ocx}_{tt-2}")
                elif ocx > 0:
                    w.append(f"p6:ob{half}_{ocx-1}_{tt+6}")
                elif half == 1:
                    w.append(f"p6:ob0_15_{tt+6}")
                ctx.t(lambda t, tt=tt, tpar=tpar: t.transpose(
                    ptr6[tpar][:], ffn6[:, tt * 128:(tt + 1) * 128], idf[:]),
                    ev=f"p6:ftr{half}_{ocx}_{tt}", waits=w)
                w = [f"p6:ftr{half}_{ocx}_{tt}", f"p6:apx{half}_{ocx}",
                     f"p5:beta{tgl}"]
                if tt == 0:
                    w.append(f"p6:od{half}_{ocx-2}" if ocx > 1 else
                             (f"p6:od0_{14 + ocx}" if half == 1 else None))
                ctx.v(lambda v, tt=tt, tpar=tpar, tgl=tgl, obp=obp: v.scalar_tensor_tensor(
                    ob6[obp][:, tt * 128:(tt + 1) * 128], ptr6[tpar][:],
                    betab[:, tgl:tgl + 1], apx6[obp][:, tt * 128:(tt + 1) * 128],
                    OP.mult, OP.add),
                    ev=f"p6:ob{half}_{ocx}_{tt}", waits=[x for x in w if x])
            ctx.d(lambda s, ocx=ocx, half=half, obp=obp: s.dma_start(
                outr[:, half * 8:(half + 1) * 8, ocx * 128:(ocx + 1) * 128],
                ob6[obp][:].rearrange("p (i c) -> p i c", c=128)),
                ev=f"p6:od{half}_{ocx}",
                waits=[f"p6:ob{half}_{ocx}_{tt}" for tt in range(8)])
        freebag(p6ps)

    if DEBUG:
        ctx.v(lambda v: v.tensor_copy(ga5[:, 0:4], svals[:, 0:4]), waits=["a:sv"])
        ctx.v(lambda v: v.tensor_copy(ga5[:, 8:24], mxc1b[:]))
        ctx.v(lambda v: v.tensor_copy(ga5[:, 24:40], alpsq[:]))
        ctx.v(lambda v: v.tensor_copy(ga5[:, 40:56], betab[:]), ev="dbg:v")
        ctx.d(lambda s: s.dma_start(dbg_d[:, 0:56], ga5[:, 0:56]), waits=["dbg:v"])

    ctx.bar("fin")
    freebag(pbag)
    freebag(bag)
    freebag(perm)

    for i, val in enumerate(CSTVALS):
        try:
            nc.const_aps.register(F32, float(val), cst[:, i:i + 1])
        except Exception:
            pass
    sems = {s: nc.alloc_semaphore(f"sem_{s}") for s in ALLSEMS}

    def replay(engine, lst, inc):
        lastw = {}
        for fn, waits, sem in lst:
            for (we, wv) in waits:
                if lastw.get(we, -1) < wv:
                    engine.wait_ge(sems[we], wv)
                    lastw[we] = wv
            fn(engine).then_inc(sems[sem], inc)

    with nc.Block() as blk:
        @blk.tensor
        def _(t):
            replay(t, ctx.q["t"], 1)

        @blk.vector
        def _(v):
            replay(v, ctx.q["v"], 1)

        @blk.scalar
        def _(a):
            replay(a, ctx.q["a"], 1)

        @blk.gpsimd
        def _(g):
            replay(g, ctx.q["g"], 1)

        @blk.sync
        def _(s):
            replay(s, ctx.q["d"], 16)
            for ds in DSEMS:
                s.wait_ge(sems[ds], ctx.cnt[ds])

    return nc


_NC_CACHE = [None]


def kernel(L0_attn_x, L0_attn_Wq, L0_attn_Wkv, L0_attn_Wo, L0_attn_bo,
           L0_ffn_w_up, L0_ffn_w_down):
    X = np.ascontiguousarray(np.asarray(L0_attn_x, np.float32))
    Wq = np.ascontiguousarray(np.asarray(L0_attn_Wq, np.float32))
    Wkv = np.ascontiguousarray(np.asarray(L0_attn_Wkv, np.float32))
    Wo = np.ascontiguousarray(np.asarray(L0_attn_Wo, np.float32))
    bo = np.ascontiguousarray(np.asarray(L0_attn_bo, np.float32))
    WupT = np.ascontiguousarray(np.asarray(L0_ffn_w_up, np.float32).T)
    WdnT = np.ascontiguousarray(np.asarray(L0_ffn_w_down, np.float32).T)

    idf = np.eye(128, dtype=np.float32)
    idb = np.eye(128, dtype=np.float32).astype(BFNP)
    onesb = np.ones((128, 128), dtype=BFNP)
    onesf = np.ones((128, 128), np.float32)
    h128 = hmat(128).astype(BFNP)
    cstm = np.tile(np.array(CSTVALS, np.float32)[None, :], (128, 1))
    bo_t = np.ascontiguousarray(bo.reshape(16, 128).T)

    if _NC_CACHE[0] is None:
        _NC_CACHE[0] = build_program()
    nc = _NC_CACHE[0]

    in_maps = []
    for core in range(NCORE):
        b, qh = core // 2, core % 2
        rows = own_rows(qh)
        in_maps.append({
            "x": X[b],
            "xo": np.ascontiguousarray(X[b, rows]),
            "wq": Wq, "wkv": Wkv, "wo": Wo, "bo": bo_t,
            "wupT": WupT, "wdnT": WdnT,
            "idf_d": idf, "idb_d": idb, "onesb_d": onesb, "onesf_d": onesf,
            "h128_d": h128,
            "askew_d": build_askew(qh),
            "umask_d": np.full((128, 1), 1.0 if qh == 0 else 0.0, np.float32),
            "cst_d": cstm,
        })
    res = run_bass_kernel_spmd(nc, in_maps, core_ids=list(range(NCORE)))
    out = np.zeros((4, S, D), np.float32)
    for core in range(NCORE):
        b, qh = core // 2, core % 2
        out[b, own_rows(qh), :] = res.results[core]["out"]
    return out


if __name__ == "__main__":
    import time
    t0 = time.time()
    build_program()
    print(f"build ok in {time.time() - t0:.1f}s")


# version 16
# speedup vs baseline: 1.0345x; 1.0003x over previous
import math
import numpy as np
import ml_dtypes
import concourse.bass as bass
import concourse.mybir as mybir
from concourse.bass_utils import run_bass_kernel_spmd

F32 = mybir.dt.float32
BF16 = mybir.dt.bfloat16
I16 = mybir.dt.int16
I32 = mybir.dt.int32
OP = mybir.AluOpType
AF = mybir.ActivationFunctionType
AX = mybir.AxisListType.X

BFNP = ml_dtypes.bfloat16

D = 2048
S = 4096
TOWN = 2048
NCORE = 8
SKEW_W = 7680
SKEW_BASE = 3968

SLOPES = [float(np.float32((2.0 ** (-0.5)) ** (h + 1))) for h in range(16)]
MARGIN = 75.0
KEEPC = [257 + int(math.ceil(MARGIN / s)) for s in SLOPES]


def kept_js(hmax, qt):
    i0, i1 = qt * 1024, qt * 1024 + 1024
    C = KEEPC[hmax]
    out = []
    for j in range(32):
        j0, j1 = j * 128, j * 128 + 128
        dmin = max(0, j0 - (i1 - 1), i0 - (j1 - 1))
        if dmin <= C:
            out.append(j)
    return out

C_EXPB, C_ZERO, C_ONE, C_MONE, C_LNSC, C_EPS, C_QSC, C_VMSC, C_MHALF = range(9)
CSTVALS = [-15.0, 0.0, 1.0, -1.0, 1.0 / 2048.0, 1e-5, 1.0 / math.sqrt(128.0),
           1.0 / 4096.0, -0.5]

ENGS = ("t", "v", "a", "g", "d")
NDSEM = 16
DSEMS = tuple(f"d{i}" for i in range(NDSEM))
ALLSEMS = ("t", "v", "a", "g") + DSEMS


class Ctx:

    def __init__(self):
        self.q = {e: [] for e in ENGS}
        self.cnt = {s: 0 for s in ALLSEMS}
        self.events = {}
        self.dtotal = 0

    def op(self, eng, fn, ev=None, waits=(), flush=False):
        w = []
        for x in waits:
            if x is None:
                continue
            we, wv = self.events[x] if isinstance(x, str) else x
            if we != eng:
                w.append((we, wv))
        pf = getattr(self, "need_flush", None)
        if pf is None:
            self.need_flush = pf = {}
        if eng in pf and pf[eng] is not None:
            w.append((eng, pf[eng]))
            pf[eng] = None
        if eng == "d":
            sem = DSEMS[self.dtotal % NDSEM]
            self.dtotal += 1
            self.cnt[sem] += 16
            self.q[eng].append((fn, w, sem))
            if ev is not None:
                self.events[ev] = (sem, self.cnt[sem])
            return (sem, self.cnt[sem])
        self.q[eng].append((fn, w, eng))
        self.cnt[eng] += 1
        if flush:
            pf[eng] = self.cnt[eng]
        if ev is not None:
            self.events[ev] = (eng, self.cnt[eng])
        return (eng, self.cnt[eng])

    def t(self, fn, **kw):
        return self.op("t", fn, **kw)

    def v(self, fn, **kw):
        return self.op("v", fn, **kw)

    def a(self, fn, **kw):
        return self.op("a", fn, **kw)

    def g(self, fn, **kw):
        return self.op("g", fn, **kw)

    def d(self, fn, **kw):
        return self.op("d", fn, **kw)

    def bar(self, name):
        evs = []
        for e in ("t", "v", "a", "g"):
            self.events[f"{name}:{e}"] = (e, self.cnt[e])
            evs.append(f"{name}:{e}")
        for i, s in enumerate(DSEMS):
            self.events[f"{name}:d{i}"] = (s, self.cnt[s])
            evs.append(f"{name}:d{i}")
        self.events[f"{name}:d"] = (DSEMS[(self.dtotal - 1) % NDSEM],
                                    self.cnt[DSEMS[(self.dtotal - 1) % NDSEM]])
        return evs


def hmat(n):
    H = np.ones((1, 1), np.float32)
    while H.shape[0] < n:
        H = np.block([[H, H], [H, -H]])
    return H


def build_askew(qh):
    p = np.arange(128)[:, None]
    x = np.arange(SKEW_W)[None, :]
    d = (x - SKEW_BASE + qh * 512) - p
    g = np.where(np.abs(d) <= 256, -32768.0, 257.0 - np.abs(d))
    return g.astype(np.int16)


def own_rows(qh):
    return np.concatenate([np.arange(qh * 512 + k * 1024, qh * 512 + k * 1024 + 512)
                           for k in range(4)])


DEBUG = False
DBG_WA = False
DBG_KV = False
DBG_YF = False
DBG_XNF = False


def build_program():
    nc = bass.Bass("TRN2", target_bir_lowering=False)
    ctx = Ctx()

    x_full = nc.dram_tensor("x", [S, D], F32, kind="ExternalInput")
    x_own = nc.dram_tensor("xo", [TOWN, D], F32, kind="ExternalInput")
    wq_d = nc.dram_tensor("wq", [D, D], F32, kind="ExternalInput")
    wkv_d = nc.dram_tensor("wkv", [D, 1024], F32, kind="ExternalInput")
    wo_d = nc.dram_tensor("wo", [D, D], F32, kind="ExternalInput")
    bo_d = nc.dram_tensor("bo", [128, 16], F32, kind="ExternalInput")
    wup_d = nc.dram_tensor("wupT", [D, 4096], F32, kind="ExternalInput")
    wdn_d = nc.dram_tensor("wdnT", [4096, D], F32, kind="ExternalInput")
    idf_d = nc.dram_tensor("idf_d", [128, 128], F32, kind="ExternalInput")
    idb_d = nc.dram_tensor("idb_d", [128, 128], BF16, kind="ExternalInput")
    onesb_d = nc.dram_tensor("onesb_d", [128, 128], BF16, kind="ExternalInput")
    onesf_d = nc.dram_tensor("onesf_d", [128, 128], F32, kind="ExternalInput")
    h128_d = nc.dram_tensor("h128_d", [128, 128], BF16, kind="ExternalInput")
    askew_d = nc.dram_tensor("askew_d", [128, SKEW_W], I16, kind="ExternalInput")
    umask_d = nc.dram_tensor("umask_d", [128, 1], F32, kind="ExternalInput")
    cst_d = nc.dram_tensor("cst_d", [128, len(CSTVALS)], F32, kind="ExternalInput")
    out_d = nc.dram_tensor("out", [TOWN, D], F32, kind="ExternalOutput")

    kscr = "ExternalOutput"
    q_scr = nc.dram_tensor("q_scr", [D, TOWN], BF16, kind=kscr)
    ao_scr = nc.dram_tensor("ao_scr", [D, TOWN], BF16, kind=kscr)
    ap_scr = nc.dram_tensor("ap_scr", [TOWN, D], F32, kind=kscr)
    i1_scr = nc.dram_tensor("i1_scr", [D, TOWN], BF16, kind=kscr)
    g_scr = nc.dram_tensor("g_scr", [4096, TOWN], F32, kind=kscr)
    td_scr = nc.dram_tensor("td_scr", [4096, D], BF16, kind=kscr)
    if DEBUG:
        dbg_d = nc.dram_tensor("dbg", [128, 72], F32, kind="ExternalOutput")
    if DEBUG or DBG_KV:
        kv_dbg = nc.dram_tensor("kv_dbg", [128, 2 * 16384], BF16, kind="ExternalOutput")
    if DEBUG or DBG_YF:
        yf_dbg = nc.dram_tensor("yf_dbg", [128, 8192], F32, kind="ExternalOutput")
    if DEBUG or DBG_XNF:
        xnf_dbg = nc.dram_tensor("xnf_dbg", [128, 8192], BF16, kind="ExternalOutput")
    if DEBUG or DBG_WA:
        wa_dbg = nc.dram_tensor("wa_dbg", [128, 4], F32, kind="ExternalOutput")

    qsr = q_scr.rearrange("(o p) q -> p o q", p=128)
    aosr = ao_scr.rearrange("(o p) q -> p o q", p=128)
    apsr = ap_scr.rearrange("(i p) d -> p i d", p=128)
    i1sr = i1_scr.rearrange("(o p) q -> p o q", p=128)
    gsr = g_scr.rearrange("(o p) q -> o p q", p=128)
    gslr = g_scr.rearrange("(c p) q -> p c q", p=128)
    tdsr = td_scr.rearrange("(i p) d -> p i d", p=128)
    outr = out_d.rearrange("(i p) d -> p i d", p=128)
    xor_ = x_own.rearrange("(i p) d -> p i d", p=128)
    wup_r = wup_d.rearrange("(i p) d -> p i d", p=128)
    wdn_r = wdn_d.rearrange("(i p) d -> p i d", p=128)
    wkv_r = wkv_d.rearrange("(i p) d -> i p d", p=128)

    def sb(name, shape, dtype, bag):
        g = nc.sbuf_tensor(name, shape, dtype)
        t = g.__enter__()
        bag.append(g)
        return t

    def ps(name, shape, dtype, bag):
        g = nc.psum_tensor(name, shape, dtype)
        t = g.__enter__()
        bag.append(g)
        return t

    def freebag(bag):
        for g in reversed(bag):
            g.__exit__(None, None, None)
        bag.clear()

    perm = []
    idf = sb("idf", [128, 128], F32, perm)
    idb = sb("idb", [128, 128], BF16, perm)
    onesb = sb("onesb", [128, 128], BF16, perm)
    onesf = sb("onesf", [128, 128], F32, perm)
    h128 = sb("h128", [128, 128], BF16, perm)
    cst = sb("cst", [128, len(CSTVALS)], F32, perm)
    umask = sb("umask", [128, 1], F32, perm)
    bo_s = sb("bo_s", [128, 16], F32, perm)
    svals = sb("svals", [128, 8], F32, perm)
    mxc1b = sb("mxc1b", [128, 16], F32, perm)
    wacc = sb("wacc", [128, 4], F32, perm)
    alpsq = sb("alpsq", [128, 16], F32, perm)
    betab = sb("betab", [128, 16], F32, perm)

    def cc(i):
        return cst[:, i:i + 1]

    for dst, src in ((idf, idf_d), (idb, idb_d), (onesb, onesb_d), (onesf, onesf_d),
                     (h128, h128_d), (cst, cst_d), (umask, umask_d), (bo_s, bo_d)):
        ctx.d(lambda s, dst=dst, src=src: s.dma_start(dst[:], src[:, :]))
    ctx.bar("cb")

    def emit_ln(tag, prev_tag, xs, t0, xt, xcb, sqb, rb_full, rpar, xld_waits=(),
                xc_waits=(), want_cw=False):
        for stage in range(3):
            emit_ln_stage(stage, tag, prev_tag, xs, t0, xt, xcb, sqb, rb_full, rpar,
                          xld_waits, xc_waits, want_cw)

    def emit_ln_stage(stage, tag, prev_tag, xs, t0, xt, xcb, sqb, rb_full, rpar,
                      xld_waits=(), xc_waits=(), want_cw=False):
        rb = rb_full[:, rpar * 8:(rpar + 1) * 8]
        if stage == 0:
            ctx.d(lambda s: s.dma_start(xt[:], xs[t0:t0 + 128, :]), ev=f"{tag}:xld",
                  waits=xld_waits)
            ctx.v(lambda v: v.tensor_reduce(rb[:, 0:1], xt[:], AX, OP.add),
                  waits=[f"{tag}:xld"], flush=True)
            ctx.v(lambda v: v.tensor_single_scalar(rb[:, 1:2], rb[:, 0:1],
                                                   -1.0 / 2048.0, OP.mult), flush=True)
            return
        if stage == 1:
            ctx.v(lambda v: v.tensor_scalar(xcb[:], xt[:], rb[:, 1:2], None, OP.add),
                  ev=f"{tag}:xc", waits=xc_waits)
            ctx.v(lambda v: v.tensor_mul(sqb[:], xcb[:], xcb[:]))
            ctx.v(lambda v: v.tensor_reduce(rb[:, 2:3], sqb[:], AX, OP.add),
                  ev=f"{tag}:ssq",
                  waits=[f"{prev_tag}:rstd"] if prev_tag else ())
            ctx.a(lambda a: a.activation(rb[:, 3:4], rb[:, 2:3], AF.Ln,
                                         bias=cc(C_EPS), scale=cc(C_LNSC)),
                  waits=[f"{tag}:ssq"], flush=True)
            ctx.a(lambda a: a.activation(rb[:, 4:5], rb[:, 3:4], AF.Exp,
                                         bias=cc(C_ZERO), scale=cc(C_MHALF)),
                  ev=f"{tag}:rstd")
            return
        ctx.v(lambda v: v.tensor_scalar(xcb[:], xcb[:], rb[:, 4:5], None, OP.mult),
              ev=f"{tag}:xn", waits=[f"{tag}:rstd"])
        if want_cw:
            ctx.v(lambda v: v.tensor_reduce(rb[:, 5:6], xcb[:], AX, OP.add,
                                            apply_absolute_value=True),
                  ev=f"{tag}:asum",
                  waits=[f"{prev_tag}:cw"] if prev_tag else ())
            ctx.a(lambda a: a.activation(rb[:, 6:7], rb[:, 5:6], AF.Exp,
                                         bias=cc(C_ZERO), scale=cc(C_MONE)),
                  waits=[f"{tag}:asum"], flush=True)
            ctx.a(lambda a: a.activation(rb[:, 6:7], rb[:, 6:7], AF.Ln,
                                         bias=cc(C_ONE), scale=cc(C_ONE)), flush=True)
            ctx.a(lambda a: a.activation(rb[:, 7:8], rb[:, 6:7], AF.Exp,
                                         bias=cc(C_ZERO), scale=cc(C_MONE)),
                  ev=f"{tag}:cw")
            ctx.v(lambda v: v.tensor_scalar(xcb[:], xcb[:], rb[:, 7:8], None, OP.mult),
                  ev=f"{tag}:xn", waits=[f"{tag}:cw"])

    def emit_tr(tag, src, dst, dst_col, ptr, nch, out_stride, evac, dtw=(), ident=None):
        last = [None, None]
        for c in range(nch):
            par = c % 2
            w = list(dtw) if c < 2 else []
            if last[par] is not None:
                w.append(last[par])
            idm = ident if ident is not None else idf
            ctx.t(lambda t, c=c, par=par, idm=idm: t.transpose(
                ptr[par][:], src[:, c * 128:(c + 1) * 128], idm[:]),
                ev=f"{tag}:tr{c}", waits=w)
            o0 = c * out_stride + dst_col
            if evac == "a":
                ctx.a(lambda a, c=c, par=par, o0=o0: a.activation(
                    dst[:, o0:o0 + 128], ptr[par][:], AF.Copy,
                    bias=0.0, scale=1.0),
                    ev=f"{tag}:ev{c}", waits=[f"{tag}:tr{c}"])
            else:
                ctx.v(lambda v, c=c, par=par, o0=o0: v.tensor_copy(
                    dst[:, o0:o0 + 128], ptr[par][:]),
                    ev=f"{tag}:ev{c}", waits=[f"{tag}:tr{c}"])
            last[par] = f"{tag}:ev{c}"
        ctx.events[f"{tag}:T"] = ctx.events[f"{tag}:tr{nch-1}"]
        ctx.events[f"{tag}:E"] = ctx.events[f"{tag}:ev{nch-1}"]

    bag, pbag = [], []
    wqb = sb("wqb", [128, 16 * 2048], BF16, bag)
    wstr = [sb(f"wstr{i}", [128, 2048], F32, bag) for i in range(2)]
    wsb = [sb(f"wsb{i}", [128, 2048], F32, bag) for i in range(2)]
    xt_b = [sb(f"xt{i}", [128, 2048], F32, bag) for i in range(2)]
    xc2_b = [sb(f"xc{i}", [128, 2048], F32, bag) for i in range(2)]
    sq_b = sb("sq", [128, 2048], F32, bag)
    r_b = sb("r", [128, 16], F32, bag)
    xncwfm = sb("xncwfm", [128, 16 * 512], BF16, bag)
    qfm = sb("qfm", [128, 16 * 512], BF16, bag)
    ptrA = [ps(f"ptrA{i}", [128, 128], F32, pbag) for i in range(2)]
    pq = [ps(f"pq{i}", [128, 512], F32, pbag) for i in range(2)]
    psml = ps("psml", [128, 4], F32, pbag)

    for ic in range(16):
        par = ic % 2
        w = [f"cb:d{i}" for i in range(16)] if ic < 2 else [f"q:wc{ic-2}"]
        ctx.d(lambda s, ic=ic, par=par: s.dma_start(
            wstr[par][:], wq_d[ic * 128:(ic + 1) * 128, :]), ev=f"q:wl{ic}", waits=w)
        ctx.v(lambda v, ic=ic, par=par: v.tensor_copy(
            wqb[:, ic * 2048:(ic + 1) * 2048], wstr[par][:]),
            ev=f"q:wc{ic}", waits=[f"q:wl{ic}"])

    wchunks = []
    for k in range(32):
        wchunks.append((wup_d[(k % 16) * 128:(k % 16 + 1) * 128,
                              (k // 16) * 2048:(k // 16 + 1) * 2048], 0))
    for k in range(32):
        wchunks.append((wdn_d[(k % 32) * 128:(k % 32 + 1) * 128,
                              (k // 32) * 2048:(k // 32 + 1) * 2048], 1))
    ctx.v(lambda v: v.memset(wacc[:, 0:4], 0.0), ev="q:wz", waits=["cb:d0"])

    def emit_wsum(k):
        srcw, col = wchunks[k]
        par = k % 2
        ctx.d(lambda s, srcw=srcw, par=par: s.dma_start(wsb[par][:], srcw),
              ev=f"q:wl{k}", waits=["q:wz"] if k < 2 else [f"q:wr{k-2}"])
        ctx.v(lambda v, par=par: v.tensor_reduce(
            wacc[:, 2:3], wsb[par][:], AX, OP.add, apply_absolute_value=True),
            waits=[f"q:wl{k}"], flush=True)
        ctx.v(lambda v, col=col: v.tensor_add(
            wacc[:, col:col + 1], wacc[:, col:col + 1], wacc[:, 2:3]),
            ev=f"q:wr{k}", flush=True)

    for qt in range(4):
        for k in range(qt * 16, qt * 16 + 16):
            emit_wsum(k)
        for tp_ in range(2):
            tgs = [qt * 4 + tp_ * 2, qt * 4 + tp_ * 2 + 1]
            for st_ in range(3):
                for i_, tg in enumerate(tgs):
                    tag = f"q:t{tg}"
                    prevtag = f"q:t{tg-2}" if tg >= 2 else None
                    emit_ln_stage(st_, tag, prevtag, x_own, tg * 128, xt_b[i_],
                                  xc2_b[i_], sq_b, r_b, tg % 2,
                                  xld_waits=[f"q:t{tg-2}:xc"] if tg >= 2 else
                                  [f"cb:d{i}" for i in range(16)],
                                  xc_waits=[f"q:r{tg-2}:T"] if tg >= 2 else (),
                                  want_cw=True)
            for i_, tg in enumerate(tgs):
                tt = tg - qt * 4
                emit_tr(f"q:r{tg}", xc2_b[i_], xncwfm, tt * 128, ptrA, 16, 512, "a",
                        dtw=[f"q:t{tg}:xn"] +
                            ([f"q:r{tg-1}:E"] if tg >= 1 else []) +
                            ([f"q:mm{qt-1}_15_last"] if (tt == 0 and qt > 0) else []))
        trall = [f"q:r{qt * 4 + tt}:E" for tt in range(4)]
        for oc in range(16):
            par = oc % 2
            for ic in range(16):
                w = []
                if ic == 0:
                    w = list(trall)
                    if oc >= 2:
                        w.append(f"q:qe{qt}_{oc-2}")
                    elif qt > 0:
                        w.append(f"q:qe{qt-1}_{oc+14}")
                    if oc < 2:
                        w.append(f"q:qs{qt-1}" if qt > 0 else None)
                ctx.t(lambda t, oc=oc, ic=ic, par=par: t.matmul(
                    pq[par][:], wqb[:, ic * 2048 + oc * 128: ic * 2048 + oc * 128 + 128],
                    xncwfm[:, ic * 512:(ic + 1) * 512],
                    start=(ic == 0), stop=(ic == 15)),
                    ev=(f"q:mm{qt}_{oc}" if ic == 15 else None), waits=w)
            ctx.a(lambda a, oc=oc, par=par: a.activation(
                qfm[:, oc * 512:(oc + 1) * 512], pq[par][:], AF.Copy,
                bias=0.0, scale=CSTVALS[C_QSC]),
                ev=f"q:qe{qt}_{oc}", waits=[f"q:mm{qt}_{oc}"] +
                ([f"q:qs{qt-1}"] if (qt > 0 and oc < 2) else []))
        ctx.events[f"q:mm{qt}_15_last"] = ctx.events[f"q:mm{qt}_15"]
        ctx.d(lambda s, qt=qt: s.dma_start(
            qsr[:, :, qt * 512:(qt + 1) * 512],
            qfm[:].rearrange("p (o c) -> p o c", c=512)),
            ev=f"q:qs{qt}", waits=[f"q:qe{qt}_{o}" for o in range(16)])

    ctx.t(lambda t: t.matmul(psml[:, 0:2], onesf[:], wacc[:, 0:2],
                             start=True, stop=True), ev="q:ws", waits=["q:wr63"])
    ctx.v(lambda v: v.tensor_scalar(svals[:, 0:1], psml[:, 0:1],
                                    1.0 / 8388608.0, 1e-5, OP.mult, OP.max),
          waits=["q:ws"], flush=True)
    ctx.v(lambda v: v.tensor_scalar(svals[:, 2:3], psml[:, 1:2],
                                    1.0 / 8388608.0, 1e-5, OP.mult, OP.max), flush=True)
    ctx.v(lambda v: v.reciprocal(svals[:, 1:2], svals[:, 0:1]), flush=True)
    ctx.v(lambda v: v.reciprocal(svals[:, 3:4], svals[:, 2:3]), ev="a:sv", flush=True)
    if DEBUG or DBG_WA:
        ctx.d(lambda s: s.dma_start(wa_dbg[:, :], wacc[:]), waits=["a:sv"])

    ctx.bar("q")
    freebag(pbag)
    freebag(bag)

    bag, pbag = [], []
    xt_b = [sb(f"fxt{i}", [128, 2048], F32, bag) for i in range(2)]
    xc2_b = [sb(f"fxc{i}", [128, 2048], F32, bag) for i in range(2)]
    sq_b = sb("fsq", [128, 2048], F32, bag)
    r_b = sb("fr", [128, 16], F32, bag)
    yfm = sb("yfm", [128, 16 * 512], F32, bag)
    hia = sb("hia", [128, 2048], BF16, bag)
    loa = sb("loa", [128, 2048], BF16, bag)
    ubuf = sb("ubuf", [128, 2048], F32, bag)
    i32b = sb("i32b", [128, 2048], I32, bag)
    gab = sb("gab", [128, 2048], F32, bag)
    gbb = sb("gbb", [128, 2048], F32, bag)
    intstm = sb("intstm", [128, 2048], BF16, bag)
    intsfm = sb("intsfm", [128, 16 * 512], BF16, bag)
    rq = sb("rq", [128, 4], F32, bag)
    ptrA = [ps(f"fptr{i}", [128, 128], F32, pbag) for i in range(2)]
    ptm = ps("ptm", [128, 2048], F32, pbag)
    pbtr = [ps(f"fbtr{i}", [128, 128], BF16, pbag) for i in range(2)]

    for qt in range(4):
        for tp_ in range(2):
            tgs = [qt * 4 + tp_ * 2, qt * 4 + tp_ * 2 + 1]
            for st_ in range(3):
                for i_, tg in enumerate(tgs):
                    tag = f"f:t{tg}"
                    prevtag = f"f:t{tg-2}" if tg >= 2 else None
                    emit_ln_stage(st_, tag, prevtag, x_own, tg * 128, xt_b[i_],
                                  xc2_b[i_], sq_b, r_b, tg % 2,
                                  xld_waits=[f"f:t{tg-2}:xc"] if tg >= 2 else
                                  [f"q:d{i}" for i in range(16)] + ["q:t", "q:v", "q:a"],
                                  xc_waits=[f"f:r{tg-2}:T"] if tg >= 2 else ())
            for i_, tg in enumerate(tgs):
                tt = tg - qt * 4
                emit_tr(f"f:r{tg}", xc2_b[i_], yfm, tt * 128, ptrA, 16, 512, "v",
                        dtw=[f"f:t{tg}:xn"] +
                            ([f"f:r{tg-1}:E"] if tg >= 1 else []) +
                            ([f"f:losplit{qt-1}"] if (tt == 0 and qt > 0) else []))
        ydone = [f"f:r{qt * 4 + tt}:E" for tt in range(4)]
        bit = 1
        k = 0
        while bit < 16:
            for c in range(16):
                if c & bit:
                    continue
                a0, b0 = c * 512, (c | bit) * 512
                ctx.v(lambda v, a0=a0, b0=b0: v.tensor_sub(
                    yfm[:, b0:b0 + 512], yfm[:, a0:a0 + 512], yfm[:, b0:b0 + 512]),
                    waits=ydone if k == 0 else ())
                ctx.v(lambda v, a0=a0, b0=b0: v.scalar_tensor_tensor(
                    yfm[:, a0:a0 + 512], yfm[:, a0:a0 + 512], 2.0,
                    yfm[:, b0:b0 + 512], OP.mult, OP.subtract),
                    ev=f"f:bf{qt}_{k}")
                k += 1
            bit *= 2
        bfd = f"f:bf{qt}_{k-1}"
        if (DEBUG or DBG_YF) and qt == 0:
            ctx.d(lambda s: s.dma_start(yf_dbg[:, :], yfm[:]), waits=[bfd])
        for tt in range(4):
            tg = qt * 4 + tt
            yv = yfm[:].rearrange("p (c q) -> p c q", q=512)[:, :, tt * 128:tt * 128 + 128]
            w = [bfd]
            if tg > 0:
                w.append(f"f:fm{tg-1}_15")
            ctx.v(lambda v, yv=yv: v.tensor_copy(hia[:], yv), ev=f"f:hi{tg}", waits=w)
            ctx.v(lambda v, yv=yv: v.tensor_tensor(
                loa[:].rearrange("p (c q) -> p c q", q=128), yv,
                hia[:].rearrange("p (c q) -> p c q", q=128), OP.subtract),
                ev=f"f:lo{tg}")
            for c in range(16):
                ctx.t(lambda t, c=c: t.matmul(
                    ptm[:, c * 128:(c + 1) * 128], hia[:, c * 128:(c + 1) * 128],
                    h128[:], start=(c % 4 == 0), stop=False),
                    waits=[f"f:lo{tg}"] +
                          ([f"f:u{tg-1}"] if (c == 0 and tg > 0) else []))
                ctx.t(lambda t, c=c: t.matmul(
                    ptm[:, c * 128:(c + 1) * 128], loa[:, c * 128:(c + 1) * 128],
                    h128[:], start=False, stop=(c % 4 == 3)),
                    ev=f"f:fm{tg}_{c}")
            mmd = [f"f:fm{tg}_{c}" for c in range(3, 16, 4)]
            ctx.v(lambda v: v.tensor_reduce(rq[:, 0:1], ptm[:], AX, OP.max,
                                            apply_absolute_value=True), waits=mmd,
                  flush=True)
            ctx.v(lambda v, tg=tg: v.tensor_single_scalar(
                mxc1b[:, tg:tg + 1], rq[:, 0:1],
                float(np.float32(1e-5) * np.float32(math.sqrt(2048.0))), OP.max),
                flush=True)
            ctx.v(lambda v, tg=tg: v.reciprocal(rq[:, 2:3], mxc1b[:, tg:tg + 1]),
                  flush=True)
            ctx.v(lambda v: v.tensor_scalar(ubuf[:], ptm[:], rq[:, 2:3], 127.0,
                                            OP.mult, OP.mult),
                  ev=f"f:u{tg}")
            ctx.v(lambda v: v.tensor_copy(i32b[:], ubuf[:]), ev=f"f:i32{tg}")
            ctx.v(lambda v: v.tensor_sub(gbb[:], ubuf[:], i32b[:]), ev=f"f:ib{tg}")
            ctx.v(lambda v: v.tensor_single_scalar(gab[:], gbb[:], 0.5, OP.is_gt))
            ctx.v(lambda v: v.tensor_single_scalar(gbb[:], gbb[:], -0.5, OP.is_lt))
            ctx.v(lambda v: v.tensor_add(gab[:], gab[:], i32b[:]), ev=f"f:ia{tg}")
            ctx.v(lambda v: v.tensor_sub(intstm[:], gab[:], gbb[:]),
                  ev=f"f:ints{tg}",
                  waits=[f"f:i1r{tg-1}:T"] if tg > 0 else ())
            emit_tr(f"f:i1r{tg}", intstm, intsfm, tt * 128, pbtr, 16, 512, "v",
                    dtw=[f"f:ints{tg}"] +
                        ([f"f:is{qt-1}"] if (tt == 0 and qt > 0) else []), ident=idb)
        ctx.events[f"f:losplit{qt}"] = ctx.events[f"f:lo{qt * 4 + 3}"]
        ctx.d(lambda s, qt=qt: s.dma_start(
            i1sr[:, :, qt * 512:(qt + 1) * 512],
            intsfm[:].rearrange("p (o c) -> p o c", c=512)),
            ev=f"f:is{qt}", waits=[f"f:i1r{qt * 4 + tt}:E" for tt in range(4)])

    ctx.bar("f")
    freebag(pbag)
    freebag(bag)

    kvbag = []
    k_res = sb("k_res", [128, 4 * 4096], BF16, kvbag)
    v_res = sb("v_res", [128, 4 * 32 * 128], BF16, kvbag)
    askew = sb("askew", [128, SKEW_W], I16, kvbag)

    bag, pbag = [], []
    wkvb = sb("wkvb", [128, 16 * 1024], BF16, bag)
    kstr = [sb(f"kstr{i}", [128, 1024], F32, bag) for i in range(2)]
    xt_b = [sb(f"axt{i}", [128, 2048], F32, bag) for i in range(2)]
    xc2_b = [sb(f"axc{i}", [128, 2048], F32, bag) for i in range(4)]
    sq_b = sb("asq", [128, 2048], F32, bag)
    r_b = sb("ar", [128, 32], F32, bag)
    xnfm = sb("axnfm", [128, 16 * 512], BF16, bag)
    vtmp = sb("vtmp", [128, 512], BF16, bag)

    ptrA = [ps(f"aptr{i}", [128, 128], F32, pbag) for i in range(2)]
    pkv = [ps(f"pkv{i}", [128, 512], F32, pbag) for i in range(2)]
    pvtr = [ps(f"avtr{i}", [128, 128], BF16, pbag) for i in range(2)]

    ctx.d(lambda s: s.dma_start(askew[:], askew_d[:, :]), waits=[f"f:d{i}" for i in range(16)])
    for ic in range(16):
        par = ic % 2
        w = [f"f:d{i}" for i in range(16)] if ic < 2 else [f"a:wkc{ic-2}"]
        ctx.d(lambda s, ic=ic, par=par: s.dma_start(kstr[par][:], wkv_r[ic, :, :]),
              ev=f"a:wkl{ic}", waits=w)
        ctx.v(lambda v, ic=ic, par=par: v.tensor_copy(
            wkvb[:, ic * 1024:(ic + 1) * 1024], kstr[par][:]),
            ev=f"a:wkc{ic}", waits=[f"a:wkl{ic}"])
    for grp in range(8):
        for tp_ in range(2):
            tgs = [grp * 4 + tp_ * 2, grp * 4 + tp_ * 2 + 1]
            for st_ in range(3):
                for i_, tg in enumerate(tgs):
                    tag = f"a:t{tg}"
                    prevtag = f"a:t{tg-2}" if tg >= 2 else None
                    emit_ln_stage(st_, tag, prevtag, x_full, tg * 128, xt_b[tg % 2],
                                  xc2_b[tg % 4], sq_b, r_b, tg % 4,
                                  xld_waits=[f"a:t{tg-2}:xc"] if tg >= 2 else
                                  ["f:v", "f:t", "f:a", "f:g"],
                                  xc_waits=[f"a:r{tg-4}:T"] if tg >= 4 else ())
            for i_, tg in enumerate(tgs):
                tt = tg - grp * 4
                emit_tr(f"a:r{tg}", xc2_b[tg % 4], xnfm, tt * 128, ptrA, 16, 512, "a",
                        dtw=[f"a:t{tg}:xn"] +
                            ([f"a:r{tg-1}:E"] if tg >= 1 else []) +
                            ([f"a:kv{grp-1}_7mm"] if (tt == 0 and grp > 0) else []))
        trall = [f"a:r{grp * 4 + tt}:E" for tt in range(4)]
        if (DEBUG or DBG_XNF) and grp == 0:
            ctx.d(lambda s: s.dma_start(xnf_dbg[:, :], xnfm[:]), waits=trall)
        for oc in range(8):
            par = oc % 2
            for ic in range(16):
                w = []
                if ic == 0:
                    w = list(trall) + ["a:wkc15"]
                    if oc >= 2:
                        w.append(f"a:kve{grp}_{oc-2}")
                    elif grp > 0:
                        w.append(f"a:kve{grp-1}_{oc+6}")
                ctx.t(lambda t, oc=oc, ic=ic, par=par: t.matmul(
                    pkv[par][:], wkvb[:, ic * 1024 + oc * 128: ic * 1024 + oc * 128 + 128],
                    xnfm[:, ic * 512:(ic + 1) * 512], start=(ic == 0), stop=(ic == 15)),
                    ev=(f"a:kvm{grp}_{oc}" if ic == 15 else None), waits=w)
            if oc < 4:
                ctx.a(lambda a, oc=oc, grp=grp, par=par: a.activation(
                    k_res[:, oc * 4096 + grp * 512: oc * 4096 + grp * 512 + 512],
                    pkv[par][:], AF.Copy, bias=0.0, scale=1.0),
                    ev=f"a:kve{grp}_{oc}", waits=[f"a:kvm{grp}_{oc}"])
            else:
                gkv = oc - 4
                w = [f"a:kvm{grp}_{oc}"]
                if oc > 4:
                    w.append(f"a:vtr{grp}_{oc-1}_3")
                elif grp > 0:
                    w.append(f"a:vtr{grp-1}_7_3")
                ctx.a(lambda a, par=par: a.activation(
                    vtmp[:], pkv[par][:], AF.Copy, bias=0.0, scale=1.0),
                    ev=f"a:vt{grp}_{oc}", waits=w)
                for jl in range(4):
                    tpar = jl % 2
                    w2 = [f"a:vt{grp}_{oc}"]
                    if jl >= 2:
                        w2.append(f"a:vev{grp}_{oc}_{jl-2}")
                    ctx.t(lambda t, jl=jl, tpar=tpar: t.transpose(
                        pvtr[tpar][:], vtmp[:, jl * 128:(jl + 1) * 128], idb[:]),
                        ev=f"a:vtr{grp}_{oc}_{jl}", waits=w2)
                    col = (gkv * 32 + grp * 4 + jl) * 128
                    ctx.v(lambda v, col=col, tpar=tpar: v.tensor_copy(
                        v_res[:, col:col + 128], pvtr[tpar][:]),
                        ev=f"a:vev{grp}_{oc}_{jl}", waits=[f"a:vtr{grp}_{oc}_{jl}"])
                ctx.events[f"a:kve{grp}_{oc}"] = ctx.events[f"a:vev{grp}_{oc}_3"]
            if oc == 7:
                ctx.events[f"a:kv{grp}_7mm"] = ctx.events[f"a:kvm{grp}_{oc}"]

    if DEBUG or DBG_KV:
        ctx.d(lambda s: s.dma_start(kv_dbg[:, 0:16384], k_res[:]),
              waits=[f"a:kve7_{o}" for o in range(8)])
        ctx.d(lambda s: s.dma_start(kv_dbg[:, 16384:32768], v_res[:]))

    ctx.bar("pa")
    freebag(pbag)
    freebag(bag)

    bag, pbag = [], []
    qt_b = [sb(f"qtl{i}", [128, 16 * 512], BF16, bag) for i in range(2)]
    ao_b = sb("ao", [128, 16 * 512], BF16, bag)
    t_b = [sb(f"tb{i}", [128, 512], F32, bag) for i in range(4)]
    p_b = [sb(f"pb{i}", [128, 512], BF16, bag) for i in range(4)]
    zg_b = [sb(f"zg{i}", [128, 512], F32, bag) for i in range(2)]
    rz_b = [sb(f"rz{i}", [128, 512], F32, bag) for i in range(2)]
    vme_b = sb("vme", [128, 4 * 64], BF16, bag)
    dt_b = sb("dt", [128, 64], F32, bag)

    vmbag = []
    pvm = ps("pvm", [128, 64], F32, vmbag)
    for gkv in range(4):
        for j in range(32):
            w = []
            if gkv == 0 and j == 0:
                w = ["pa:t", "pa:v", "pa:a"]
            elif j == 0:
                w = [f"p2:vme{gkv-1}"]
            ctx.t(lambda t, gkv=gkv, j=j: t.matmul(
                pvm[:], v_res[:, (gkv * 32 + j) * 128:(gkv * 32 + j) * 128 + 128],
                onesb[:, 0:64], start=(j == 0), stop=(j == 31)),
                ev=(f"p2:vmm{gkv}" if j == 31 else None), waits=w)
        ctx.a(lambda a, gkv=gkv: a.activation(
            vme_b[:, gkv * 64:(gkv + 1) * 64], pvm[:], AF.Copy,
            bias=0.0, scale=CSTVALS[C_VMSC]),
            ev=f"p2:vme{gkv}", waits=[f"p2:vmm{gkv}"])
    freebag(vmbag)

    ppv = [ps(f"ppv{i}", [128, 512], F32, pbag) for i in range(2)]
    pz = [ps(f"pz{i}", [128, 512], F32, pbag) for i in range(2)]
    psc = [ps(f"psc{i}", [128, 512], F32, pbag) for i in range(4)]

    def njof(qt, pr):
        return len(kept_js(2 * pr + 1, qt))

    def _lastevt(kind, qt, pr, hi_, k):
        if k >= 0:
            return f"p2:{kind}{qt}_{pr}_{2 * pr + hi_}_{k}"
        if pr > 0:
            n = njof(qt, pr - 1)
            return f"p2:{kind}{qt}_{pr-1}_{2 * (pr - 1) + hi_}_{n + k}"
        if qt > 0:
            n = njof(qt - 1, 7)
            return f"p2:{kind}{qt-1}_7_{14 + hi_}_{n + k}"
        return None

    def lastap(qt, pr, hi_, k):
        return _lastevt("ap", qt, pr, hi_, k)

    def lastex(qt, pr, hi_, k):
        return _lastevt("ex", qt, pr, hi_, k)

    def lastpv(qt, pr, hi_, k):
        return _lastevt("pz", qt, pr, hi_, k)

    ctx.d(lambda s: s.dma_start(
        qt_b[0][:].rearrange("p (o c) -> p o c", c=512), qsr[:, :, 0:512]),
        ev="p2:ql0", waits=["q:qs0", "pa:t", "pa:v", "pa:a"] +
        [f"pa:d{i}" for i in range(16)])

    for qt in range(4):
        qpar = qt % 2
        if qt + 1 < 4:
            ctx.d(lambda s, qt=qt: s.dma_start(
                qt_b[(qt + 1) % 2][:].rearrange("p (o c) -> p o c", c=512),
                qsr[:, :, (qt + 1) * 512:(qt + 2) * 512]),
                ev=f"p2:ql{qt+1}",
                waits=[f"q:qs{qt+1}"] +
                      ([f"p2:sc{qt-1}_7_15_{njof(qt-1, 7) - 1}"] if qt >= 1 else []))
        for pr in range(8):
            h0, h1 = 2 * pr, 2 * pr + 1
            gkv = h0 // 4
            jset = kept_js(h1, qt)
            nj = len(jset)
            for k in range(nj + 1):
                if k < nj:
                    j = jset[k]
                    for hi_, h in enumerate((h0, h1)):
                        sbk = 2 * hi_ + (k % 2)
                        w = [f"p2:ql{qt}"]
                        ap = lastap(qt, pr, hi_, k - 2)
                        if ap:
                            w.append(ap)
                        if qt == 0 and pr == 0 and k == 0:
                            w += ["pa:t", "pa:a", "pa:v", f"p2:vme3"]
                        ctx.t(lambda t, gkv=gkv, j=j, h=h, sbk=sbk, qpar=qpar: t.matmul(
                            psc[sbk][:],
                            k_res[:, gkv * 4096 + j * 128: gkv * 4096 + j * 128 + 128],
                            qt_b[qpar][:, h * 512:(h + 1) * 512],
                            start=True, stop=True),
                            ev=f"p2:sc{qt}_{pr}_{h}_{k}", waits=w)
                kk = k - 1
                if kk < 0 or kk > nj - 1:
                    continue
                jj = jset[kk]
                for hi_, h in enumerate((h0, h1)):
                    sbk = 2 * hi_ + (kk % 2)
                    xoff = qt * 1024 - jj * 128 + SKEW_BASE
                    w = [f"p2:sc{qt}_{pr}_{h}_{kk}"]
                    ex = lastex(qt, pr, hi_, kk - 2)
                    if ex:
                        w.append(ex)
                    if qt == 0 and pr == 0 and kk == 0:
                        w.append("pa:d")
                    ctx.v(lambda v, h=h, sbk=sbk, xoff=xoff: v.scalar_tensor_tensor(
                        t_b[sbk][:], askew[:, xoff:xoff + 512], SLOPES[h], psc[sbk][:],
                        OP.mult, OP.add),
                        ev=(None if jj == 0 else f"p2:ap{qt}_{pr}_{h}_{kk}"), waits=w)
                    if jj == 0:
                        ctx.v(lambda v, sbk=sbk: v.memset(t_b[sbk][0:64, :], -100000.0),
                              ev=f"p2:ap{qt}_{pr}_{h}_{kk}")
                    w = [f"p2:ap{qt}_{pr}_{h}_{kk}"]
                    pv = lastpv(qt, pr, hi_, kk - 2)
                    if pv:
                        w.append(pv)
                    ctx.a(lambda a, sbk=sbk: a.activation(
                        p_b[sbk][:], t_b[sbk][:], AF.Exp,
                        bias=cc(C_EXPB), scale=cc(C_ONE)),
                        ev=f"p2:ex{qt}_{pr}_{h}_{kk}", waits=w)
                    w = [f"p2:ex{qt}_{pr}_{h}_{kk}"]
                    if kk == 0:
                        if pr > 0:
                            w.append(f"p2:aow{qt}_{pr-1}_{2 * (pr - 1) + hi_}")
                        elif qt > 0:
                            w.append(f"p2:aow{qt-1}_7_{14 + hi_}")
                    ctx.t(lambda t, gkv=gkv, jj=jj, hi_=hi_, sbk=sbk, kk=kk, nj=nj: t.matmul(
                        ppv[hi_][:],
                        v_res[:, (gkv * 32 + jj) * 128:(gkv * 32 + jj) * 128 + 128],
                        p_b[sbk][:], start=(kk == 0), stop=(kk == nj - 1)),
                        ev=(f"p2:pvm{qt}_{pr}_{h}" if kk == nj - 1 else None), waits=w)
                    ctx.t(lambda t, hi_=hi_, sbk=sbk, kk=kk, nj=nj: t.matmul(
                        pz[hi_][:], onesb[:], p_b[sbk][:],
                        start=(kk == 0), stop=(kk == nj - 1)),
                        ev=f"p2:pz{qt}_{pr}_{h}_{kk}")
            for hi_, h in enumerate((h0, h1)):
                ctx.v(lambda v, hi_=hi_: v.tensor_single_scalar(
                    zg_b[hi_][:], pz[hi_][:], 1e-35, OP.max),
                    ev=f"p2:zg{qt}_{pr}_{h}", waits=[f"p2:pz{qt}_{pr}_{h}_{nj - 1}"])
                ctx.a(lambda a, hi_=hi_: a.activation(
                    rz_b[hi_][:], zg_b[hi_][:], AF.Ln,
                    bias=cc(C_ZERO), scale=cc(C_ONE)),
                    waits=[f"p2:zg{qt}_{pr}_{h}"])
                ctx.a(lambda a, hi_=hi_: a.activation(
                    rz_b[hi_][:], rz_b[hi_][:], AF.Exp,
                    bias=cc(C_ZERO), scale=cc(C_MONE)),
                    ev=f"p2:rz{qt}_{pr}_{h}")
                w = [f"p2:rz{qt}_{pr}_{h}", f"p2:pvm{qt}_{pr}_{h}"]
                if qt > 0 and pr == 0:
                    w.append(f"p2:aod{qt-1}")
                ctx.v(lambda v, h=h, hi_=hi_: v.tensor_mul(
                    ao_b[:, h * 512:(h + 1) * 512], ppv[hi_][:], rz_b[hi_][:]),
                    ev=f"p2:ao0{qt}_{pr}_{h}", waits=w)
                if qt == 0:
                    ctx.v(lambda v, h=h, gkv=gkv: v.tensor_sub(
                        dt_b[:], vme_b[:, gkv * 64:(gkv + 1) * 64],
                        ao_b[:, h * 512:h * 512 + 64]),
                        waits=[f"p2:ao0{qt}_{pr}_{h}", f"p2:vme{gkv}"], flush=True)
                    ctx.v(lambda v, h=h: v.scalar_tensor_tensor(
                        ao_b[:, h * 512:h * 512 + 64], dt_b[:], umask[:, 0:1],
                        ao_b[:, h * 512:h * 512 + 64], OP.mult, OP.add),
                        ev=f"p2:aow{qt}_{pr}_{h}")
                else:
                    ctx.events[f"p2:aow{qt}_{pr}_{h}"] = ctx.events[f"p2:ao0{qt}_{pr}_{h}"]
        ctx.d(lambda s, qt=qt: s.dma_start(
            aosr[:, :, qt * 512:(qt + 1) * 512],
            ao_b[:].rearrange("p (o c) -> p o c", c=512)),
            ev=f"p2:aod{qt}",
            waits=[f"p2:aow{qt}_{pr}_{h}" for pr in range(8)
                   for h in (2 * pr, 2 * pr + 1)])

    ctx.bar("p2")
    freebag(pbag)
    freebag(bag)
    freebag(kvbag)

    bag, pbag = [], []
    aores = sb("aores", [128, 16 * 2048], BF16, bag)
    wob = sb("wob", [128, 16 * 2048], BF16, bag)
    wstr3 = [sb(f"wstr3{i}", [128, 2048], F32, bag) for i in range(2)]
    apfm = sb("apfm", [128, 2048], F32, bag)
    aptm = sb("aptm", [128, 16 * 128], F32, bag)
    poc3 = ps("poc3", [128, 2048], F32, pbag)
    ptr3 = [ps(f"ptr3{i}", [128, 128], F32, pbag) for i in range(2)]

    for ic in range(16):
        par = ic % 2
        w = [f"p2:d{i}" for i in range(16)] if ic < 2 else [f"p3:wc{ic-2}"]
        ctx.d(lambda s, ic=ic, par=par: s.dma_start(
            wstr3[par][:], wo_d[ic * 128:(ic + 1) * 128, :]),
            ev=f"p3:wl{ic}", waits=w)
        ctx.v(lambda v, ic=ic, par=par: v.tensor_copy(
            wob[:, ic * 2048:(ic + 1) * 2048], wstr3[par][:]),
            ev=f"p3:wc{ic}", waits=[f"p3:wl{ic}"])
    ctx.d(lambda s: s.dma_start(
        aores[:].rearrange("p (o c) -> p o c", c=2048), aosr[:, :, :]),
        ev="p3:aol", waits=[f"p2:d{i}" for i in range(16)])

    for oc in range(16):
        for ic in range(16):
            for ns in range(4):
                w = []
                if ic == 0 and ns == 0:
                    w = (["p3:aol", "p3:wc15"] if oc == 0 else [f"p3:ape{oc-1}"])
                ctx.t(lambda t, oc=oc, ic=ic, ns=ns: t.matmul(
                    poc3[:, ns * 512:(ns + 1) * 512],
                    wob[:, ic * 2048 + oc * 128: ic * 2048 + oc * 128 + 128],
                    aores[:, ic * 2048 + ns * 512: ic * 2048 + ns * 512 + 512],
                    start=(ic == 0), stop=(ic == 15)),
                    ev=(f"p3:mm{oc}" if (ic == 15 and ns == 3) else None), waits=w)
        w = [f"p3:mm{oc}"]
        if oc > 0:
            w.append(f"p3:tr{oc-1}:T")
        ctx.v(lambda v, oc=oc: v.tensor_scalar(
            apfm[:], poc3[:], bo_s[:, oc:oc + 1], None, OP.add),
            ev=f"p3:ape{oc}", waits=w)
        emit_tr(f"p3:tr{oc}", apfm, aptm, 0, ptr3, 16, 128, "v",
                dtw=[f"p3:ape{oc}"] + ([f"p3:aps{oc-1}"] if oc > 0 else []))
        ctx.d(lambda s, oc=oc: s.dma_start(
            apsr[:, :, oc * 128:(oc + 1) * 128],
            aptm[:].rearrange("p (i c) -> p i c", c=128)),
            ev=f"p3:aps{oc}", waits=[f"p3:tr{oc}:E"])

    ctx.bar("p3")
    freebag(pbag)
    freebag(bag)

    bag, pbag = [], []
    i1res = sb("i1res", [128, 16 * 2048], BF16, bag)
    wt4 = [sb(f"wt4{i}", [128, 16 * 128], F32, bag) for i in range(2)]
    tern4 = [sb(f"tern4{i}", [128, 16 * 128], BF16, bag) for i in range(2)]
    ga4 = sb("ga4", [128, 2048], F32, bag)
    gb4 = sb("gb4", [128, 2048], F32, bag)
    h4 = sb("h4", [128, 2048], F32, bag)
    g4 = [sb(f"g4{i}", [128, 2048], F32, bag) for i in range(2)]
    poc4 = [ps(f"poc4{i}", [128, 2048], F32, pbag) for i in range(2)]

    ctx.d(lambda s: s.dma_start(
        i1res[:].rearrange("p (o c) -> p o c", c=2048), i1sr[:, :, :]),
        ev="p4:i1l", waits=[f"p3:d{i}" for i in range(16)])
    for oc in range(2):
        ctx.d(lambda s, oc=oc: s.dma_start(
            wt4[oc % 2][:].rearrange("p (i c) -> p i c", c=128),
            wup_r[:, :, oc * 128:(oc + 1) * 128]),
            ev=f"p4:wl{oc}", waits=[f"p3:d{i}" for i in range(16)])
    for ocx in range(32):
        par = ocx % 2
        w = [f"p4:wl{ocx}", "a:sv"]
        if ocx >= 2:
            w.append(f"p4:mm{ocx-2}")
        ctx.v(lambda v, par=par: v.tensor_scalar(
            ga4[:], wt4[par][:], svals[:, 1:2], 0.5, OP.mult, OP.is_gt), waits=w)
        ctx.v(lambda v, par=par: v.tensor_scalar(
            gb4[:], wt4[par][:], svals[:, 1:2], -0.5, OP.mult, OP.is_lt))
        ctx.v(lambda v, par=par: v.tensor_sub(tern4[par][:], ga4[:], gb4[:]),
              ev=f"p4:tq{ocx}")
        if ocx + 2 < 32:
            ctx.d(lambda s, ocx=ocx, par=par: s.dma_start(
                wt4[par][:].rearrange("p (i c) -> p i c", c=128),
                wup_r[:, :, (ocx + 2) * 128:(ocx + 3) * 128]),
                ev=f"p4:wl{ocx+2}", waits=[f"p4:tq{ocx}"])
        for ic in range(16):
            for ns in range(4):
                w = []
                if ic == 0 and ns == 0:
                    w = [f"p4:tq{ocx}"] + (["p4:i1l"] if ocx == 0 else []) + \
                        ([f"p4:h{ocx-2}"] if ocx >= 2 else [])
                ctx.t(lambda t, ocx=ocx, ic=ic, ns=ns, par=par: t.matmul(
                    poc4[par][:, ns * 512:(ns + 1) * 512],
                    tern4[par][:, ic * 128:(ic + 1) * 128],
                    i1res[:, ic * 2048 + ns * 512: ic * 2048 + ns * 512 + 512],
                    start=(ic == 0), stop=(ic == 15)),
                    ev=(f"p4:mm{ocx}" if (ic == 15 and ns == 3) else None), waits=w)
        ctx.a(lambda a, par=par: a.activation(
            h4[:], poc4[par][:], AF.Relu, bias=0.0, scale=1.0),
            waits=[f"p4:mm{ocx}"], ev=f"p4:h{ocx}")
        w = [f"p4:gs{ocx-2}", f"p4:h{ocx}"] if ocx >= 2 else [f"p4:h{ocx}"]
        ctx.v(lambda v, par=par: v.tensor_mul(g4[par][:], h4[:], h4[:]),
              ev=f"p4:g{ocx}", waits=w)
        ctx.d(lambda s, ocx=ocx, par=par: s.dma_start(gsr[ocx, :, :], g4[par][:]),
              ev=f"p4:gs{ocx}", waits=[f"p4:g{ocx}"])

    ctx.bar("p4")
    freebag(pbag)
    freebag(bag)

    bag, pbag = [], []
    i2half = sb("i2half", [128, 32 * 1024], BF16, bag)
    big45 = sb("big45", [128, 8192], F32, bag)
    gsl = big45
    wt6 = [big45[:, 0:4096], big45[:, 4096:8192]]
    hia5 = sb("hia5", [128, 2048], BF16, bag)
    loa5 = sb("loa5", [128, 2048], BF16, bag)
    t2s = sb("t2s", [128, 4096], F32, bag)
    u5 = sb("u5", [128, 1024], F32, bag)
    i325 = sb("i325", [128, 1024], I32, bag)
    ga5 = sb("ga5", [128, 2048], F32, bag)
    gb5 = sb("gb5", [128, 2048], F32, bag)
    ints5 = sb("ints5", [128, 4096], BF16, bag)
    rq5 = sb("rq5", [128, 8], F32, bag)
    tern6 = [sb(f"tern6{i}", [128, 32 * 128], BF16, bag) for i in range(2)]
    ffn6 = sb("ffn6", [128, 1024], F32, bag)
    ap6 = [sb(f"ap6{i}", [128, 8 * 128], F32, bag) for i in range(2)]
    xo6 = [sb(f"xo6{i}", [128, 8 * 128], F32, bag) for i in range(2)]
    apx6 = [sb(f"apx6{i}", [128, 8 * 128], F32, bag) for i in range(2)]
    ob6 = [sb(f"ob6{i}", [128, 8 * 128], F32, bag) for i in range(2)]
    pre5 = sb("pre5", [128, 48], F32, bag)

    ctx.v(lambda v: v.tensor_scalar(
        pre5[:, 0:16], mxc1b[:], svals[:, 0:1],
        float(1.0 / (127.0 * math.sqrt(2048.0))), OP.mult, OP.mult),
        waits=["a:sv"], flush=True)
    ctx.v(lambda v: v.tensor_mul(alpsq[:], pre5[:, 0:16], pre5[:, 0:16]),
          flush=True)
    ctx.v(lambda v: v.reciprocal(pre5[:, 16:32], alpsq[:]), flush=True)
    ctx.v(lambda v: v.tensor_single_scalar(pre5[:, 32:48], pre5[:, 16:32],
                                           float(1e-5 * 64.0), OP.mult),
          ev="p5:pre", flush=True)
    for half in range(2):
        p5ps = []
        ptm5 = [ps(f"ptm5_{half}_{i}", [128, 1024], F32, p5ps) for i in range(2)]
        pbtr5 = [ps(f"pbtr5{i}_{half}", [128, 128], BF16, p5ps) for i in range(2)]
        for slq in range(4):
            sl = half * 4 + slq
            if sl == 0:
                w = [f"p4:d{i}" for i in range(16)] + ["p4:t", "p4:v"]
            elif slq == 0:
                w = [f"p6:od0_15"]
            else:
                w = [f"p5:ld{sl-1}"]
            ctx.d(lambda s, sl=sl: s.dma_start(
                gsl[:].rearrange("p (c q) -> p c q", q=256),
                gslr[:, :, sl * 256:(sl + 1) * 256]),
                ev=f"p5:gl{sl}", waits=w)
            bit = 1
            k = 0
            while bit < 32:
                for c in range(32):
                    if c & bit:
                        continue
                    a0, b0 = c * 256, (c | bit) * 256
                    ctx.v(lambda v, a0=a0, b0=b0: v.tensor_sub(
                        gsl[:, b0:b0 + 256], gsl[:, a0:a0 + 256], gsl[:, b0:b0 + 256]),
                        waits=[f"p5:gl{sl}"] if k == 0 else ())
                    ctx.v(lambda v, a0=a0, b0=b0: v.scalar_tensor_tensor(
                        gsl[:, a0:a0 + 256], gsl[:, a0:a0 + 256], 2.0,
                        gsl[:, b0:b0 + 256], OP.mult, OP.subtract),
                        ev=f"p5:bf{sl}_{k}")
                    k += 1
                bit *= 2
            bfd = f"p5:bf{sl}_{k-1}"
            for tt in range(2):
                tg = sl * 2 + tt
                for fh in range(2):
                    gv = gsl[:].rearrange("p (c q) -> p c q", q=256)[
                        :, fh * 16:(fh + 1) * 16, tt * 128:tt * 128 + 128]
                    w = [bfd]
                    if fh == 1:
                        w.append(f"p5:fm{tg}_1_7")
                    elif tg > 0:
                        w.append(f"p5:fm{tg-1}_3_7")
                    ctx.v(lambda v, gv=gv: v.tensor_copy(hia5[:], gv),
                          ev=f"p5:hi{tg}_{fh}", waits=w)
                    ctx.v(lambda v, gv=gv: v.tensor_tensor(
                        loa5[:].rearrange("p (c q) -> p c q", q=128), gv,
                        hia5[:].rearrange("p (c q) -> p c q", q=128), OP.subtract),
                        ev=f"p5:lo{tg}_{fh}")
                    for fqh in range(2):
                        fq = fh * 2 + fqh
                        par5 = fq % 2
                        for cq in range(8):
                            ch5 = fqh * 8 + cq
                            ctx.t(lambda t, cq=cq, ch5=ch5, par5=par5: t.matmul(
                                ptm5[par5][:, cq * 128:(cq + 1) * 128],
                                hia5[:, ch5 * 128:(ch5 + 1) * 128],
                                h128[:], start=(cq % 4 == 0), stop=False),
                                waits=[f"p5:lo{tg}_{fh}"] +
                                      ([f"p5:t2e{tg}_{fq-2}"] if (cq == 0 and fq >= 2) else
                                       ([f"p5:t2e{tg-1}_{fq+2}"] if (cq == 0 and tg > 0) else [])))
                            ctx.t(lambda t, cq=cq, ch5=ch5, par5=par5: t.matmul(
                                ptm5[par5][:, cq * 128:(cq + 1) * 128],
                                loa5[:, ch5 * 128:(ch5 + 1) * 128],
                                h128[:], start=False, stop=(cq % 4 == 3)),
                                ev=f"p5:fm{tg}_{fq}_{cq}")
                        ctx.a(lambda a, fq=fq, par5=par5: a.activation(
                            t2s[:, fq * 1024:(fq + 1) * 1024], ptm5[par5][:],
                            AF.Copy, bias=0.0, scale=1.0),
                            ev=f"p5:t2e{tg}_{fq}",
                            waits=[f"p5:fm{tg}_{fq}_3", f"p5:fm{tg}_{fq}_7"])
                ctx.v(lambda v: v.tensor_reduce(rq5[:, 0:1], t2s[:], AX, OP.max,
                                                apply_absolute_value=True),
                      waits=[f"p5:t2e{tg}_{fq}" for fq in range(4)], flush=True)
                ctx.v(lambda v, tg=tg: v.tensor_max(
                    rq5[:, 1:2], rq5[:, 0:1], pre5[:, 32 + tg:33 + tg]),
                    flush=True)
                ctx.v(lambda v: v.reciprocal(rq5[:, 2:3], rq5[:, 1:2]), flush=True)
                ctx.v(lambda v: v.tensor_single_scalar(rq5[:, 7:8], rq5[:, 2:3],
                                                       127.0, OP.mult),
                      ev=f"p5:inv{tg}", flush=True)
                ctx.v(lambda v, tg=tg: v.tensor_scalar(
                    rq5[:, 6:7], rq5[:, 1:2], alpsq[:, tg:tg + 1], svals[:, 2:3],
                    OP.mult, OP.mult), flush=True)
                ctx.v(lambda v, tg=tg: v.tensor_single_scalar(
                    betab[:, tg:tg + 1], rq5[:, 6:7],
                    float(1.0 / (127.0 * 64.0)), OP.mult), ev=f"p5:beta{tg}",
                    flush=True)
                for quar in range(4):
                    q0 = quar * 1024
                    gh = ga5[:, (quar % 2) * 1024:(quar % 2) * 1024 + 1024]
                    gh2 = gb5[:, (quar % 2) * 1024:(quar % 2) * 1024 + 1024]
                    w = [f"p5:inv{tg}"]
                    if quar >= 1:
                        w.append(f"p5:iq{tg}_{quar-1}a")
                    elif tg > 0:
                        w.append(f"p5:iq{tg-1}_3a")
                    ctx.a(lambda a, q0=q0: a.activation(
                        u5[:], t2s[:, q0:q0 + 1024], AF.Copy, bias=0.0,
                        scale=rq5[:, 7:8]), waits=w, ev=f"p5:u{tg}_{quar}")
                    ctx.v(lambda v: v.tensor_copy(i325[:], u5[:]),
                          waits=[f"p5:u{tg}_{quar}"], ev=f"p5:i32{tg}_{quar}")
                    ctx.v(lambda v, gh2=gh2: v.tensor_sub(gh2, u5[:], i325[:]))
                    ctx.v(lambda v, gh=gh, gh2=gh2: v.tensor_single_scalar(
                        gh, gh2, 0.5, OP.is_gt))
                    ctx.v(lambda v, gh2=gh2: v.tensor_single_scalar(
                        gh2, gh2, -0.5, OP.is_lt))
                    ctx.v(lambda v, gh=gh: v.tensor_add(gh, gh, i325[:]),
                          ev=f"p5:iq{tg}_{quar}a")
                    w = [f"p5:itr{tg-1}_31"] if (quar == 0 and tg > 0) else []
                    ctx.v(lambda v, q0=q0, gh=gh, gh2=gh2: v.tensor_sub(
                        ints5[:, q0:q0 + 1024], gh, gh2),
                        ev=f"p5:iq{tg}_{quar}", waits=w)
                for c in range(32):
                    par = c % 2
                    w = [f"p5:iq{tg}_{c // 8}"]
                    if c >= 2:
                        w.append(f"p5:ie{tg}_{c-2}")
                    ctx.t(lambda t, c=c, par=par: t.transpose(
                        pbtr5[par][:], ints5[:, c * 128:(c + 1) * 128], idb[:]),
                        ev=f"p5:itr{tg}_{c}", waits=w)
                    toff = slq * 256 + tt * 128
                    w2 = [f"p5:itr{tg}_{c}"]
                    if tg == half * 8 and c == 0:
                        w2.append(f"p6:mm{1 - half}_15" if half == 1 else None)
                    ctx.a(lambda a, c=c, par=par, toff=toff: a.activation(
                        i2half[:, c * 1024 + toff: c * 1024 + toff + 128],
                        pbtr5[par][:], AF.Copy, bias=0.0, scale=1.0),
                        ev=f"p5:ie{tg}_{c}", waits=[x for x in w2 if x])
            ctx.events[f"p5:ld{sl}"] = ctx.events[f"p5:lo{sl * 2 + 1}_1"]
        i2done = [f"p5:ie{(half * 4 + slq) * 2 + tt}_{c}"
                  for slq in range(4) for tt in range(2) for c in range(32)]
        freebag(p5ps)

        p6ps = []
        pdn = [ps(f"pdn{i}_{half}", [128, 1024], F32, p6ps) for i in range(2)]
        ptr6 = [ps(f"ptr6{i}_{half}", [128, 128], F32, p6ps) for i in range(2)]

        def p6_load_quant(oc, extra):
            par = oc % 2
            if half == 0:
                ctx.d(lambda s, oc=oc, par=par: s.dma_start(
                    wt6[par].rearrange("p (i c) -> p i c", c=128),
                    wdn_r[:, :, oc * 128:(oc + 1) * 128]),
                    ev=f"p6:wl{oc}", waits=[f"p5:ld{3}"] + extra)
                ctx.v(lambda v, par=par: v.tensor_scalar(
                    ga5[:], wt6[par][:, 0:2048], svals[:, 3:4], 0.5,
                    OP.mult, OP.is_gt), waits=[f"p6:wl{oc}", "a:sv"] + extra)
                ctx.v(lambda v, par=par: v.tensor_scalar(
                    gb5[:], wt6[par][:, 0:2048], svals[:, 3:4], -0.5,
                    OP.mult, OP.is_lt))
                ctx.v(lambda v, par=par: v.tensor_sub(
                    tern6[par][:, 0:2048], ga5[:], gb5[:]))
                ctx.v(lambda v, par=par: v.tensor_scalar(
                    ga5[:], wt6[par][:, 2048:4096], svals[:, 3:4], 0.5,
                    OP.mult, OP.is_gt))
                ctx.v(lambda v, par=par: v.tensor_scalar(
                    gb5[:], wt6[par][:, 2048:4096], svals[:, 3:4], -0.5,
                    OP.mult, OP.is_lt))
                ctx.v(lambda v, par=par: v.tensor_sub(
                    tern6[par][:, 2048:4096], ga5[:], gb5[:]), ev=f"p6:tq0_{oc}")
                ctx.d(lambda s, oc=oc, par=par: s.dma_start(
                    tdsr[:, :, oc * 128:(oc + 1) * 128],
                    tern6[par][:].rearrange("p (i c) -> p i c", c=128)),
                    ev=f"p6:tds{oc}", waits=[f"p6:tq0_{oc}"])
            else:
                ctx.d(lambda s, oc=oc, par=par: s.dma_start(
                    tern6[par][:].rearrange("p (i c) -> p i c", c=128),
                    tdsr[:, :, oc * 128:(oc + 1) * 128]),
                    ev=f"p6:tq1_{oc}", waits=[f"p6:tds{oc}"] + extra)

        for ocx in range(16):
            par = ocx % 2
            if ocx == 0:
                fw = [] if half == 0 else [f"p6:od0_15"]
                p6_load_quant(0, fw)
                p6_load_quant(1, fw)
            for ic in range(32):
                for ns in range(2):
                    w = []
                    if ic == 0 and ns == 0:
                        w = [f"p6:tq{half}_{ocx}"] + (i2done if ocx == 0 else []) + \
                            ([f"p6:fe{half}_{ocx-2}"] if ocx >= 2 else [])
                    ctx.t(lambda t, ocx=ocx, ic=ic, ns=ns, par=par: t.matmul(
                        pdn[par][:, ns * 512:(ns + 1) * 512],
                        tern6[par][:, ic * 128:(ic + 1) * 128],
                        i2half[:, ic * 1024 + ns * 512: ic * 1024 + ns * 512 + 512],
                        start=(ic == 0), stop=(ic == 31)),
                        ev=(f"p6:mm{half}_{ocx}" if (ic == 31 and ns == 1) else None),
                        waits=w)
            if ocx + 2 < 16:
                ex = [f"p6:mm{half}_{ocx}"]
                if half == 0:
                    ex.append(f"p6:tds{ocx}")
                p6_load_quant(ocx + 2, ex)
            w = [f"p6:mm{half}_{ocx}"]
            if ocx > 0:
                w.append(f"p6:ftr{half}_{ocx-1}_7")
            ctx.a(lambda a, par=par: a.activation(
                ffn6[:], pdn[par][:], AF.Copy, bias=0.0, scale=1.0),
                ev=f"p6:fe{half}_{ocx}", waits=w)
            ctx.d(lambda s, ocx=ocx, par=par, half=half: s.dma_start(
                ap6[par][:].rearrange("p (i c) -> p i c", c=128),
                apsr[:, half * 8:(half + 1) * 8, ocx * 128:(ocx + 1) * 128]),
                ev=f"p6:apl{half}_{ocx}",
                waits=[f"p3:aps{ocx}"] +
                      ([f"p6:apx{half}_{ocx-2}"] if ocx >= 2 else
                       ([f"p6:apx0_{14 + ocx}"] if half == 1 else [])))
            ctx.d(lambda s, ocx=ocx, par=par, half=half: s.dma_start(
                xo6[par][:].rearrange("p (i c) -> p i c", c=128),
                xor_[:, half * 8:(half + 1) * 8, ocx * 128:(ocx + 1) * 128]),
                ev=f"p6:xl{half}_{ocx}")
            obp = ocx % 2
            ctx.v(lambda v, par=par, obp=obp: v.tensor_add(
                apx6[obp][:], ap6[par][:], xo6[par][:]),
                ev=f"p6:apx{half}_{ocx}",
                waits=[f"p6:apl{half}_{ocx}", f"p6:xl{half}_{ocx}"] +
                      ([f"p6:ob{half}_{ocx-2}_7"] if ocx >= 2 else []))
            for tt in range(8):
                tpar = tt % 2
                tgl = half * 8 + tt
                w = [f"p6:fe{half}_{ocx}"]
                if tt >= 2:
                    w.append(f"p6:ob{half}_{ocx}_{tt-2}")
                elif ocx > 0:
                    w.append(f"p6:ob{half}_{ocx-1}_{tt+6}")
                elif half == 1:
                    w.append(f"p6:ob0_15_{tt+6}")
                ctx.t(lambda t, tt=tt, tpar=tpar: t.transpose(
                    ptr6[tpar][:], ffn6[:, tt * 128:(tt + 1) * 128], idf[:]),
                    ev=f"p6:ftr{half}_{ocx}_{tt}", waits=w)
                w = [f"p6:ftr{half}_{ocx}_{tt}", f"p6:apx{half}_{ocx}",
                     f"p5:beta{tgl}"]
                if tt == 0:
                    w.append(f"p6:od{half}_{ocx-2}" if ocx > 1 else
                             (f"p6:od0_{14 + ocx}" if half == 1 else None))
                ctx.v(lambda v, tt=tt, tpar=tpar, tgl=tgl, obp=obp: v.scalar_tensor_tensor(
                    ob6[obp][:, tt * 128:(tt + 1) * 128], ptr6[tpar][:],
                    betab[:, tgl:tgl + 1], apx6[obp][:, tt * 128:(tt + 1) * 128],
                    OP.mult, OP.add),
                    ev=f"p6:ob{half}_{ocx}_{tt}", waits=[x for x in w if x])
            ctx.d(lambda s, ocx=ocx, half=half, obp=obp: s.dma_start(
                outr[:, half * 8:(half + 1) * 8, ocx * 128:(ocx + 1) * 128],
                ob6[obp][:].rearrange("p (i c) -> p i c", c=128)),
                ev=f"p6:od{half}_{ocx}",
                waits=[f"p6:ob{half}_{ocx}_{tt}" for tt in range(8)])
        freebag(p6ps)

    if DEBUG:
        ctx.v(lambda v: v.tensor_copy(ga5[:, 0:4], svals[:, 0:4]), waits=["a:sv"])
        ctx.v(lambda v: v.tensor_copy(ga5[:, 8:24], mxc1b[:]))
        ctx.v(lambda v: v.tensor_copy(ga5[:, 24:40], alpsq[:]))
        ctx.v(lambda v: v.tensor_copy(ga5[:, 40:56], betab[:]), ev="dbg:v")
        ctx.d(lambda s: s.dma_start(dbg_d[:, 0:56], ga5[:, 0:56]), waits=["dbg:v"])

    ctx.bar("fin")
    freebag(pbag)
    freebag(bag)
    freebag(perm)

    for i, val in enumerate(CSTVALS):
        try:
            nc.const_aps.register(F32, float(val), cst[:, i:i + 1])
        except Exception:
            pass
    sems = {s: nc.alloc_semaphore(f"sem_{s}") for s in ALLSEMS}

    def replay(engine, lst, inc):
        lastw = {}
        for fn, waits, sem in lst:
            for (we, wv) in waits:
                if lastw.get(we, -1) < wv:
                    engine.wait_ge(sems[we], wv)
                    lastw[we] = wv
            fn(engine).then_inc(sems[sem], inc)

    with nc.Block() as blk:
        @blk.tensor
        def _(t):
            replay(t, ctx.q["t"], 1)

        @blk.vector
        def _(v):
            replay(v, ctx.q["v"], 1)

        @blk.scalar
        def _(a):
            replay(a, ctx.q["a"], 1)

        @blk.gpsimd
        def _(g):
            replay(g, ctx.q["g"], 1)

        @blk.sync
        def _(s):
            replay(s, ctx.q["d"], 16)
            for ds in DSEMS:
                s.wait_ge(sems[ds], ctx.cnt[ds])

    return nc


_NC_CACHE = [None]


def kernel(L0_attn_x, L0_attn_Wq, L0_attn_Wkv, L0_attn_Wo, L0_attn_bo,
           L0_ffn_w_up, L0_ffn_w_down):
    X = np.ascontiguousarray(np.asarray(L0_attn_x, np.float32))
    Wq = np.ascontiguousarray(np.asarray(L0_attn_Wq, np.float32))
    Wkv = np.ascontiguousarray(np.asarray(L0_attn_Wkv, np.float32))
    Wo = np.ascontiguousarray(np.asarray(L0_attn_Wo, np.float32))
    bo = np.ascontiguousarray(np.asarray(L0_attn_bo, np.float32))
    WupT = np.ascontiguousarray(np.asarray(L0_ffn_w_up, np.float32).T)
    WdnT = np.ascontiguousarray(np.asarray(L0_ffn_w_down, np.float32).T)

    idf = np.eye(128, dtype=np.float32)
    idb = np.eye(128, dtype=np.float32).astype(BFNP)
    onesb = np.ones((128, 128), dtype=BFNP)
    onesf = np.ones((128, 128), np.float32)
    h128 = hmat(128).astype(BFNP)
    cstm = np.tile(np.array(CSTVALS, np.float32)[None, :], (128, 1))
    bo_t = np.ascontiguousarray(bo.reshape(16, 128).T)

    if _NC_CACHE[0] is None:
        _NC_CACHE[0] = build_program()
    nc = _NC_CACHE[0]

    in_maps = []
    for core in range(NCORE):
        b, qh = core // 2, core % 2
        rows = own_rows(qh)
        in_maps.append({
            "x": X[b],
            "xo": np.ascontiguousarray(X[b, rows]),
            "wq": Wq, "wkv": Wkv, "wo": Wo, "bo": bo_t,
            "wupT": WupT, "wdnT": WdnT,
            "idf_d": idf, "idb_d": idb, "onesb_d": onesb, "onesf_d": onesf,
            "h128_d": h128,
            "askew_d": build_askew(qh),
            "umask_d": np.full((128, 1), 1.0 if qh == 0 else 0.0, np.float32),
            "cst_d": cstm,
        })
    res = run_bass_kernel_spmd(nc, in_maps, core_ids=list(range(NCORE)))
    out = np.zeros((4, S, D), np.float32)
    for core in range(NCORE):
        b, qh = core // 2, core % 2
        out[b, own_rows(qh), :] = res.results[core]["out"]
    return out


if __name__ == "__main__":
    import time
    t0 = time.time()
    build_program()
    print(f"build ok in {time.time() - t0:.1f}s")
